# version 09
# baseline (speedup 1.0000x reference)
import numpy as np
import ml_dtypes
from contextlib import ExitStack

import concourse.bass as bass
import concourse.mybir as mybir
from concourse.bass_utils import run_bass_kernel_spmd

F32 = mybir.dt.float32
BF16 = mybir.dt.bfloat16
I32 = mybir.dt.int32
AF = mybir.ActivationFunctionType
ALU = mybir.AluOpType
AXX = mybir.AxisListType.X
NPBF = ml_dtypes.bfloat16

C = 1024
H = 16
NL = 64
REG = 1e-6
LN_EPS = 1e-5
FFN_EPS = 1e-8
MU_SCALE = 10752.0
NEWTON_ITERS = 31
DS = 0.125

ENGINES = ("sync", "tensor", "vector", "scalar", "gpsimd")
DMACH = ("c", "w", "x", "o", "y", "s", "t", "k", "l", "m", "n", "b", "d")


def _hilo(a):
    a = np.asarray(a, np.float64)
    hi = a.astype(np.float32)
    lo = (a - hi).astype(np.float32)
    return hi, lo


def host_consts(T):
    nss = T // 128
    sp = ((2.0 ** (-0.5)) ** np.arange(1, 17)) * 0.125
    cst = {}
    cst["identb"] = np.eye(128).astype(NPBF)
    cst["identf"] = np.eye(128).astype(np.float32)
    cst["onesrow"] = np.ones((1, 128), np.float32)
    cst["onescol"] = np.ones((128, 1), np.float32)
    cst["onesrowb"] = np.ones((1, 128), NPBF)
    cst["iotac"] = np.arange(128, dtype=np.float32).reshape(128, 1)
    cst["i64d"] = np.eye(64, dtype=np.float32)
    cst["i2stk"] = np.concatenate([np.eye(64), np.eye(64)], 0).astype(np.float32)
    cst["i2rep"] = np.tile(cst["i2stk"], (1, 8)).astype(np.float32)
    koffs = list(range(-(nss - 1), 4))
    nko = len(koffs)
    bc = np.zeros((128, H * nko))
    for h in range(H):
        for oi, ko in enumerate(koffs):
            bc[:, h * nko + oi] = sp[h] * (np.arange(128) + 128.0 * ko)
    cst["bcol_hi"], cst["bcol_lo"] = _hilo(bc)
    ii = np.arange(128)[:, None]
    jj = np.arange(128)[None, :]
    Et = np.zeros((128, 128 * H))
    Ep = np.zeros((128, 128 * H))
    for h in range(H):
        e = np.exp(sp[h] * (jj - ii))
        Et[:, h * 128:(h + 1) * 128] = e
        Ep[:, h * 128:(h + 1) * 128] = np.where(ii > jj, e, 1.0)
    cst["Et_hi"], cst["Et_lo"] = _hilo(Et)
    cst["Ep_hi"], cst["Ep_lo"] = _hilo(Ep)
    F1 = np.zeros((128, 512))
    for h in range(H):
        F1[h, :] = np.exp(sp[h] * np.arange(512))
    cst["F1_hi"], cst["F1_lo"] = _hilo(F1)
    cc = np.zeros((128, 48))
    for h in range(H):
        for m in range(3):
            cc[:, h * 3 + m] = np.exp(sp[h] * 128.0 * (m - 3))
    cst["ct_hi"], cst["ct_lo"] = _hilo(cc)
    consts = [1.0 / 1024, -1.0, 1e-5, 1e-8, 127.0, -128.0, 1.0, 0.5, 1.5, 2.0,
              3.0, 4.0, 10752.0, 1e-6, 1.0 / 127.0, 1.0 / 2 ** 20, 1.0 / 65.0,
              1e-10, 1e-30, float(T - 1), 0.0, float(2 ** 23)]
    consts += [float(i * 128) for i in range(T // 128)]
    ctab = np.zeros((128, 22 + T // 128), np.float32)
    for i, v in enumerate(consts):
        ctab[:, i] = np.float32(v)
    cst["ctable"] = ctab
    cst["_koffs"] = koffs
    return cst


class Sched:
    def __init__(self):
        self.items = {e: [] for e in ENGINES}
        self.count = {e: 0 for e in ENGINES}
        for chn in DMACH:
            self.count[chn] = 0

    def emit(self, eng, fn, wait=(), ch=None):
        if eng == "sync":
            assert ch in DMACH, ch
            self.count[ch] += 16
            sem_inc = (ch, 16)
        else:
            self.count[eng] += 1
            sem_inc = (eng, 1)
        w = {}
        for m in wait:
            if m is None:
                continue
            s, v = m
            w[s] = max(w.get(s, 0), v)
        self.items[eng].append((fn, sorted(w.items()), sem_inc))
        return (sem_inc[0], self.count[sem_inc[0]])

    def bar(self):
        keys = [e for e in ENGINES if e != "sync"] + list(DMACH)
        return tuple((e, self.count[e]) for e in keys if self.count[e] > 0)


def build_program(T=2048, NSEQ=2, debug=(), maxph=9):
    nc = bass.Bass("TRN2", target_bir_lowering=False)
    NCH = T // 128
    NTQ = T // 512
    NSS = T // 128
    NTB = T // 512
    cst = host_consts(T)
    KOFFS = cst["_koffs"]
    NKO = len(KOFFS)
    KOFF0 = KOFFS[0]

    def dI(name, shape, dt=F32):
        return nc.dram_tensor(name, list(shape), dt, kind="ExternalInput").ap()

    x_h = dI("x", (NSEQ * T, C))
    w_h = {n: dI(n, (C, C)) for n in ("wq_a", "wk_a", "wv_a", "wo_a", "wffn",
                                      "wq_b", "wpr_b")}
    w_h["wkv_b"] = dI("wkv_b", (C, 512))
    bias_a_h = dI("bias_a", (4, C))
    bqfm_h = dI("bqfm", (128, 8))
    bpr_b_h = dI("bpr_b", (1, C))
    temp_h = dI("temp", (1, 1))
    lidx_h = dI("lidx", (1, NL), I32)
    ch_ = {}
    for n, a in cst.items():
        if isinstance(a, np.ndarray):
            ch_[n] = dI(n, a.shape, BF16 if a.dtype == NPBF else F32)
    out_h = nc.dram_tensor("out", [NSEQ * T, C], F32, kind="ExternalOutput").ap()
    _dbg_specs = {
        "eKt": ([128, (T // 128) * H], F32), "ltm": ([64, C], BF16),
        "nlb": ([64, H], F32), "cqb": ([64, H], F32), "zbacc": ([128, 8 * 65], F32),
        "w0sb": ([128, 8 * 64], F32), "ablk": ([128, 8 * 128], F32),
        "mstk": ([128, 8 * 64], F32), "ysol": ([64, H * 65], BF16),
        "l2fm": ([128, 8 * 64], BF16), "qfm": ([128, 8 * 512], BF16),
        "atfm": ([128, 8 * 512], BF16), "R_CI": ([T, C], F32),
        "kfmb": ([128, 2 * T], BF16), "qfmb": ([128, 8 * T], BF16),
        "vtmb": ([128, (T // 128) * 4 * 65], BF16), "obfm": ([128, 8 * 512], BF16),
        "kfm": ([128, 8 * T], BF16), "vtm": ([128, (T // 128) * H * 65], BF16),
        "phiktm": ([128, 512], BF16), "R_CII": ([T, C], F32),
        "ypsx": ([128, 65], F32), "mblk0": ([128, 128], F32),
        "ohb": ([128, 4 * 64], BF16), "w0psw": ([128, 512], F32),
        "ccol": ([128, 32], F32), "ctable": ([128, 22 + T // 128], F32), "biasT": ([128, 2 * C], F32),
        "wqffn": ([128, 8 * C], BF16), "idxb": ([128, NL], F32),
        "R0": ([128, C], F32), "xlnb0": ([128, C], BF16), "xfm0": ([128, C], BF16),
        "ktm0": ([128, C], BF16),
    }
    dbg_h = {n: nc.dram_tensor("dbg_" + n, list(_dbg_specs[n][0]), _dbg_specs[n][1],
                               kind="ExternalOutput").ap() for n in debug}
    kfm_d = nc.dram_tensor("kfm_sp", [128, 8 * T], BF16, kind="Internal").ap()
    kfm_d3 = kfm_d.rearrange("p (c t) -> p c t", c=8)
    biasT_d = nc.dram_tensor("biasT_sp", [128, 2 * C], F32, kind="Internal").ap()

    S = Sched()

    with ExitStack() as ctx:
        ent = ctx.enter_context
        sems = {e: ent(nc.semaphore("sem_" + e)) for e in ENGINES}
        for chn in DMACH:
            sems[chn] = ent(nc.semaphore("sem_" + chn))

        Rall = ent(nc.sbuf_tensor("Rall", [128, NCH * C], F32))
        R = [Rall[:, i * C:(i + 1) * C] for i in range(NCH)]
        wqffn = ent(nc.sbuf_tensor("wqffn", [128, 8 * C], BF16))
        cs = {}
        for n in ("identb", "identf", "onesrow", "onescol", "onesrowb", "iotac",
                  "i64d", "i2stk", "i2rep", "ctable"):
            cs[n] = ent(nc.sbuf_tensor(n + "_s", list(cst[n].shape),
                                       BF16 if cst[n].dtype == NPBF else F32))
        idxb_s = ent(nc.sbuf_tensor("idxb_s", [128, NL], F32))
        stage = ent(nc.sbuf_tensor("stage", [128, 1024], F32))
        stage2 = ent(nc.sbuf_tensor("stage2", [128, 128], F32)) if debug else None
        bprow = ent(nc.sbuf_tensor("bprow", [1, C], BF16))
        bqfm_s = ent(nc.sbuf_tensor("bqfm_s", [128, 8], BF16))
        ccol = ent(nc.sbuf_tensor("ccol", [128, 32], F32))
        crow = ent(nc.sbuf_tensor("crow", [1, 64], F32))
        scr = ent(nc.sbuf_tensor("scr", [128, 1024], F32))

        PSW0 = ent(nc.psum_tensor("PSW0", [128, 1024], F32))
        PSW1 = ent(nc.psum_tensor("PSW1", [128, 1024], F32))
        PS0 = ent(nc.psum_tensor("PS0", [128, 512], F32))
        PS1 = ent(nc.psum_tensor("PS1", [128, 512], F32))
        PT = ent(nc.psum_tensor("PT", [128, 1024], BF16))
        PSX = ent(nc.psum_tensor("PSX", [128, 512], F32))

        (CTEMP, CINVT, CNEGINVT, CEPS, CMU, CREGMU, CMAGIC, CWS, CWSD, CRWS,
         CACC, CRED, CSUM, CSQ, CMEAN, CVEPS, CRSTD, CNMR, CAMAX, COSC) = range(20)

        def col(i):
            return ccol[:, i:i + 1]

        def Et_h(h):
            return EtEp_s[:, h * 128:(h + 1) * 128]

        def Ep_h(h):
            return EtEp_s[:, 2048 + h * 128:2048 + (h + 1) * 128]

        _KIDX = {1.0 / 1024: 0, -1.0: 1, 1e-5: 2, 1e-8: 3, 127.0: 4, -128.0: 5,
                 1.0: 6, 0.5: 7, 1.5: 8, 2.0: 9, 3.0: 10, 4.0: 11, 10752.0: 12,
                 1e-6: 13, 1.0 / 127.0: 14, 1.0 / 2 ** 20: 15, 1.0 / 65.0: 16,
                 1e-10: 17, 1e-30: 18, float(T - 1): 19, 0.0: 20,
                 float(2 ** 23): 21}
        for _i in range(T // 128):
            _KIDX.setdefault(float(_i * 128), 22 + _i)
        ctab_ready = [None]

        def KC(v, rows=128):
            i = _KIDX[float(v)]
            return cs["ctable"][0:rows, i:i + 1]

        def dma(dst, src, wait=(), ch="c"):
            return S.emit("sync", lambda e, d=dst, s=src: e.dma_start(out=d, in_=s),
                          wait, ch=ch)

        def ev(eng, name, wait=(), **kw):
            if eng == "vector":
                conv = False
                for k in ("scalar1", "scalar2", "scalar"):
                    v = kw.get(k)
                    if isinstance(v, float):
                        in0 = kw.get("in0")
                        if in0 is not None:
                            bp = in0.base_partition
                            if callable(bp):
                                bp = bp()
                            npart = in0.shape[0]
                        else:
                            bp, npart = 0, 128
                        i = _KIDX[float(v)]
                        kw[k] = cs["ctable"][bp:bp + npart, i:i + 1]
                        conv = True
                if conv and ctab_ready[0] is not None:
                    wait = tuple(wait) + (ctab_ready[0],)
            elif eng == "scalar":
                for k in ("scale", "bias"):
                    v = kw.get(k)
                    if isinstance(v, float):
                        kw[k] = np.float32(v)
            _drain = eng == "vector" and name in (
                "reduce_sum", "reduce_max", "tensor_reduce", "reciprocal",
                "bn_stats", "bn_aggr")

            def fn(e, _n=name, _kw=dict(kw), _d=_drain):
                ins = getattr(e, _n)(**_kw)
                if _d:
                    ins = e.drain()
                return ins
            return S.emit(eng, fn, wait)

        def mm(out, lhsT, rhs, start=True, stop=True, tile=None, wait=()):
            import traceback
            site = traceback.extract_stack(limit=3)[-2]

            def fn(e, o=out, l=lhsT, r=rhs, st=start, sp=stop, tp=tile, _s=site):
                try:
                    return e.matmul(o, l, r, start=st, stop=sp, tile_position=tp)
                except Exception as ex:
                    raise RuntimeError(
                        f"mm scheduled at {_s.filename}:{_s.lineno}: {ex}") from ex
            return S.emit("tensor", fn, wait)

        def transp(out, in_, ident, wait=(), tile_position=None):
            def fn(e, o=out, i=in_, d=ident, tp=tile_position):
                return e.transpose(o, i, d, tile_position=tp)
            return S.emit("tensor", fn, wait)

        def vcp(dst, src, wait=()):
            return ev("vector", "tensor_copy", wait=wait, out=dst, in_=src)

        def vrecip(out, in_, wait=()):
            def fn(e, o=out, i=in_):
                e.drain()
                e.reciprocal(out=o, in_=i)
                return e.drain()
            return S.emit("vector", fn, wait)

        def dtap(name, srcs, wait):
            if name in dbg_h:
                return dma(dbg_h[name], srcs, wait=wait, ch="t")
            return None

        with ExitStack() as sctx0:
            e0 = sctx0.enter_context
            biasT0 = e0(nc.sbuf_tensor("biasT0", [128, 2 * C], F32))
            land4 = e0(nc.sbuf_tensor("land4", [1, 4 * C], F32))
            land1 = e0(nc.sbuf_tensor("land1", [1, C], F32))
            bqstg = e0(nc.sbuf_tensor("bqstg", [128, 8], F32))
            idx_i32 = e0(nc.sbuf_tensor("idx_i32", [1, NL], I32))

            for n in cs:
                dma(cs[n][:], ch_[n])
            dma(ccol[0:1, CTEMP:CTEMP + 1], temp_h)
            dma(idx_i32[:], lidx_h)
            dma(land4[:], bias_a_h.rearrange("(o a) c -> o (a c)", o=1))
            dma(land1[:], bpr_b_h)
            m_call = dma(bqstg[:], bqfm_h)
            ctab_ready[0] = m_call

            m_bq = ev("vector", "tensor_copy", wait=(m_call,), out=bqfm_s[:],
                      in_=bqstg[:])

            m1 = vrecip(ccol[0:1, CINVT:CINVT + 1], ccol[0:1, CTEMP:CTEMP + 1],
                        wait=(m_call,))
            m2 = ev("vector", "memset", wait=(), ap=ccol[0:1, CEPS:CEPS + 1],
                    constant=3.0)
            m2b = vrecip(ccol[0:1, CEPS:CEPS + 1], ccol[0:1, CEPS:CEPS + 1], wait=(m2,))
            m3 = ev("vector", "tensor_scalar", wait=(m2b,), out=ccol[0:1, CEPS:CEPS + 1],
                    in0=ccol[0:1, CEPS:CEPS + 1], scalar1=4.0, scalar2=1.0,
                    op0=ALU.mult, op1=ALU.subtract)
            m4 = ev("vector", "tensor_scalar", wait=(m3,), out=ccol[0:1, CEPS:CEPS + 1],
                    in0=ccol[0:1, CEPS:CEPS + 1], scalar1=3.0, scalar2=1.0,
                    op0=ALU.mult, op1=ALU.subtract)
            m5 = ev("scalar", "activation", wait=(m4,), out=ccol[0:1, CEPS:CEPS + 1],
                    in_=ccol[0:1, CEPS:CEPS + 1], func=AF.Abs)
            m6 = ev("vector", "tensor_scalar", wait=(m5,), out=ccol[0:1, CMU:CMU + 1],
                    in0=ccol[0:1, CEPS:CEPS + 1], scalar1=10752.0, scalar2=None,
                    op0=ALU.mult)
            m6a = ev("vector", "tensor_scalar", wait=(m5,),
                     out=ccol[0:1, CSUM:CSUM + 1],
                     in0=ccol[0:1, CEPS:CEPS + 1], scalar1=float(2 ** 23),
                     scalar2=None, op0=ALU.mult)
            m6b = ev("vector", "tensor_tensor", wait=(m6, m6a),
                     out=ccol[0:1, CMU:CMU + 1], in0=ccol[0:1, CMU:CMU + 1],
                     in1=ccol[0:1, CSUM:CSUM + 1], op=ALU.mult)
            m7 = ev("vector", "tensor_scalar", wait=(m6b,), out=ccol[0:1, CREGMU:CREGMU + 1],
                    in0=ccol[0:1, CMU:CMU + 1], scalar1=1e-6, scalar2=None, op0=ALU.add)
            m8 = vrecip(ccol[0:1, CMAGIC:CMAGIC + 1], ccol[0:1, CEPS:CEPS + 1],
                        wait=(m5,))
            m9 = ev("vector", "tensor_scalar", wait=(m8,), out=ccol[0:1, CMAGIC:CMAGIC + 1],
                    in0=ccol[0:1, CMAGIC:CMAGIC + 1], scalar1=1.5, scalar2=None,
                    op0=ALU.mult)
            m10 = ev("vector", "tensor_scalar", wait=(m1,),
                     out=ccol[0:1, CNEGINVT:CNEGINVT + 1],
                     in0=ccol[0:1, CINVT:CINVT + 1], scalar1=-1.0, scalar2=None,
                     op0=ALU.mult)
            m11 = mm(PSX[:, 0:7], cs["onesrow"][:], ccol[0:1, 0:7],
                     wait=(m7, m9, m10, m_call))
            m_cols = vcp(ccol[:, 0:7], PSX[:, 0:7], wait=(m11,))

            mi2 = vcp(crow[0:1, 0:NL], idx_i32[:], wait=(m_call,))
            mi3 = ev("vector", "tensor_scalar", wait=(mi2,), out=crow[0:1, 0:NL],
                     in0=crow[0:1, 0:NL], scalar1=0.0, scalar2=float(T - 1),
                     op0=ALU.max, op1=ALU.min)
            mi4 = mm(PSX[:, 16:16 + NL], cs["onesrow"][:], crow[0:1, 0:NL],
                     wait=(mi3, m_cols))
            m_idx = vcp(idxb_s[:], PSX[:, 16:16 + NL], wait=(mi4,))

            m_bt = (m_idx,)
            for bi, row in ((0, 1), (1, 2)):
                ma = mm(PSW0[:, 0:512], cs["onesrow"][:],
                        land4[0:1, row * C:row * C + 512],
                        wait=m_bt + (m_call,))
                mb = mm(PSW0[:, 512:1024], cs["onesrow"][:],
                        land4[0:1, row * C + 512:row * C + 1024],
                        wait=())
                m_bt = (ev("vector", "tensor_copy", wait=(ma, mb),
                           out=biasT0[:, bi * C:(bi + 1) * C], in_=PSW0[:, 0:1024]),)
            m_btsp = dma(biasT_d, biasT0[:], wait=m_bt, ch="c")
            m_bp = vcp(bprow[:], land1[:], wait=(m_call,))

            mz = ev("vector", "memset", wait=(), ap=col(CACC), constant=0.0)
            m_p = (mz, m_bp, m_cols)
            for ci in range(8):
                md = dma(stage[:, 0:1024], w_h["wffn"][ci * 128:(ci + 1) * 128, :],
                         wait=m_p, ch="w")
                mrd = ev("vector", "reduce_sum", wait=(md,), out=col(CRED),
                         in_=stage[:, 0:1024], axis=AXX, apply_absolute_value=True)
                ma = ev("vector", "tensor_add", wait=(mrd,), out=col(CACC),
                        in0=col(CACC), in1=col(CRED))
                m_p = (md, ma)
            mw1 = mm(PSX[0:1, 30:31], col(CACC), cs["onescol"][:], wait=m_p + (m_call,))
            mw2 = ev("vector", "tensor_scalar", wait=(mw1,), out=crow[0:1, 32:33],
                     in0=PSX[0:1, 30:31], scalar1=float(1.0 / 2 ** 20), scalar2=1e-8,
                     op0=ALU.mult, op1=ALU.add)
            mw3 = vrecip(crow[0:1, 33:34], crow[0:1, 32:33], wait=(mw2,))
            mw4 = ev("vector", "tensor_scalar", wait=(mw2,), out=crow[0:1, 34:35],
                     in0=crow[0:1, 32:33], scalar1=float(1.0 / 127.0), scalar2=None,
                     op0=ALU.mult)
            mw5 = mm(PSX[:, 32:35], cs["onesrow"][:], crow[0:1, 32:35], wait=(mw3, mw4))
            mw6 = vcp(col(CWS), PSX[:, 32:33], wait=(mw5,))
            mw7 = vcp(col(CRWS), PSX[:, 33:34], wait=(mw5,))
            mw8 = vcp(col(CWSD), PSX[:, 34:35], wait=(mw5,))
            m_p = (mw6, mw7, mw8)
            for ci in range(8):
                md = dma(stage[:, 0:1024], w_h["wffn"][ci * 128:(ci + 1) * 128, :],
                         wait=m_p, ch="w")
                q1 = ev("vector", "tensor_scalar", wait=(md,), out=stage[:, 0:1024],
                        in0=stage[:, 0:1024], scalar1=col(CRWS), scalar2=None,
                        op0=ALU.mult)
                q2 = ev("vector", "tensor_scalar", wait=(q1,), out=stage[:, 0:1024],
                        in0=stage[:, 0:1024], scalar1=col(CMAGIC), scalar2=col(CMAGIC),
                        op0=ALU.add, op1=ALU.subtract)
                q3 = ev("vector", "tensor_scalar", wait=(q2,),
                        out=wqffn[:, ci * C:(ci + 1) * C], in0=stage[:, 0:1024],
                        scalar1=-1.0, scalar2=1.0, op0=ALU.max, op1=ALU.min)
                m_p = (md, q3)
        m_start_done = S.bar()

        ln_gate = [()]

        def ln_chunk(src, out_bf, out_f32, wait):
            ms = ev("vector", "reduce_sum", wait=wait, out=col(CSUM), in_=src, axis=AXX)
            mq = ev("scalar", "activation", wait=wait + ln_gate[0], out=scr[:, 0:1024],
                    in_=src, func=AF.Square, accum_out=col(CSQ))
            mmn = ev("vector", "tensor_scalar", wait=(ms,), out=col(CMEAN),
                     in0=col(CSUM), scalar1=float(1.0 / C), scalar2=None, op0=ALU.mult)
            mv1 = ev("vector", "tensor_scalar", wait=(mq,), out=col(CVEPS),
                     in0=col(CSQ), scalar1=float(1.0 / C), scalar2=None, op0=ALU.mult)
            mv2 = ev("vector", "scalar_tensor_tensor", wait=(mmn, mv1), out=col(CVEPS),
                     in0=col(CMEAN), scalar=col(CMEAN), in1=col(CVEPS),
                     op0=ALU.mult, op1=ALU.subtract)
            mv3 = ev("vector", "tensor_scalar", wait=(mv2,), out=col(CVEPS),
                     in0=col(CVEPS), scalar1=-1.0, scalar2=float(LN_EPS),
                     op0=ALU.mult, op1=ALU.add)
            msq = ev("scalar", "activation", wait=(mv3,), out=col(CRSTD),
                     in_=col(CVEPS), func=AF.Sqrt)
            mrc = vrecip(col(CRSTD), col(CRSTD), wait=(msq,))
            mnm = ev("vector", "tensor_scalar", wait=(mrc, mmn), out=col(CNMR),
                     in0=col(CMEAN), scalar1=col(CRSTD), scalar2=-1.0,
                     op0=ALU.mult, op1=ALU.mult)
            outs = []
            if out_bf is not None:
                outs.append(ev("vector", "tensor_scalar", wait=(mnm,), out=out_bf,
                               in0=src, scalar1=col(CRSTD), scalar2=col(CNMR),
                               op0=ALU.mult, op1=ALU.add))
            if out_f32 is not None:
                outs.append(ev("vector", "tensor_scalar", wait=(mnm,), out=out_f32,
                               in0=src, scalar1=col(CRSTD), scalar2=col(CNMR),
                               op0=ALU.mult, op1=ALU.add))
            ln_gate[0] = (mv1, outs[-1])
            return outs

        def transpose8(src_bf, dst_bf_3d, wait, nblk=8, vwait=()):
            mts = []
            for b in range(nblk):
                mts.append(transp(PT[:, b * 128:(b + 1) * 128],
                                  src_bf[:, b * 128:(b + 1) * 128], cs["identb"][:],
                                  wait=wait if b == 0 else ()))
            return vcp(dst_bf_3d, PT[:, 0:nblk * 128], wait=(mts[-1],) + tuple(vwait))

        def proj_tm(xfm_ap, w_sb, psw, nout, wait):
            last = None
            nh_cnt = (nout + 511) // 512
            for ci in range(8):
                for nh in range(nh_cnt):
                    n0, n1 = nh * 512, min(nout, (nh + 1) * 512)
                    last = mm(psw[:, n0:n1], xfm_ap[:, ci * 128:(ci + 1) * 128],
                              w_sb[:, ci * nout + n0: ci * nout + n1],
                              start=(ci == 0), stop=(ci == 7),
                              wait=wait if (ci == 0 and nh == 0) else ())
            return last

        def load_w_bf(dst, src_h, ncols, wait):
            m_prev = wait
            last = None
            for ci in range(8):
                md = dma(stage[:, 0:ncols], src_h[ci * 128:(ci + 1) * 128, 0:ncols],
                         wait=m_prev, ch="w")
                mv = vcp(dst[:, ci * ncols:(ci + 1) * ncols], stage[:, 0:ncols],
                         wait=(md,))
                m_prev = (md, mv)
                last = mv
            return last

        m_seq_entry = m_start_done
        for sq in range(NSEQ):
            sP = ExitStack()
            zbacc = sP.enter_context(nc.sbuf_tensor(f"zbacc{sq}", [128, 8 * 65], F32))
            w0acc = sP.enter_context(nc.sbuf_tensor(f"w0acc{sq}", [128, 8 * 64], F32))
            ysol = sP.enter_context(nc.sbuf_tensor(f"ysol{sq}", [64, H * 65], BF16))
            l2fm = sP.enter_context(nc.sbuf_tensor(f"l2fm{sq}", [128, 8 * 64], BF16))
            cqb = sP.enter_context(nc.sbuf_tensor(f"cqb{sq}", [64, H], F32))
            with ExitStack() as sA:
                eA = sA.enter_context
                vtm = eA(nc.sbuf_tensor(f"vtm{sq}", [128, NCH * H * 65], BF16))
                eKt = eA(nc.sbuf_tensor(f"eKt{sq}", [128, NCH * H], F32))
                ktm_t = [eA(nc.sbuf_tensor(f"ktm{sq}_{i}", [128, C], BF16))
                         for i in range(2)]
                xlnb = [eA(nc.sbuf_tensor(f"xlnb{sq}_{i}", [128, C], BF16))
                        for i in range(2)]
                xfm = [eA(nc.sbuf_tensor(f"xfm{sq}_{i}", [128, C], BF16))
                       for i in range(2)]
                ltm = eA(nc.sbuf_tensor(f"ltm{sq}", [64, C], BF16))
                nlb = eA(nc.sbuf_tensor(f"nlb{sq}", [64, H], F32))
                phikp = eA(nc.sbuf_tensor(f"phikp{sq}", [64, 2 * 512], BF16))
                phiktm = eA(nc.sbuf_tensor(f"phiktm{sq}", [128, 512], BF16))
                ohb = eA(nc.sbuf_tensor(f"ohb{sq}", [128, 4 * 64], BF16))
                biasT = eA(nc.sbuf_tensor(f"biasTa{sq}", [128, 2 * C], F32))
                sA1 = ExitStack()
                wkb = sA1.enter_context(nc.sbuf_tensor(f"wkb{sq}", [128, 8 * C], BF16))
                wvb = sA1.enter_context(nc.sbuf_tensor(f"wvb{sq}", [128, 8 * C], BF16))
                kstg = sA1.enter_context(nc.sbuf_tensor(f"kstg{sq}", [128, 2 * 1024],
                                                        BF16))

                vtm4 = vtm[:].rearrange("p (n h e) -> p n h e", n=NCH, h=H)

                m_bias = dma(biasT[:], biasT_d, wait=(m_btsp,), ch="b")
                mwk = load_w_bf(wkb, w_h["wk_a"], C, m_seq_entry)
                mwv = load_w_bf(wvb, w_h["wv_a"], C, (mwk,))
                mz1 = ev("vector", "memset", wait=m_seq_entry, ap=zbacc[:], constant=0.0)
                mzw = ev("vector", "memset", wait=(), ap=w0acc[:], constant=0.0)
                mvo = mz1
                for i_ in range(NCH):
                    mvo = ev("vector", "memset", wait=m_seq_entry if i_ == 0 else (),
                             ap=vtm[:, i_ * H * 65:(i_ + 1) * H * 65]
                             .rearrange("p (h e) -> p h e", h=H)[:, :, 64:65],
                             constant=1.0)

                m_x = dma(Rall[:].rearrange("p (n c) -> p n c", n=NCH),
                          x_h[sq * T:(sq + 1) * T, :]
                          .rearrange("(n p) c -> p n c", p=128),
                          wait=m_seq_entry, ch="x")
                m_kevac = [None, None]
                m_spill = [None, None]
                m_kcons = [None, None]
                m_vcons = None
                m_xfm_prev = None
                m_kfm_last = None
                m_lmm_last = None
                m_ek_last = None
                m_vev_last = None
                for i in range(NCH):
                    par = i % 2
                    mx = m_x
                    w0_ = (mx, mwv) + (m_kcons[par] if m_kcons[par] else ())
                    mln = ln_chunk(R[i], xlnb[par][:], None, w0_)
                    mxf = transpose8(xlnb[par][:],
                                     xfm[par][:].rearrange("p (a b) -> p a b", a=8),
                                     tuple(mln) + ((m_xfm_prev,) if m_xfm_prev else ()))
                    m_xfm_prev = mxf
                    mk_mm = proj_tm(xfm[par][:], wkb, PSW0, C,
                                    (mxf, mwk) + ((m_kevac[1 - par],) if m_kevac[1 - par] else ()))
                    mke = ev("vector", "tensor_add", wait=(mk_mm, m_bias),
                             out=ktm_t[par][:],
                             in0=PSW0[:, 0:1024], in1=biasT[:, 0:C])
                    m_kevac[par] = mke
                    if sq == 0 and i == 0 and "ccol" in dbg_h:
                        mcc = vcp(stage2[:, 0:32], ccol[:, 0:32], wait=tuple(mln))
                        dtap("ccol", stage2[:, 0:32], (mcc,))
                        dtap("ctable", cs["ctable"][:], (mcc,))
                    if sq == 0 and i == 0:
                        dtap("R0", R[0], (mx,))
                        dtap("xlnb0", xlnb[0][:], tuple(mln))
                        dtap("xfm0", xfm[0][:], (mxf,))
                        dtap("ktm0", ktm_t[0][:], (mke,))
                        dtap("biasT", biasT[:], (m_bias,))
                        dtap("wqffn", wqffn[:], m_p)
                        dtap("idxb", idxb_s[:], (m_idx,))
                    msq2 = ev("scalar", "activation", wait=(mke,), out=scr[:, 0:1024],
                              in_=ktm_t[par][:], func=AF.Square)
                    mnk = ev("vector", "reduce_sum", wait=(msq2,),
                             out=eKt[:, i * H:(i + 1) * H],
                             in_=scr[:, 0:1024].rearrange("p (h d) -> p h d", h=H),
                             axis=AXX)
                    ln_gate[0] = ln_gate[0] + (mnk,)
                    mek = ev("scalar", "activation", wait=(mnk, m_cols),
                             out=eKt[:, i * H:(i + 1) * H],
                             in_=eKt[:, i * H:(i + 1) * H], func=AF.Exp,
                             scale=col(CNEGINVT))
                    m_ek_last = mek
                    kstg3 = kstg[:, par * 1024:(par + 1) * 1024].rearrange(
                        "p (c u) -> p c u", c=8)
                    mkf = transpose8(ktm_t[par][:], kstg3, (mke, mxf),
                                     vwait=(m_spill[par],) if m_spill[par] else ())
                    m_kfm_last = mkf
                    m_spill[par] = dma(kfm_d3[:, :, i * 128:(i + 1) * 128], kstg3,
                                       wait=(mkf,), ch="k" if par == 0 else "l")
                    moh = ev("vector", "tensor_scalar", wait=(m_idx,),
                             out=ohb[:, 0:64], in0=idxb_s[:], scalar1=cs["iotac"][:],
                             scalar2=float(i * 128), op0=ALU.subtract, op1=ALU.is_equal)
                    ml1 = mm(PS0[0:64, 0:512], ohb[:, 0:64], ktm_t[par][:, 0:512],
                             start=(i == 0), stop=(i == NCH - 1), wait=(moh, mke))
                    ml2 = mm(PS1[0:64, 0:512], ohb[:, 0:64], ktm_t[par][:, 512:1024],
                             start=(i == 0), stop=(i == NCH - 1), wait=())
                    m_lmm_last = ml2
                    mv_mm = proj_tm(xfm[par][:], wvb, PSW1, C,
                                    (mxf, mwv, mvo) + ((m_vcons,) if m_vcons else ()))
                    mve = ev("vector", "tensor_tensor", wait=(mv_mm, m_bias),
                             out=vtm4[:, i, :, 0:64],
                             in0=PSW1[:, 0:1024].rearrange("p (h d) -> p h d", h=H),
                             in1=biasT[:, C:2 * C].rearrange("p (h d) -> p h d", h=H),
                             op=ALU.add)
                    m_vcons = mve
                    m_vev_last = mve
                    m_kcons[par] = (mkf, ml2, mnk)

                sA1.close()
                kbuf = eA(nc.sbuf_tensor(f"kbuf{sq}", [128, 2 * 8 * 512], BF16))

                def kpre_fetch(tb, wait):
                    h2 = tb % 2
                    return dma(kbuf[:, h2 * 4096:(h2 + 1) * 4096]
                               .rearrange("p (c u) -> p c u", c=8),
                               kfm_d3[:, :, tb * 512:(tb + 1) * 512],
                               wait=wait, ch="m" if h2 == 0 else "n")

                def kread(tb, p, hh):
                    h2 = tb % 2
                    return kbuf[hh * 64:(hh + 1) * 64,
                                h2 * 4096 + p * 512:h2 * 4096 + (p + 1) * 512]

                if maxph < 1:
                    if sq == 0:
                        dtap("eKt", eKt[:], (m_ek_last,))
                        dma(dbg_h["kfm"], kfm_d,
                            wait=(m_spill[0], m_spill[1]), ch="t") \
                            if "kfm" in dbg_h else None
                        dtap("vtm", vtm[:], (m_vev_last,))
                    mlt1x = vcp(ltm[:, 0:512], PS0[0:64, 0:512], wait=(m_lmm_last,))
                    mlt2x = vcp(ltm[:, 512:1024], PS1[0:64, 0:512], wait=(m_lmm_last,))
                    if sq == 0:
                        dtap("ltm", ltm[:], (mlt1x, mlt2x))
                    m_seq_entry = S.bar()
                    continue
                mlt1 = vcp(ltm[:, 0:512], PS0[0:64, 0:512], wait=(m_lmm_last,))
                mlt2 = vcp(ltm[:, 512:1024], PS1[0:64, 0:512], wait=(m_lmm_last,))
                msq3 = ev("scalar", "activation", wait=(mlt1, mlt2) + ln_gate[0],
                          out=scr[0:64, 0:1024], in_=ltm[:], func=AF.Square)
                mnl = ev("vector", "reduce_sum", wait=(msq3,), out=nlb[:],
                         in_=scr[0:64, 0:1024].rearrange("p (h d) -> p h d", h=H),
                         axis=AXX)
                ln_gate[0] = ln_gate[0] + (mnl,)
                mnlb = ev("vector", "tensor_scalar", wait=(mnl, m_cols), out=nlb[:],
                          in0=nlb[:], scalar1=col(CNEGINVT)[0:64, :], scalar2=None,
                          op0=ALU.mult)
                mt8 = []
                for b in range(8):
                    mt8.append(transp(PT[:, b * 64:(b + 1) * 64],
                                      ltm[:, b * 128:(b + 1) * 128],
                                      cs["identb"][0:64, 0:64],
                                      wait=(mlt1, mlt2, m_kfm_last) if b == 0 else ()))
                ml2f = ev("vector", "tensor_scalar", wait=(mt8[-1],), out=l2fm[:],
                          in0=PT[:, 0:512], scalar1=2.0, scalar2=None, op0=ALU.mult)
                mqlb = []
                for h in range(H):
                    p, hh = h // 2, h % 2
                    mqlb.append(mm(PSX[0:64, 40 + h:41 + h],
                                   l2fm[hh * 64:(hh + 1) * 64, p * 64:(p + 1) * 64],
                                   bqfm_s[hh * 64:(hh + 1) * 64, p:p + 1],
                                   tile=(hh * 64, 0) if hh else None,
                                   wait=(ml2f, m_bq) if h == 0 else ()))
                mcq1 = vcp(cqb[:], PSX[0:64, 40:40 + H], wait=(mqlb[-1],))
                mcq = ev("vector", "scalar_tensor_tensor", wait=(mcq1, mnlb), out=cqb[:],
                         in0=cqb[:], scalar=col(CINVT)[0:64, :], in1=nlb[:],
                         op0=ALU.mult, op1=ALU.add)

                m_exp_prev = [None, None]
                m_tr_last = None
                m_zev = [None, None]
                m_w0_last = None
                m_w0ev = [None] * 16
                m_pe_prev = None
                kinit_w = (m_spill[0], m_spill[1], m_kevac[0], m_kevac[1], m_vcons)
                m_kpre = [kpre_fetch(0, kinit_w),
                          kpre_fetch(1, kinit_w) if NTB > 1 else None]
                m_kread = [None, None]
                for tb in range(NTB):
                    moh4 = []
                    for tc in range(4):
                        ch = tb * 4 + tc
                        moh4.append(ev("vector", "tensor_scalar",
                                       wait=(m_idx,) +
                                            ((m_w0_last,) if m_w0_last else ()),
                                       out=ohb[:, tc * 64:(tc + 1) * 64], in0=idxb_s[:],
                                       scalar1=cs["iotac"][:], scalar2=float(ch * 128),
                                       op0=ALU.subtract, op1=ALU.is_equal))
                    if sq == 0 and tb == 0:
                        dtap("ohb", ohb[:], (moh4[-1],))
                    for p in range(8):
                        mexp = []
                        for hh in range(2):
                            h = 2 * p + hh
                            ps = (PS0, PS1)[hh]
                            w_ = (ml2f, m_kpre[tb % 2], mnlb)
                            if m_exp_prev[hh]:
                                w_ += (m_exp_prev[hh],)
                            if m_tr_last:
                                w_ += (m_tr_last,)
                            mmk = mm(ps[0:64, 0:512],
                                     l2fm[hh * 64:(hh + 1) * 64, p * 64:(p + 1) * 64],
                                     kread(tb, p, hh),
                                     tile=(hh * 64, 0) if hh else None, wait=w_)
                            m_kread[tb % 2] = mmk
                            me = ev("scalar", "activation", wait=(mmk, mnlb),
                                    out=phikp[:, hh * 512:(hh + 1) * 512],
                                    in_=ps[0:64, 0:512], func=AF.Exp,
                                    scale=col(CINVT)[0:64, :], bias=nlb[0:64, h:h + 1])
                            mexp.append(me)
                            m_exp_prev[hh] = me
                        mt = []
                        for tc in range(4):
                            for hh in range(2):
                                mt.append(transp(
                                    PT[:, (tc * 2 + hh) * 64:(tc * 2 + hh + 1) * 64],
                                    phikp[:, hh * 512 + tc * 128: hh * 512 + (tc + 1) * 128],
                                    cs["identb"][0:64, 0:64],
                                    wait=(mexp[0], mexp[1]) +
                                         ((m_pe_prev,) if m_pe_prev else ())
                                    if (tc == 0 and hh == 0) else ()))
                        m_tr_last = mt[-1]
                        mpes = []
                        for tc in range(4):
                            ch = tb * 4 + tc
                            for hh in range(2):
                                h = 2 * p + hh
                                mpes.append(ev(
                                    "vector", "tensor_scalar",
                                    wait=(mt[-1], m_ek_last) if not mpes else (),
                                    out=phiktm[:, (tc * 2 + hh) * 64:(tc * 2 + hh + 1) * 64],
                                    in0=PT[:, (tc * 2 + hh) * 64:(tc * 2 + hh + 1) * 64],
                                    scalar1=eKt[:, ch * H + h: ch * H + h + 1],
                                    scalar2=None, op0=ALU.mult))
                        mpe = mpes[-1]
                        m_pe_prev = mpe
                        mz_l = [None, None]
                        for hh in range(2):
                            h = 2 * p + hh
                            zsl = PSX[hh * 64:(hh + 1) * 64, 0:65]
                            for tc in range(4):
                                ch = tb * 4 + tc
                                mz_l[hh] = mm(
                                    zsl,
                                    phiktm[:, (tc * 2 + hh) * 64:(tc * 2 + hh + 1) * 64],
                                    vtm4[:, ch, h, :],
                                    start=(tc == 0), stop=(tc == 3),
                                    tile=(0, 64) if hh else None,
                                    wait=(mpe, m_vev_last, m_zev[0], m_zev[1])
                                    if tc == 0 else ())
                        mw_l = [None, None]
                        for hh in range(2):
                            w0ps = PSW0[hh * 64:(hh + 1) * 64, p * 64:(p + 1) * 64]
                            for tc in range(4):
                                if tc == 0:
                                    w0w = (moh4[-1], m_w0ev[p * 2 + hh],
                                           m_w0ev[p * 2 + 1 - hh])
                                    if tb == 0:
                                        w0w += (mpe, mzw)
                                else:
                                    w0w = ()
                                mw_l[hh] = mm(
                                    w0ps,
                                    ohb[:, tc * 64:(tc + 1) * 64],
                                    phiktm[:, (tc * 2 + hh) * 64:(tc * 2 + hh + 1) * 64],
                                    start=(tc == 0), stop=(tc == 3),
                                    tile=(0, 64) if hh else None,
                                    wait=w0w)
                        m_w0_last = mw_l[1]
                        for hh in range(2):
                            h = 2 * p + hh
                            zsl = PSX[hh * 64:(hh + 1) * 64, 0:65]
                            w0ps = PSW0[hh * 64:(hh + 1) * 64, p * 64:(p + 1) * 64]
                            w0sl = w0acc[hh * 64:(hh + 1) * 64, p * 64:(p + 1) * 64]
                            mza = ev("vector", "tensor_add", wait=(mz_l[1],),
                                     out=zbacc[hh * 64:(hh + 1) * 64, p * 65:(p + 1) * 65],
                                     in0=zbacc[hh * 64:(hh + 1) * 64, p * 65:(p + 1) * 65],
                                     in1=zsl)
                            m_zev[hh] = mza
                            m_w0ev[p * 2 + hh] = ev(
                                "vector", "tensor_add", wait=(mw_l[1],),
                                out=w0sl, in0=w0sl, in1=w0ps)
                    if tb + 2 < NTB:
                        m_kpre[tb % 2] = kpre_fetch(tb + 2, (m_kread[tb % 2],))
                m_zb_last = m_zev
                if sq == 0:
                    dtap("eKt", eKt[:], (m_ek_last,))
                    dtap("ltm", ltm[:], (mlt1, mlt2))
                    dtap("nlb", nlb[:], (mnlb,))
                    dtap("cqb", cqb[:], (mcq,))
                    dtap("l2fm", l2fm[:], (ml2f,))
                    if "kfm" in dbg_h:
                        dma(dbg_h["kfm"], kfm_d, wait=(m_spill[0], m_spill[1]),
                            ch="t")
                    dtap("vtm", vtm[:], (m_vev_last,))
                    dtap("zbacc", zbacc[:], (m_zev[0], m_zev[1]))
                    dtap("phiktm", phiktm[:], (m_pe_prev,))
                m_B_done = S.bar()

            if maxph < 2:
                sP.close()
                m_seq_entry = S.bar()
                continue
            with ExitStack() as sS:
                eS = sS.enter_context
                ablk = eS(nc.sbuf_tensor(f"ablk{sq}", [128, 8 * 128], F32))
                mstk = eS(nc.sbuf_tensor(f"mstk{sq}", [128, 8 * 64], F32))
                mblk = eS(nc.sbuf_tensor(f"mblk{sq}", [128, 8 * 128], F32))
                eall = eS(nc.sbuf_tensor(f"eall{sq}", [128, 8 * 64], F32))
                w0sb = eS(nc.sbuf_tensor(f"w0sb{sq}", [128, 8 * 64], F32))
                ystg = eS(nc.sbuf_tensor(f"ystg{sq}", [128, 8 * 65], BF16))

                mwe = []
                for p in range(8):
                    for hh in range(2):
                        mwe.append(ev("vector", "tensor_scalar",
                                      wait=(m_w0_last,) + tuple(m_w0ev)
                                      if not mwe else (),
                                      out=w0sb[hh * 64:(hh + 1) * 64, p * 64:(p + 1) * 64],
                                      in0=w0acc[hh * 64:(hh + 1) * 64, p * 64:(p + 1) * 64],
                                      scalar1=0.5, scalar2=None, op0=ALU.mult))
                if sq == 0 and "w0psw" in dbg_h:
                    mwcp = vcp(stage[:, 0:512], w0acc[:, 0:512], wait=(mwe[-1],))
                    dtap("w0psw", stage[:, 0:512], (mwcp,))
                mz2 = ev("vector", "memset", wait=m_B_done, ap=ablk[:], constant=0.0)
                mz3 = ev("vector", "memset", wait=(), ap=mblk[:], constant=0.0)
                m_tp = None
                mas = None
                for p in range(8):
                    for hh in range(2):
                        sl = w0sb[hh * 64:(hh + 1) * 64, p * 64:(p + 1) * 64]
                        psl = PSX[hh * 64:(hh + 1) * 64, 64:128]
                        mt_ = mm(psl, sl,
                                 cs["identf"][hh * 64:(hh + 1) * 64,
                                              hh * 64:(hh + 1) * 64],
                                 tile=(hh * 64, hh * 64) if hh else None,
                                 wait=(mwe[-1], mz2) + ((m_tp,) if m_tp else ()))
                        dst = ablk[hh * 64:(hh + 1) * 64,
                                   p * 128 + hh * 64: p * 128 + (hh + 1) * 64]
                        ma1 = ev("vector", "tensor_add", wait=(mt_,), out=dst, in0=sl,
                                 in1=psl)
                        mas = ev("vector", "scalar_tensor_tensor", wait=(ma1, m_cols),
                                 out=dst, in0=cs["i2stk"][hh * 64:(hh + 1) * 64, :],
                                 scalar=col(CREGMU)[hh * 64:(hh + 1) * 64, :], in1=dst,
                                 op0=ALU.mult, op1=ALU.add)
                        m_tp = ma1
                mmi = ev("vector", "tensor_scalar", wait=(mas,), out=mstk[:],
                         in0=cs["i2rep"][:], scalar1=float(1.0 / 65.0), scalar2=None,
                         op0=ALU.mult)
                m_it = mmi
                m_e_prev = None
                for it in range(NEWTON_ITERS):
                    mb1 = vcp(mblk[0:64, :].rearrange("p (n c) -> p n c", n=8)[:, :, 0:64],
                              mstk[0:64, :].rearrange("p (n c) -> p n c", n=8),
                              wait=(m_it, mz3))
                    mb2 = vcp(mblk[64:128, :].rearrange("p (n c) -> p n c", n=8)[:, :, 64:128],
                              mstk[64:128, :].rearrange("p (n c) -> p n c", n=8),
                              wait=(m_it,))
                    mq_ = None
                    for p in range(8):
                        mq_ = mm(PSW1[:, p * 64:(p + 1) * 64],
                                 ablk[:, p * 128:(p + 1) * 128],
                                 mstk[:, p * 64:(p + 1) * 64],
                                 wait=(mas, m_it) + ((m_e_prev,) if m_e_prev else ())
                                 if p == 0 else ())
                    me_ = ev("vector", "tensor_tensor", wait=(mq_,), out=eall[:],
                             in0=cs["i2rep"][:], in1=PSW1[:, 0:512], op=ALU.subtract)
                    m_e_prev = me_
                    md_ = None
                    for p in range(8):
                        md_ = mm(PSW0[:, p * 64:(p + 1) * 64],
                                 mblk[:, p * 128:(p + 1) * 128],
                                 eall[:, p * 64:(p + 1) * 64],
                                 wait=(me_, mb1, mb2, mwe[-1]) if p == 0 else ())
                    m_it = ev("vector", "tensor_add", wait=(md_,), out=mstk[:],
                              in0=mstk[:], in1=PSW0[:, 0:512])
                mb1 = vcp(mblk[0:64, :].rearrange("p (n c) -> p n c", n=8)[:, :, 0:64],
                          mstk[0:64, :].rearrange("p (n c) -> p n c", n=8), wait=(m_it,))
                mb2 = vcp(mblk[64:128, :].rearrange("p (n c) -> p n c", n=8)[:, :, 64:128],
                          mstk[64:128, :].rearrange("p (n c) -> p n c", n=8), wait=(m_it,))
                if sq == 0 and "mblk0" in dbg_h:
                    dtap("mblk0", mblk[:, 0:128], (mb1, mb2))
                m_ye_prev = None
                mystg = []
                for p in range(8):
                    my1 = mm(PSX[:, 0:65], mblk[:, p * 128:(p + 1) * 128],
                             zbacc[:, p * 65:(p + 1) * 65],
                             wait=(mb1, mb2, m_zb_last[0], m_zb_last[1]) +
                                  ((m_ye_prev,) if m_ye_prev else ()))
                    my2 = vcp(ysol[0:64, (2 * p) * 65:(2 * p + 1) * 65], PSX[0:64, 0:65],
                              wait=(my1,))
                    my3 = vcp(ystg[64:128, p * 65:(p + 1) * 65],
                              PSX[64:128, 0:65], wait=(my1,))
                    m_ye_prev = my3
                    if sq == 0 and p == 0 and "ypsx" in dbg_h:
                        mt_c = vcp(stage2[:, 0:65], PSX[:, 0:65], wait=(my1,))
                        mt_y = dtap("ypsx", stage2[:, 0:65], (mt_c,))
                        m_ye_prev = mt_c
                    mystg.append(my3)
                m_ys = dma(ysol[0:64, :].rearrange("p (a b s) -> p a b s", a=8, b=2)
                           [:, :, 1, :],
                           ystg[64:128, :].rearrange("p (a s) -> p a s", a=8),
                           wait=tuple(mystg), ch="y")
                if sq == 0:
                    dtap("w0sb", w0sb[:], (mwe[-1],))
                    dtap("ablk", ablk[:], (mas,))
                    dtap("mstk", mstk[:], (m_it,))
                    dtap("ysol", ysol[:], (m_ys,))
                m_S_done = S.bar()

            if maxph < 3:
                sP.close()
                m_seq_entry = S.bar()
                continue
            with ExitStack() as sC:
                eC = sC.enter_context
                stgB = eC(nc.sbuf_tensor(f"stgBc{sq}", [128, 8 * 512], BF16))
                wqab = eC(nc.sbuf_tensor(f"wqab{sq}", [128, 8 * C], BF16))
                woab = eC(nc.sbuf_tensor(f"woab{sq}", [128, 8 * C], BF16))
                qfm = eC(nc.sbuf_tensor(f"qfm{sq}", [128, 8 * 512], BF16))
                atfm = eC(nc.sbuf_tensor(f"atfm{sq}", [128, 8 * 512], BF16))
                phiq = eC(nc.sbuf_tensor(f"phiq{sq}", [64, 2 * 512], BF16))
                xlnb2 = [eC(nc.sbuf_tensor(f"xlnb2{sq}_{i}", [128, C], BF16))
                         for i in range(2)]
                xfm2 = [eC(nc.sbuf_tensor(f"xfm2{sq}_{i}", [128, C], BF16))
                        for i in range(2)]
                qtmp = eC(nc.sbuf_tensor(f"qtmp{sq}", [128, C], BF16))
                aqtm = eC(nc.sbuf_tensor(f"aqtm{sq}", [128, C], BF16))
                aqfm = eC(nc.sbuf_tensor(f"aqfm{sq}", [128, C], BF16))
                scf = eC(nc.sbuf_tensor(f"scf{sq}", [128, 512], F32))
                borow = eC(nc.sbuf_tensor(f"borow{sq}", [1, C], BF16))

                qfm3 = qfm[:].rearrange("p (c t) -> p c t", c=8)
                m_bod = dma(stage[0:1, 0:1024], bias_a_h[3:4, :], wait=m_S_done, ch="d")
                m_bo = vcp(borow[:], stage[0:1, 0:1024], wait=(m_bod,))
                mwq = load_w_bf(wqab, w_h["wq_a"], C, tuple(m_S_done) + (m_bo,))
                mwo = load_w_bf(woab, w_h["wo_a"], C, (mwq,))

                m_ffn_prev = None
                m_stgB_dma = None
                for tb in range(NTB):
                    mq_done = []
                    m_qt_prev = None
                    for tc in range(4):
                        ch = tb * 4 + tc
                        par = tc % 2
                        w_ = (mwo,) if (tb == 0 and tc == 0) else ()
                        if m_ffn_prev:
                            w_ += (m_ffn_prev,)
                        mln = ln_chunk(R[ch], xlnb2[par][:], None, w_)
                        mxf = transpose8(xlnb2[par][:],
                                         xfm2[par][:].rearrange("p (a b) -> p a b", a=8),
                                         tuple(mln))
                        mqmm = proj_tm(xfm2[par][:], wqab, PSW0, C, (mxf, mwq))
                        mqe = vcp(qtmp[:], PSW0[:, 0:1024], wait=(mqmm,))
                        mqt = transpose8(qtmp[:], qfm3[:, :, tc * 128:(tc + 1) * 128],
                                         (mqe,) + ((m_qt_prev,) if m_qt_prev else ()))
                        m_qt_prev = mqt
                        mq_done.append(mqt)
                    m_cq_prev = None
                    m_ynext = None
                    m_f2_prev = None
                    m_ob_prev = None
                    for h in range(H):
                        p, hh = h // 2, h % 2
                        ps = (PS0, PS1)[h % 2]
                        mmq = mm(ps[0:64, 0:512],
                                 l2fm[hh * 64:(hh + 1) * 64, p * 64:(p + 1) * 64],
                                 qfm[hh * 64:(hh + 1) * 64, p * 512:(p + 1) * 512],
                                 tile=(hh * 64, 0) if hh else None,
                                 wait=(mq_done[-1],) + ((m_cq_prev,) if m_cq_prev else ()))
                        mpq = ev("scalar", "activation", wait=(mmq, mcq),
                                 out=phiq[:, (h % 2) * 512:(h % 2 + 1) * 512],
                                 in_=ps[0:64, 0:512], func=AF.Exp,
                                 scale=col(CINVT)[0:64, :], bias=cqb[0:64, h:h + 1])
                        m_cq_prev = mpq
                        myp = mm(PSW1[0:65, 0:512], ysol[0:64, h * 65:(h + 1) * 65],
                                 phiq[:, (h % 2) * 512:(h % 2 + 1) * 512],
                                 wait=(mpq, m_ys) + ((m_ynext,) if m_ynext else ()))
                        mn1 = ev("vector", "tensor_scalar", wait=(myp,),
                                 out=scf[0:1, 0:512], in0=PSW1[64:65, 0:512],
                                 scalar1=1e-10, scalar2=None, op0=ALU.max)
                        mn2 = vrecip(scf[0:1, 0:512], scf[0:1, 0:512], wait=(mn1,))
                        mf2 = mm(PSX[0:64, 0:512], cs["onesrow"][0:1, 0:64],
                                 scf[0:1, 0:512],
                                 wait=(mn2,) + ((m_f2_prev,) if m_f2_prev else ()))
                        mf2e = vcp(scr[0:64, 0:512], PSX[0:64, 0:512],
                                   wait=(mf2,) + ln_gate[0] +
                                        ((m_ob_prev,) if m_ob_prev else ()))
                        m_f2_prev = mf2e
                        if hh == 0:
                            mae = ev("vector", "tensor_tensor", wait=(mf2e, myp),
                                     out=atfm[0:64, p * 512:(p + 1) * 512],
                                     in0=PSW1[0:64, 0:512], in1=scr[0:64, 0:512],
                                     op=ALU.mult)
                        else:
                            mae = ev("vector", "tensor_tensor",
                                     wait=(mf2e, myp) + ((m_stgB_dma,) if m_stgB_dma else ()),
                                     out=stgB[0:64, p * 512:(p + 1) * 512],
                                     in0=PSW1[0:64, 0:512], in1=scr[0:64, 0:512],
                                     op=ALU.mult)
                        m_ynext = mae
                        m_ob_prev = mae
                    m_stgB_dma = dma(atfm[64:128, :], stgB[0:64, :], wait=(m_ynext,),
                                     ch="s")
                    m_at_ev = m_stgB_dma
                    if sq == 0 and tb == 0:
                        dtap("qfm", qfm[:], (mq_done[-1],))
                        dtap("atfm", atfm[:], (m_at_ev, m_ynext))
                    for tc in range(4):
                        ch = tb * 4 + tc
                        mwos = None
                        for ci in range(8):
                            for nh in range(2):
                                mwos = mm(
                                    PSW0[:, nh * 512:(nh + 1) * 512],
                                    atfm[:, ci * 512 + tc * 128: ci * 512 + (tc + 1) * 128],
                                    woab[:, ci * C + nh * 512: ci * C + (nh + 1) * 512],
                                    start=(ci == 0), stop=False,
                                    wait=(m_at_ev,) if (ci == 0 and nh == 0) else ())
                        for nh in range(2):
                            mwos = mm(PSW0[:, nh * 512:(nh + 1) * 512],
                                      cs["onesrowb"][:],
                                      borow[0:1, nh * 512:(nh + 1) * 512],
                                      start=False, stop=True, wait=(m_bo,))
                        mra = ev("vector", "tensor_add", wait=(mwos,),
                                 out=R[ch], in0=R[ch], in1=PSW0[:, 0:1024])
                        mln2 = ln_chunk(R[ch], None, scr[:, 0:1024], (mra,))
                        mam = ev("vector", "reduce_max", wait=(mln2[-1],), out=col(CAMAX),
                                 in_=scr[:, 0:1024], axis=AXX, apply_absolute_value=True)
                        mas_ = ev("vector", "tensor_scalar", wait=(mam,), out=col(CAMAX),
                                  in0=col(CAMAX), scalar1=float(FFN_EPS), scalar2=None,
                                  op0=ALU.add)
                        mrs = vrecip(col(CRED), col(CAMAX), wait=(mas_,))
                        ms127 = ev("vector", "tensor_scalar", wait=(mrs,), out=col(CRED),
                                   in0=col(CRED), scalar1=127.0, scalar2=None, op0=ALU.mult)
                        mosc = ev("vector", "tensor_scalar", wait=(mas_,), out=col(COSC),
                                  in0=col(CAMAX), scalar1=col(CWSD), scalar2=None,
                                  op0=ALU.mult)
                        mz1_ = ev("vector", "tensor_scalar", wait=(ms127,),
                                  out=scr[:, 0:1024], in0=scr[:, 0:1024],
                                  scalar1=col(CRED), scalar2=-128.0,
                                  op0=ALU.mult, op1=ALU.max)
                        mz2_ = ev("vector", "tensor_scalar", wait=(mz1_,),
                                  out=scr[:, 0:1024], in0=scr[:, 0:1024],
                                  scalar1=127.0, scalar2=None, op0=ALU.min)
                        mrd = ev("vector", "tensor_scalar", wait=(mz2_,), out=aqtm[:],
                                 in0=scr[:, 0:1024], scalar1=col(CMAGIC),
                                 scalar2=col(CMAGIC), op0=ALU.add, op1=ALU.subtract)
                        ln_gate[0] = ln_gate[0] + (mrd,)
                        mqt2 = transpose8(aqtm[:], aqfm[:].rearrange("p (a b) -> p a b", a=8),
                                          (mrd,))
                        mfmm = proj_tm(aqfm[:], wqffn, PSW1, C, (mqt2,))
                        m_ffn_prev = ev("vector", "scalar_tensor_tensor", wait=(mfmm, mosc),
                                        out=R[ch], in0=PSW1[:, 0:1024],
                                        scalar=col(COSC), in1=R[ch],
                                        op0=ALU.mult, op1=ALU.add)
                if sq == 0:
                    for ch in range(NCH):
                        dtap("R_CI" if ch == 0 else None, None, None) if False else None
                    if "R_CI" in dbg_h:
                        for ch in range(NCH):
                            dma(dbg_h["R_CI"][ch * 128:(ch + 1) * 128, :], R[ch],
                                wait=(m_ffn_prev,), ch="t")
                m_CI_done = S.bar()

            if maxph < 4:
                sP.close()
                m_seq_entry = S.bar()
                continue
            sP.close()
            with ExitStack() as sD:
                eD = sD.enter_context
                stgB = eD(nc.sbuf_tensor(f"stgBd{sq}", [128, 8 * 512], BF16))
                EtEp_s = eD(nc.sbuf_tensor(f"EtEp_s{sq}", [128, 2 * 128 * H], F32))
                bcol_s = eD(nc.sbuf_tensor(f"bcol_s{sq}", [128, H * NKO], F32))
                F1_s = eD(nc.sbuf_tensor(f"F1_s{sq}", [128, 512], F32))
                ct_s = eD(nc.sbuf_tensor(f"ct_s{sq}", [128, 48], F32))
                with ExitStack() as sDlo:
                    eL = sDlo.enter_context
                    EtEp_lo = eL(nc.sbuf_tensor(f"EtEp_lo{sq}", [128, 4096], F32))
                    bcol_lo = eL(nc.sbuf_tensor(f"bcol_lo{sq}", [128, H * NKO], F32))
                    F1_lo = eL(nc.sbuf_tensor(f"F1_lo{sq}", [128, 512], F32))
                    ct_lo = eL(nc.sbuf_tensor(f"ct_lo{sq}", [128, 48], F32))
                    dma(bcol_s[:], ch_["bcol_hi"], wait=m_CI_done, ch="d")
                    dma(bcol_lo[:], ch_["bcol_lo"], ch="d")
                    dma(F1_s[:], ch_["F1_hi"], ch="d")
                    dma(F1_lo[:], ch_["F1_lo"], ch="d")
                    dma(ct_s[:], ch_["ct_hi"], ch="d")
                    dma(ct_lo[:], ch_["ct_lo"], ch="d")
                    dma(EtEp_s[:, 0:2048], ch_["Et_hi"], ch="d")
                    dma(EtEp_lo[:, 0:2048], ch_["Et_lo"], ch="d")
                    dma(EtEp_s[:, 2048:4096], ch_["Ep_hi"], ch="d")
                    m_tbl = dma(EtEp_lo[:, 2048:4096], ch_["Ep_lo"], ch="d")
                    m_d1 = ev("vector", "tensor_add", wait=(m_tbl,), out=bcol_s[:],
                              in0=bcol_s[:], in1=bcol_lo[:])
                    m_d2 = ev("vector", "tensor_add", wait=(m_tbl,), out=F1_s[:],
                              in0=F1_s[:], in1=F1_lo[:])
                    m_d3 = ev("vector", "tensor_add", wait=(m_tbl,), out=ct_s[:],
                              in0=ct_s[:], in1=ct_lo[:])
                    m_d4 = ev("vector", "tensor_add", wait=(m_tbl,),
                              out=EtEp_s[:, 0:2048],
                              in0=EtEp_s[:, 0:2048], in1=EtEp_lo[:, 0:2048])
                    m_d5 = ev("vector", "tensor_add", wait=(m_tbl,),
                              out=EtEp_s[:, 2048:4096],
                              in0=EtEp_s[:, 2048:4096], in1=EtEp_lo[:, 2048:4096])
                    m_dconst = (m_d1, m_d2, m_d3, m_d4, m_d5)
                wslot = eD(nc.sbuf_tensor(f"wslot{sq}", [128, 8 * C], BF16))
                qfmb = eD(nc.sbuf_tensor(f"qfmb{sq}", [128, 8 * T], BF16))
                kfmb = eD(nc.sbuf_tensor(f"kfmb{sq}", [128, 2 * T], BF16))
                vtmb = eD(nc.sbuf_tensor(f"vtmb{sq}", [128, NCH * 4 * 65], BF16))
                sD1 = ExitStack()
                eP = sD1.enter_context
                wkvb2 = eP(nc.sbuf_tensor(f"wkvb2{sq}", [128, 8 * 512], BF16))
                xlnb3 = [eP(nc.sbuf_tensor(f"xlnb3{sq}_{i}", [128, C], BF16))
                         for i in range(2)]
                xfm3 = [eP(nc.sbuf_tensor(f"xfm3{sq}_{i}", [128, C], BF16))
                        for i in range(2)]
                qtmp2 = eP(nc.sbuf_tensor(f"qtmp2{sq}", [128, C], BF16))
                ktmp2 = eP(nc.sbuf_tensor(f"ktmp2{sq}", [128, 256], BF16))

                qfmb3 = qfmb[:].rearrange("p (c t) -> p c t", c=8)
                kfmb3 = kfmb[:].rearrange("p (c t) -> p c t", c=2)
                vtmb4 = vtmb[:].rearrange("p (n g e) -> p n g e", n=NCH, g=4)

                mwqb = load_w_bf(wslot, w_h["wq_b"], C,
                                 tuple(m_CI_done) + m_dconst)
                mwkv = load_w_bf(wkvb2, w_h["wkv_b"], 512, (mwqb,))
                mvz = None
                for i_ in range(NCH):
                    mvz = ev("vector", "memset", wait=m_CI_done if i_ == 0 else (),
                             ap=vtmb[:, i_ * 4 * 65:(i_ + 1) * 4 * 65]
                             .rearrange("p (g e) -> p g e", g=4)[:, :, 64:65],
                             constant=1.0)

                m_prev3 = None
                m_kv_ev = None
                m_qb_ev = None
                for ch in range(NCH):
                    par = ch % 2
                    w_ = (mwkv, mvz)
                    if m_prev3:
                        w_ += (m_prev3,)
                    mln3 = ln_chunk(R[ch], xlnb3[par][:], None, w_)
                    mxf3 = transpose8(xlnb3[par][:],
                                      xfm3[par][:].rearrange("p (a b) -> p a b", a=8),
                                      tuple(mln3))
                    mqb = proj_tm(xfm3[par][:], wslot, PSW0, C,
                                  (mxf3, mwqb) + ((m_qb_ev,) if m_qb_ev else ()))
                    mqe2 = vcp(qtmp2[:], PSW0[:, 0:1024], wait=(mqb,))
                    m_qb_ev = mqe2
                    mqt3 = transpose8(qtmp2[:], qfmb3[:, :, ch * 128:(ch + 1) * 128],
                                      (mqe2,))
                    mkv = None
                    for ci in range(8):
                        mkv = mm(PSW1[:, 0:512], xfm3[par][:, ci * 128:(ci + 1) * 128],
                                 wkvb2[:, ci * 512:(ci + 1) * 512],
                                 start=(ci == 0), stop=(ci == 7),
                                 wait=(mxf3, mwkv) + ((m_kv_ev,) if m_kv_ev else ())
                                 if ci == 0 else ())
                    mke2 = vcp(ktmp2[:], PSW1[:, 0:256], wait=(mkv,))
                    mtk = []
                    for b in range(2):
                        mtk.append(transp(PT[:, b * 128:(b + 1) * 128],
                                          ktmp2[:, b * 128:(b + 1) * 128],
                                          cs["identb"][:],
                                          wait=(mke2, mqt3) if b == 0 else ()))
                    mkf2 = vcp(kfmb3[:, :, ch * 128:(ch + 1) * 128], PT[:, 0:256],
                               wait=(mtk[-1],))
                    mve2 = vcp(vtmb4[:, ch, :, 0:64],
                               PSW1[:, 256:512].rearrange("p (g d) -> p g d", g=4),
                               wait=(mkv, mvz))
                    m_kv_ev = mve2
                    m_prev3 = mkf2
                if sq == 0:
                    dtap("kfmb", kfmb[:], (m_prev3,))
                    dtap("qfmb", qfmb[:], (mqt3,))
                    dtap("vtmb", vtmb[:], (m_kv_ev,))
                    if "R_CII" in dbg_h:
                        for ch in range(NCH):
                            dma(dbg_h["R_CII"][ch * 128:(ch + 1) * 128, :], R[ch],
                                wait=(mln3[-1],), ch="t")
                m_CII_done = S.bar()

                sD1.close()
                ptile = eD(nc.sbuf_tensor(f"ptile{sq}", [128, 2 * 512], BF16))
                scf2 = eD(nc.sbuf_tensor(f"scf2{sq}", [128, 512], F32))
                lrow = eD(nc.sbuf_tensor(f"lrow{sq}", [1, 512], F32))
                f1row = eD(nc.sbuf_tensor(f"f1row{sq}", [1, 512], F32))
                obfm = eD(nc.sbuf_tensor(f"obfm{sq}", [128, 8 * 512], BF16))
                aqtm2 = eD(nc.sbuf_tensor(f"aqtm2{sq}", [128, C], BF16))
                aqfm2 = eD(nc.sbuf_tensor(f"aqfm2{sq}", [128, C], BF16))

                mwp = load_w_bf(wslot, w_h["wpr_b"], C, m_CII_done)

                m_ffn2_prev = None
                m_stgB2 = None
                m_f1_prev = None
                for tq in range(NTQ):
                    m_ob_ev = None
                    m_comb_prev = None
                    m_exp_par = [None, None]
                    m_pv_par = [None, None]
                    for h in range(H):
                        p, hh = h // 2, h % 2
                        kv = h % 4
                        kvt = kv // 2
                        upcls = []
                        m_pv_last = None
                        for ss in range(NSS):
                            ko = ss - 4 * tq
                            par2 = ss % 2
                            ps = (PS0, PS1)[par2]
                            sl = ptile[:, par2 * 512:(par2 + 1) * 512]
                            w_ = (mwp,)
                            if m_exp_par[par2]:
                                w_ += (m_exp_par[par2],)
                            mmks = mm(ps[:, 0:512],
                                      kfmb[hh * 64:(hh + 1) * 64,
                                           kvt * T + ss * 128: kvt * T + (ss + 1) * 128],
                                      qfmb[hh * 64:(hh + 1) * 64,
                                           p * T + tq * 512: p * T + (tq + 1) * 512],
                                      tile=(hh * 64, 0) if hh else None, wait=w_)
                            we_ = (mmks,)
                            if m_pv_par[par2]:
                                we_ += (m_pv_par[par2],)
                            if ko >= 4:
                                mfix = ev("scalar", "activation", wait=we_, out=sl,
                                          in_=ps[:, 0:512], func=AF.Exp, scale=DS)
                                is_up = True
                            else:
                                bc_ap = bcol_s[:, h * NKO + (ko - KOFF0):
                                               h * NKO + (ko - KOFF0) + 1]
                                mfix = ev("scalar", "activation", wait=we_, out=sl,
                                          in_=ps[:, 0:512], func=AF.Exp, scale=DS,
                                          bias=bc_ap)
                                if ko >= 0:
                                    for m2i in range(min(ko, 4)):
                                        cidx = h * 3 + (m2i - ko + 3)
                                        mfix = ev("vector", "scalar_tensor_tensor",
                                                  wait=(mfix,),
                                                  out=sl[:, m2i * 128:(m2i + 1) * 128],
                                                  in0=sl[:, m2i * 128:(m2i + 1) * 128],
                                                  scalar=ct_s[:, cidx:cidx + 1],
                                                  in1=Et_h(h), op0=ALU.mult, op1=ALU.mult)
                                    if ko <= 3:
                                        mfix = ev("vector", "tensor_tensor", wait=(mfix,),
                                                  out=sl[:, ko * 128:(ko + 1) * 128],
                                                  in0=sl[:, ko * 128:(ko + 1) * 128],
                                                  in1=Ep_h(h), op=ALU.mult)
                                is_up = False
                            m_exp_par[par2] = mfix
                            first = not any(u == is_up for u in upcls)
                            rest = [(s2 - 4 * tq) >= 4 for s2 in range(ss + 1, NSS)]
                            last = all(u != is_up for u in rest)
                            upcls.append(is_up)
                            dst = PSW0[0:65, 512:1024] if is_up else PSW0[0:65, 0:512]
                            wpv = (mfix,)
                            if first:
                                if m_comb_prev:
                                    wpv += (m_comb_prev,)
                                if m_ffn2_prev:
                                    wpv += (m_ffn2_prev,)
                            m_pv_last = mm(dst, vtmb4[:, ss, kv, :], sl,
                                           start=first, stop=last, wait=wpv)
                            m_pv_par[par2] = m_pv_last
                        any_up = any(upcls)
                        if any_up:
                            mf1r = mm(PSX[0:1, 0:512], cs["identf"][:, h:h + 1],
                                      F1_s[:],
                                      wait=(m_pv_last,) +
                                           ((m_comb_prev,) if m_comb_prev else ()))
                            mf1rv = vcp(f1row[0:1, 0:512], PSX[0:1, 0:512],
                                        wait=(mf1r,) +
                                             ((m_f1_prev,) if m_f1_prev else ()))
                            mf1 = mm(PSX[0:65, 0:512], cs["onesrow"][0:1, 0:65],
                                     f1row[0:1, :], wait=(mf1rv,))
                            m_f1_prev = mf1
                            mf1e = vcp(scf2[0:65, 0:512], PSX[0:65, 0:512], wait=(mf1,))
                            mcb1 = ev("vector", "tensor_tensor", wait=(mf1e,),
                                      out=scf2[0:65, 0:512], in0=PSW0[0:65, 512:1024],
                                      in1=scf2[0:65, 0:512], op=ALU.mult)
                            mcb2 = ev("vector", "tensor_add", wait=(mcb1,),
                                      out=scf2[0:65, 0:512], in0=scf2[0:65, 0:512],
                                      in1=PSW0[0:65, 0:512])
                        else:
                            mcb2 = vcp(scf2[0:65, 0:512], PSW0[0:65, 0:512],
                                       wait=(m_pv_last,) + ((m_comb_prev,) if m_comb_prev else ()))
                        mr1 = ev("vector", "tensor_scalar", wait=(mcb2,),
                                 out=lrow[0:1, 0:512], in0=scf2[64:65, 0:512],
                                 scalar1=1e-30, scalar2=None, op0=ALU.max)
                        mr1b = vrecip(lrow[0:1, 0:512], lrow[0:1, 0:512], wait=(mr1,))
                        mr2 = mm(PSX[0:64, 0:512], cs["onesrow"][0:1, 0:64],
                                 lrow[0:1, 0:512], wait=(mr1b,))
                        mr3 = vcp(scr[0:64, 0:512], PSX[0:64, 0:512],
                                  wait=(mr2,) + ln_gate[0])
                        if hh == 0:
                            mob = ev("vector", "tensor_tensor", wait=(mr3,),
                                     out=obfm[0:64, p * 512:(p + 1) * 512],
                                     in0=scf2[0:64, 0:512], in1=scr[0:64, 0:512],
                                     op=ALU.mult)
                        else:
                            mob = ev("vector", "tensor_tensor",
                                     wait=(mr3,) + ((m_stgB2,) if m_stgB2 else ()),
                                     out=stgB[0:64, p * 512:(p + 1) * 512],
                                     in0=scf2[0:64, 0:512], in1=scr[0:64, 0:512],
                                     op=ALU.mult)
                        m_comb_prev = mob
                        m_ob_ev = mob
                        ln_gate[0] = ln_gate[0] + (mob,)
                    m_stgB2 = dma(obfm[64:128, :], stgB[0:64, :], wait=(m_ob_ev,),
                                  ch="s")
                    m_ob_ev = m_stgB2
                    if sq == 0 and tq == 0:
                        dtap("obfm", obfm[:], (m_ob_ev, m_comb_prev))
                    for tc in range(4):
                        ch = tq * 4 + tc
                        mws = None
                        for ci in range(8):
                            for nh in range(2):
                                mws = mm(
                                    PSW1[:, nh * 512:(nh + 1) * 512],
                                    obfm[:, ci * 512 + tc * 128: ci * 512 + (tc + 1) * 128],
                                    wslot[:, ci * C + nh * 512: ci * C + (nh + 1) * 512],
                                    start=(ci == 0), stop=False,
                                    wait=(m_ob_ev, mwp) if (ci == 0 and nh == 0) else ())
                        for nh in range(2):
                            mws = mm(PSW1[:, nh * 512:(nh + 1) * 512],
                                     cs["onesrowb"][:], bprow[0:1, nh * 512:(nh + 1) * 512],
                                     start=False, stop=True, wait=(m_bp,))
                        mra2 = ev("vector", "tensor_add", wait=(mws,),
                                  out=R[ch], in0=R[ch], in1=PSW1[:, 0:1024])
                        mln4 = ln_chunk(R[ch], None, scr[:, 0:1024], (mra2,))
                        mam = ev("vector", "reduce_max", wait=(mln4[-1],), out=col(CAMAX),
                                 in_=scr[:, 0:1024], axis=AXX, apply_absolute_value=True)
                        mas_ = ev("vector", "tensor_scalar", wait=(mam,), out=col(CAMAX),
                                  in0=col(CAMAX), scalar1=float(FFN_EPS), scalar2=None,
                                  op0=ALU.add)
                        mrs = vrecip(col(CRED), col(CAMAX), wait=(mas_,))
                        ms127 = ev("vector", "tensor_scalar", wait=(mrs,), out=col(CRED),
                                   in0=col(CRED), scalar1=127.0, scalar2=None,
                                   op0=ALU.mult)
                        mosc = ev("vector", "tensor_scalar", wait=(mas_,), out=col(COSC),
                                  in0=col(CAMAX), scalar1=col(CWSD), scalar2=None,
                                  op0=ALU.mult)
                        mz1_ = ev("vector", "tensor_scalar", wait=(ms127,),
                                  out=scr[:, 0:1024], in0=scr[:, 0:1024],
                                  scalar1=col(CRED), scalar2=-128.0,
                                  op0=ALU.mult, op1=ALU.max)
                        mz2_ = ev("vector", "tensor_scalar", wait=(mz1_,),
                                  out=scr[:, 0:1024], in0=scr[:, 0:1024],
                                  scalar1=127.0, scalar2=None, op0=ALU.min)
                        mrd = ev("vector", "tensor_scalar", wait=(mz2_,), out=aqtm2[:],
                                 in0=scr[:, 0:1024], scalar1=col(CMAGIC),
                                 scalar2=col(CMAGIC), op0=ALU.add, op1=ALU.subtract)
                        ln_gate[0] = ln_gate[0] + (mrd,)
                        mqt4 = transpose8(aqtm2[:],
                                          aqfm2[:].rearrange("p (a b) -> p a b", a=8),
                                          (mrd,))
                        mfmm = proj_tm(aqfm2[:], wqffn, PSW0, C,
                                       (mqt4,) + ((m_ffn2_prev,) if m_ffn2_prev else ()))
                        mfin = ev("vector", "scalar_tensor_tensor", wait=(mfmm, mosc),
                                  out=R[ch], in0=PSW0[:, 0:1024], scalar=col(COSC),
                                  in1=R[ch], op0=ALU.mult, op1=ALU.add)
                        m_ffn2_prev = mfin
                        dma(out_h[sq * T + ch * 128: sq * T + (ch + 1) * 128, :],
                            R[ch], wait=(mfin,), ch="o")
                m_seq_entry = S.bar()


        with nc.Block() as block:
            def body_for(eng_name):
                def body(e):
                    for fn, waits, sem_inc in S.items[eng_name]:
                        for s_name, v in waits:
                            e.wait_ge(sems[s_name], v)
                        ins = fn(e)
                        ins.then_inc(sems[sem_inc[0]], sem_inc[1])
                    if eng_name == "sync":
                        for chn in DMACH:
                            if S.count[chn] > 0:
                                e.wait_ge(sems[chn], S.count[chn])
                return body
            block.sync(body_for("sync"))
            block.tensor(body_for("tensor"))
            block.vector(body_for("vector"))
            block.scalar(body_for("scalar"))
            block.gpsimd(body_for("gpsimd"))

    return nc


def kernel(**inputs):
    T = 2048
    NSEQ = 2
    ncores = 8
    x = np.ascontiguousarray(np.asarray(inputs["L0_attn_a_query"], np.float32)
                             .reshape(16, T, C))
    cst = host_consts(T)

    def wT(a):
        return np.ascontiguousarray(np.asarray(a, np.float32).T)

    bias_a = np.stack([np.asarray(inputs["L0_attn_a_bq"], np.float32),
                       np.asarray(inputs["L0_attn_a_bk"], np.float32),
                       np.asarray(inputs["L0_attn_a_bv"], np.float32),
                       np.asarray(inputs["L0_attn_a_bo"], np.float32)])
    shared = {
        "wq_a": wT(inputs["L0_attn_a_Wq"]), "wk_a": wT(inputs["L0_attn_a_Wk"]),
        "wv_a": wT(inputs["L0_attn_a_Wv"]), "wo_a": wT(inputs["L0_attn_a_Wo"]),
        "bias_a": np.ascontiguousarray(bias_a),
        "bqfm": np.ascontiguousarray(
            np.asarray(inputs["L0_attn_a_bq"], np.float32).reshape(8, 128).T),
        "wffn": wT(inputs["L0_ffn_weight"]),
        "wq_b": wT(inputs["L0_attn_b_Wq"]), "wkv_b": wT(inputs["L0_attn_b_Wkv"]),
        "wpr_b": wT(inputs["L0_attn_b_Wproj"]),
        "bpr_b": np.ascontiguousarray(
            np.asarray(inputs["L0_attn_b_bproj"], np.float32).reshape(1, C)),
        "temp": np.asarray(inputs["L0_attn_a_temperature"], np.float32).reshape(1, 1),
        "lidx": np.ascontiguousarray(
            np.asarray(inputs["L0_attn_a_landmark_idx"], np.int32).reshape(1, NL)),
    }
    for n, a in cst.items():
        if isinstance(a, np.ndarray):
            shared[n] = np.ascontiguousarray(a)
    nc = build_program(T=T, NSEQ=NSEQ)
    in_maps = []
    for core in range(ncores):
        m = dict(shared)
        m["x"] = np.ascontiguousarray(
            x[core * NSEQ:(core + 1) * NSEQ].reshape(NSEQ * T, C))
        in_maps.append(m)
    res = run_bass_kernel_spmd(nc, in_maps, core_ids=list(range(ncores)))
    out = np.concatenate(
        [np.asarray(r["out"]).reshape(NSEQ, T, C) for r in res.results], axis=0)
    return np.ascontiguousarray(out.astype(np.float32))


# version 10
# speedup vs baseline: 1.0471x; 1.0471x over previous
import numpy as np
import ml_dtypes
from contextlib import ExitStack

import concourse.bass as bass
import concourse.mybir as mybir
from concourse.bass_utils import run_bass_kernel_spmd

F32 = mybir.dt.float32
BF16 = mybir.dt.bfloat16
I32 = mybir.dt.int32
AF = mybir.ActivationFunctionType
ALU = mybir.AluOpType
AXX = mybir.AxisListType.X
NPBF = ml_dtypes.bfloat16

C = 1024
H = 16
NL = 64
REG = 1e-6
LN_EPS = 1e-5
FFN_EPS = 1e-8
MU_SCALE = 10752.0
NEWTON_ITERS = 31
DS = 0.125

ENGINES = ("sync", "tensor", "vector", "scalar", "gpsimd")
DMACH = ("c", "w", "x", "o", "y", "s", "t", "k", "l", "m", "n", "b", "d")


def _hilo(a):
    a = np.asarray(a, np.float64)
    hi = a.astype(np.float32)
    lo = (a - hi).astype(np.float32)
    return hi, lo


def host_consts(T):
    nss = T // 128
    sp = ((2.0 ** (-0.5)) ** np.arange(1, 17)) * 0.125
    cst = {}
    cst["identb"] = np.eye(128).astype(NPBF)
    cst["identf"] = np.eye(128).astype(np.float32)
    cst["onesrow"] = np.ones((1, 128), np.float32)
    cst["onescol"] = np.ones((128, 1), np.float32)
    cst["onesrowb"] = np.ones((1, 128), NPBF)
    cst["iotac"] = np.arange(128, dtype=np.float32).reshape(128, 1)
    cst["i64d"] = np.eye(64, dtype=np.float32)
    cst["i2stk"] = np.concatenate([np.eye(64), np.eye(64)], 0).astype(np.float32)
    cst["i2rep"] = np.tile(cst["i2stk"], (1, 8)).astype(np.float32)
    koffs = list(range(-(nss - 1), 4))
    nko = len(koffs)
    bc = np.zeros((128, H * nko))
    for h in range(H):
        for oi, ko in enumerate(koffs):
            bc[:, h * nko + oi] = sp[h] * (np.arange(128) + 128.0 * ko)
    cst["bcol_hi"], cst["bcol_lo"] = _hilo(bc)
    ii = np.arange(128)[:, None]
    jj = np.arange(128)[None, :]
    Et = np.zeros((128, 128 * H))
    Ep = np.zeros((128, 128 * H))
    for h in range(H):
        e = np.exp(sp[h] * (jj - ii))
        Et[:, h * 128:(h + 1) * 128] = e
        Ep[:, h * 128:(h + 1) * 128] = np.where(ii > jj, e, 1.0)
    cst["Et_hi"], cst["Et_lo"] = _hilo(Et)
    cst["Ep_hi"], cst["Ep_lo"] = _hilo(Ep)
    F1 = np.zeros((128, 512))
    for h in range(H):
        F1[h, :] = np.exp(sp[h] * np.arange(512))
    cst["F1_hi"], cst["F1_lo"] = _hilo(F1)
    cc = np.zeros((128, 48))
    for h in range(H):
        for m in range(3):
            cc[:, h * 3 + m] = np.exp(sp[h] * 128.0 * (m - 3))
    cst["ct_hi"], cst["ct_lo"] = _hilo(cc)
    consts = [1.0 / 1024, -1.0, 1e-5, 1e-8, 127.0, -128.0, 1.0, 0.5, 1.5, 2.0,
              3.0, 4.0, 10752.0, 1e-6, 1.0 / 127.0, 1.0 / 2 ** 20, 1.0 / 65.0,
              1e-10, 1e-30, float(T - 1), 0.0, float(2 ** 23), float(2.0 ** -32)]
    consts += [float(i * 128) for i in range(T // 128)]
    ctab = np.zeros((128, 23 + T // 128), np.float32)
    for i, v in enumerate(consts):
        ctab[:, i] = np.float32(v)
    cst["ctable"] = ctab
    cst["sc32row"] = np.full((1, 128), np.float32(2.0 ** -32), np.float32)
    cst["_koffs"] = koffs
    return cst


class Sched:
    def __init__(self):
        self.items = {e: [] for e in ENGINES}
        self.count = {e: 0 for e in ENGINES}
        for chn in DMACH:
            self.count[chn] = 0

    def emit(self, eng, fn, wait=(), ch=None):
        if eng == "sync":
            assert ch in DMACH, ch
            self.count[ch] += 16
            sem_inc = (ch, 16)
        else:
            self.count[eng] += 1
            sem_inc = (eng, 1)
        w = {}
        for m in wait:
            if m is None:
                continue
            s, v = m
            w[s] = max(w.get(s, 0), v)
        self.items[eng].append((fn, sorted(w.items()), sem_inc))
        return (sem_inc[0], self.count[sem_inc[0]])

    def bar(self):
        keys = [e for e in ENGINES if e != "sync"] + list(DMACH)
        return tuple((e, self.count[e]) for e in keys if self.count[e] > 0)


def build_program(T=2048, NSEQ=2, debug=(), maxph=9):
    nc = bass.Bass("TRN2", target_bir_lowering=False)
    NCH = T // 128
    NTQ = T // 512
    NSS = T // 128
    NTB = T // 512
    cst = host_consts(T)
    KOFFS = cst["_koffs"]
    NKO = len(KOFFS)
    KOFF0 = KOFFS[0]

    def dI(name, shape, dt=F32):
        return nc.dram_tensor(name, list(shape), dt, kind="ExternalInput").ap()

    x_h = dI("x", (NSEQ * T, C))
    w_h = {n: dI(n, (C, C)) for n in ("wq_a", "wk_a", "wv_a", "wo_a", "wffn",
                                      "wq_b", "wpr_b")}
    w_h["wkv_b"] = dI("wkv_b", (C, 512))
    bias_a_h = dI("bias_a", (4, C))
    bqfm_h = dI("bqfm", (128, 8))
    bpr_b_h = dI("bpr_b", (1, C))
    temp_h = dI("temp", (1, 1))
    lidx_h = dI("lidx", (1, NL), I32)
    ch_ = {}
    for n, a in cst.items():
        if isinstance(a, np.ndarray):
            ch_[n] = dI(n, a.shape, BF16 if a.dtype == NPBF else F32)
    out_h = nc.dram_tensor("out", [NSEQ * T, C], F32, kind="ExternalOutput").ap()
    _dbg_specs = {
        "eKt": ([128, (T // 128) * H], F32), "ltm": ([64, C], BF16),
        "nlb": ([64, H], F32), "cqb": ([64, H], F32), "zbacc": ([128, 8 * 65], F32),
        "w0sb": ([128, 8 * 64], F32), "ablk": ([128, 8 * 128], F32),
        "mstk": ([128, 8 * 64], F32), "ysol": ([64, H * 65], BF16),
        "l2fm": ([128, 8 * 64], BF16), "qfm": ([128, 8 * 512], BF16),
        "atfm": ([128, 8 * 512], BF16), "R_CI": ([T, C], F32),
        "kfmb": ([128, 2 * T], BF16), "qfmb": ([128, 8 * T], BF16),
        "vtmb": ([128, (T // 128) * 4 * 65], BF16), "obfm": ([128, 8 * 512], BF16),
        "kfm": ([128, 8 * T], BF16), "vtm": ([128, (T // 128) * H * 65], BF16),
        "phiktm": ([128, 512], BF16), "R_CII": ([T, C], F32),
        "ypsx": ([128, 65], F32), "mblk0": ([128, 128], F32),
        "ohb": ([128, 4 * 64], BF16), "w0psw": ([128, 512], F32),
        "ccol": ([128, 32], F32), "ctable": ([128, 23 + T // 128], F32), "biasT": ([128, 2 * C], F32),
        "wqffn": ([128, 8 * C], BF16), "idxb": ([128, NL], F32),
        "R0": ([128, C], F32), "xlnb0": ([128, C], BF16), "xfm0": ([128, C], BF16),
        "ktm0": ([128, C], BF16),
    }
    dbg_h = {n: nc.dram_tensor("dbg_" + n, list(_dbg_specs[n][0]), _dbg_specs[n][1],
                               kind="ExternalOutput").ap() for n in debug}
    kfm_d = nc.dram_tensor("kfm_sp", [128, 8 * T], BF16, kind="Internal").ap()
    kfm_d3 = kfm_d.rearrange("p (c t) -> p c t", c=8)
    biasT_d = nc.dram_tensor("biasT_sp", [128, 2 * C], F32, kind="Internal").ap()

    S = Sched()

    with ExitStack() as ctx:
        ent = ctx.enter_context
        sems = {e: ent(nc.semaphore("sem_" + e)) for e in ENGINES}
        for chn in DMACH:
            sems[chn] = ent(nc.semaphore("sem_" + chn))

        Rall = ent(nc.sbuf_tensor("Rall", [128, NCH * C], F32))
        R = [Rall[:, i * C:(i + 1) * C] for i in range(NCH)]
        wqffn = ent(nc.sbuf_tensor("wqffn", [128, 8 * C], BF16))
        cs = {}
        for n in ("identb", "identf", "onesrow", "onescol", "onesrowb", "iotac",
                  "i64d", "i2stk", "i2rep", "ctable", "sc32row"):
            cs[n] = ent(nc.sbuf_tensor(n + "_s", list(cst[n].shape),
                                       BF16 if cst[n].dtype == NPBF else F32))
        idxb_s = ent(nc.sbuf_tensor("idxb_s", [128, NL], F32))
        stage = ent(nc.sbuf_tensor("stage", [128, 1024], F32))
        stage2 = ent(nc.sbuf_tensor("stage2", [128, 128], F32)) if debug else None
        bprow = ent(nc.sbuf_tensor("bprow", [1, C], BF16))
        bqfm_s = ent(nc.sbuf_tensor("bqfm_s", [128, 8], BF16))
        ccol = ent(nc.sbuf_tensor("ccol", [128, 32], F32))
        crow = ent(nc.sbuf_tensor("crow", [1, 64], F32))
        scr = ent(nc.sbuf_tensor("scr", [128, 1024], F32))

        PSW0 = ent(nc.psum_tensor("PSW0", [128, 1024], F32))
        PSW1 = ent(nc.psum_tensor("PSW1", [128, 1024], F32))
        PS0 = ent(nc.psum_tensor("PS0", [128, 512], F32))
        PS1 = ent(nc.psum_tensor("PS1", [128, 512], F32))
        PT = ent(nc.psum_tensor("PT", [128, 1024], BF16))
        PSX = ent(nc.psum_tensor("PSX", [128, 512], F32))

        (CTEMP, CINVT, CNEGINVT, CEPS, CMU, CREGMU, CMAGIC, CWS, CWSD, CRWS,
         CACC, CRED, CSUM, CSQ, CMEAN, CVEPS, CRSTD, CNMR, CAMAX, COSC) = range(20)

        def col(i):
            return ccol[:, i:i + 1]

        def Et_h(h):
            return EtEp_s[:, h * 128:(h + 1) * 128]

        def Ep_h(h):
            return EtEp_s[:, 2048 + h * 128:2048 + (h + 1) * 128]

        _KIDX = {1.0 / 1024: 0, -1.0: 1, 1e-5: 2, 1e-8: 3, 127.0: 4, -128.0: 5,
                 1.0: 6, 0.5: 7, 1.5: 8, 2.0: 9, 3.0: 10, 4.0: 11, 10752.0: 12,
                 1e-6: 13, 1.0 / 127.0: 14, 1.0 / 2 ** 20: 15, 1.0 / 65.0: 16,
                 1e-10: 17, 1e-30: 18, float(T - 1): 19, 0.0: 20,
                 float(2 ** 23): 21, float(2.0 ** -32): 22}
        for _i in range(T // 128):
            _KIDX.setdefault(float(_i * 128), 23 + _i)
        ctab_ready = [None]

        def KC(v, rows=128):
            i = _KIDX[float(v)]
            return cs["ctable"][0:rows, i:i + 1]

        def dma(dst, src, wait=(), ch="c"):
            return S.emit("sync", lambda e, d=dst, s=src: e.dma_start(out=d, in_=s),
                          wait, ch=ch)

        def ev(eng, name, wait=(), drain=None, **kw):
            if eng == "vector":
                conv = False
                for k in ("scalar1", "scalar2", "scalar"):
                    v = kw.get(k)
                    if isinstance(v, float):
                        in0 = kw.get("in0")
                        if in0 is not None:
                            bp = in0.base_partition
                            if callable(bp):
                                bp = bp()
                            npart = in0.shape[0]
                        else:
                            bp, npart = 0, 128
                        i = _KIDX[float(v)]
                        kw[k] = cs["ctable"][bp:bp + npart, i:i + 1]
                        conv = True
                if conv and ctab_ready[0] is not None:
                    wait = tuple(wait) + (ctab_ready[0],)
            elif eng == "scalar":
                _is_copy = kw.get("func") in (AF.Copy, AF.Identity)
                for k in ("scale", "bias"):
                    v = kw.get(k)
                    if isinstance(v, float) and not _is_copy:
                        kw[k] = np.float32(v)
            _drain = eng == "vector" and name in (
                "reduce_sum", "reduce_max", "tensor_reduce", "reciprocal",
                "bn_stats", "bn_aggr")
            if drain is not None:
                _drain = drain

            import traceback as _tb
            _site = _tb.extract_stack(limit=3)[-2]

            def fn(e, _n=name, _kw=dict(kw), _d=_drain, _s=_site):
                try:
                    ins = getattr(e, _n)(**_kw)
                except Exception as ex:
                    raise RuntimeError(
                        f"ev scheduled at {_s.filename}:{_s.lineno}: {ex}") from ex
                if _d:
                    ins = e.drain()
                return ins
            return S.emit(eng, fn, wait)

        def mm(out, lhsT, rhs, start=True, stop=True, tile=None, wait=()):
            import traceback
            site = traceback.extract_stack(limit=3)[-2]

            def fn(e, o=out, l=lhsT, r=rhs, st=start, sp=stop, tp=tile, _s=site):
                try:
                    return e.matmul(o, l, r, start=st, stop=sp, tile_position=tp)
                except Exception as ex:
                    raise RuntimeError(
                        f"mm scheduled at {_s.filename}:{_s.lineno}: {ex}") from ex
            return S.emit("tensor", fn, wait)

        def transp(out, in_, ident, wait=(), tile_position=None):
            def fn(e, o=out, i=in_, d=ident, tp=tile_position):
                return e.transpose(o, i, d, tile_position=tp)
            return S.emit("tensor", fn, wait)

        def vcp(dst, src, wait=()):
            return ev("vector", "tensor_copy", wait=wait, out=dst, in_=src)

        def vrecip(out, in_, wait=()):
            def fn(e, o=out, i=in_):
                e.drain()
                e.reciprocal(out=o, in_=i)
                return e.drain()
            return S.emit("vector", fn, wait)

        def dtap(name, srcs, wait):
            if name in dbg_h:
                return dma(dbg_h[name], srcs, wait=wait, ch="t")
            return None

        with ExitStack() as sctx0:
            e0 = sctx0.enter_context
            biasT0 = e0(nc.sbuf_tensor("biasT0", [128, 2 * C], F32))
            land4 = e0(nc.sbuf_tensor("land4", [1, 4 * C], F32))
            land1 = e0(nc.sbuf_tensor("land1", [1, C], F32))
            bqstg = e0(nc.sbuf_tensor("bqstg", [128, 8], F32))
            idx_i32 = e0(nc.sbuf_tensor("idx_i32", [1, NL], I32))

            for n in cs:
                dma(cs[n][:], ch_[n])
            dma(ccol[0:1, CTEMP:CTEMP + 1], temp_h)
            dma(idx_i32[:], lidx_h)
            dma(land4[:], bias_a_h.rearrange("(o a) c -> o (a c)", o=1))
            dma(land1[:], bpr_b_h)
            m_call = dma(bqstg[:], bqfm_h)
            ctab_ready[0] = m_call

            m_bq = ev("vector", "tensor_copy", wait=(m_call,), out=bqfm_s[:],
                      in_=bqstg[:])

            m1 = vrecip(ccol[0:1, CINVT:CINVT + 1], ccol[0:1, CTEMP:CTEMP + 1],
                        wait=(m_call,))
            m2 = ev("vector", "memset", wait=(), ap=ccol[0:1, CEPS:CEPS + 1],
                    constant=3.0)
            m2b = vrecip(ccol[0:1, CEPS:CEPS + 1], ccol[0:1, CEPS:CEPS + 1], wait=(m2,))
            m3 = ev("vector", "tensor_scalar", wait=(m2b,), out=ccol[0:1, CEPS:CEPS + 1],
                    in0=ccol[0:1, CEPS:CEPS + 1], scalar1=4.0, scalar2=1.0,
                    op0=ALU.mult, op1=ALU.subtract)
            m4 = ev("vector", "tensor_scalar", wait=(m3,), out=ccol[0:1, CEPS:CEPS + 1],
                    in0=ccol[0:1, CEPS:CEPS + 1], scalar1=3.0, scalar2=1.0,
                    op0=ALU.mult, op1=ALU.subtract)
            m5 = ev("scalar", "activation", wait=(m4,), out=ccol[0:1, CEPS:CEPS + 1],
                    in_=ccol[0:1, CEPS:CEPS + 1], func=AF.Abs)
            m6 = ev("vector", "tensor_scalar", wait=(m5,), out=ccol[0:1, CMU:CMU + 1],
                    in0=ccol[0:1, CEPS:CEPS + 1], scalar1=10752.0, scalar2=None,
                    op0=ALU.mult)
            m6a = ev("vector", "tensor_scalar", wait=(m5,),
                     out=ccol[0:1, CSUM:CSUM + 1],
                     in0=ccol[0:1, CEPS:CEPS + 1], scalar1=float(2 ** 23),
                     scalar2=None, op0=ALU.mult)
            m6b = ev("vector", "tensor_tensor", wait=(m6, m6a),
                     out=ccol[0:1, CMU:CMU + 1], in0=ccol[0:1, CMU:CMU + 1],
                     in1=ccol[0:1, CSUM:CSUM + 1], op=ALU.mult)
            m7 = ev("vector", "tensor_scalar", wait=(m6b,), out=ccol[0:1, CREGMU:CREGMU + 1],
                    in0=ccol[0:1, CMU:CMU + 1], scalar1=1e-6, scalar2=None, op0=ALU.add)
            m8 = vrecip(ccol[0:1, CMAGIC:CMAGIC + 1], ccol[0:1, CEPS:CEPS + 1],
                        wait=(m5,))
            m9 = ev("vector", "tensor_scalar", wait=(m8,), out=ccol[0:1, CMAGIC:CMAGIC + 1],
                    in0=ccol[0:1, CMAGIC:CMAGIC + 1], scalar1=1.5, scalar2=None,
                    op0=ALU.mult)
            m10 = ev("vector", "tensor_scalar", wait=(m1,),
                     out=ccol[0:1, CNEGINVT:CNEGINVT + 1],
                     in0=ccol[0:1, CINVT:CINVT + 1], scalar1=-1.0, scalar2=None,
                     op0=ALU.mult)
            m11 = mm(PSX[:, 0:7], cs["onesrow"][:], ccol[0:1, 0:7],
                     wait=(m7, m9, m10, m_call))
            m_cols = vcp(ccol[:, 0:7], PSX[:, 0:7], wait=(m11,))

            mi2 = vcp(crow[0:1, 0:NL], idx_i32[:], wait=(m_call,))
            mi3 = ev("vector", "tensor_scalar", wait=(mi2,), out=crow[0:1, 0:NL],
                     in0=crow[0:1, 0:NL], scalar1=0.0, scalar2=float(T - 1),
                     op0=ALU.max, op1=ALU.min)
            mi4 = mm(PSX[:, 16:16 + NL], cs["onesrow"][:], crow[0:1, 0:NL],
                     wait=(mi3, m_cols))
            m_idx = vcp(idxb_s[:], PSX[:, 16:16 + NL], wait=(mi4,))

            m_bt = (m_idx,)
            for bi, row in ((0, 1), (1, 2)):
                ma = mm(PSW0[:, 0:512], cs["onesrow"][:],
                        land4[0:1, row * C:row * C + 512],
                        wait=m_bt + (m_call,))
                mb = mm(PSW0[:, 512:1024], cs["onesrow"][:],
                        land4[0:1, row * C + 512:row * C + 1024],
                        wait=())
                m_bt = (ev("vector", "tensor_copy", wait=(ma, mb),
                           out=biasT0[:, bi * C:(bi + 1) * C], in_=PSW0[:, 0:1024]),)
            m_btsp = dma(biasT_d, biasT0[:], wait=m_bt, ch="c")
            m_bp = vcp(bprow[:], land1[:], wait=(m_call,))

            mz = ev("vector", "memset", wait=(), ap=col(CACC), constant=0.0)
            m_p = (mz, m_bp, m_cols)
            for ci in range(8):
                md = dma(stage[:, 0:1024], w_h["wffn"][ci * 128:(ci + 1) * 128, :],
                         wait=m_p, ch="w")
                mrd = ev("vector", "reduce_sum", wait=(md,), out=col(CRED),
                         in_=stage[:, 0:1024], axis=AXX, apply_absolute_value=True)
                ma = ev("vector", "tensor_add", wait=(mrd,), out=col(CACC),
                        in0=col(CACC), in1=col(CRED))
                m_p = (md, ma)
            mw1 = mm(PSX[0:1, 30:31], col(CACC), cs["onescol"][:], wait=m_p + (m_call,))
            mw2 = ev("vector", "tensor_scalar", wait=(mw1,), out=crow[0:1, 32:33],
                     in0=PSX[0:1, 30:31], scalar1=float(1.0 / 2 ** 20), scalar2=1e-8,
                     op0=ALU.mult, op1=ALU.add)
            mw3 = vrecip(crow[0:1, 33:34], crow[0:1, 32:33], wait=(mw2,))
            mw4 = ev("vector", "tensor_scalar", wait=(mw2,), out=crow[0:1, 34:35],
                     in0=crow[0:1, 32:33], scalar1=float(1.0 / 127.0), scalar2=None,
                     op0=ALU.mult)
            mw5 = mm(PSX[:, 32:35], cs["onesrow"][:], crow[0:1, 32:35], wait=(mw3, mw4))
            mw6 = vcp(col(CWS), PSX[:, 32:33], wait=(mw5,))
            mw7 = vcp(col(CRWS), PSX[:, 33:34], wait=(mw5,))
            mw8 = vcp(col(CWSD), PSX[:, 34:35], wait=(mw5,))
            m_p = (mw6, mw7, mw8)
            for ci in range(8):
                md = dma(stage[:, 0:1024], w_h["wffn"][ci * 128:(ci + 1) * 128, :],
                         wait=m_p, ch="w")
                q1 = ev("vector", "tensor_scalar", wait=(md,), out=stage[:, 0:1024],
                        in0=stage[:, 0:1024], scalar1=col(CRWS), scalar2=None,
                        op0=ALU.mult)
                q2 = ev("vector", "tensor_scalar", wait=(q1,), out=stage[:, 0:1024],
                        in0=stage[:, 0:1024], scalar1=col(CMAGIC), scalar2=col(CMAGIC),
                        op0=ALU.add, op1=ALU.subtract)
                q3 = ev("vector", "tensor_scalar", wait=(q2,),
                        out=wqffn[:, ci * C:(ci + 1) * C], in0=stage[:, 0:1024],
                        scalar1=-1.0, scalar2=1.0, op0=ALU.max, op1=ALU.min)
                m_p = (md, q3)
        m_start_done = S.bar()

        ln_gate = [()]

        def ln_chunk(src, out_bf, out_f32, wait):
            ms = ev("vector", "reduce_sum", wait=tuple(wait) + ln_gate[0],
                    out=col(CSUM), in_=src, axis=AXX, drain=False)
            mq = ev("scalar", "activation", wait=tuple(wait) + ln_gate[0],
                    out=scr[:, 0:1024],
                    in_=src, func=AF.Square, accum_out=col(CSQ))
            mmn = ev("scalar", "activation", wait=(ms,), out=col(CMEAN),
                     in_=col(CSUM), func=AF.Copy, scale=float(1.0 / C))
            mmnn = ev("scalar", "activation", wait=(ms,), out=col(CNMR),
                      in_=col(CSUM), func=AF.Copy, scale=float(-1.0 / C))
            mv1 = ev("scalar", "activation", wait=(mq,), out=col(CVEPS),
                     in_=col(CSQ), func=AF.Copy, scale=float(1.0 / C))
            mv2 = ev("vector", "scalar_tensor_tensor", wait=(mmn, mv1), out=col(CVEPS),
                     in0=col(CMEAN), scalar=col(CMEAN), in1=col(CVEPS),
                     op0=ALU.mult, op1=ALU.subtract)
            msq = ev("scalar", "activation", wait=(mv2,), out=col(CRSTD),
                     in_=col(CVEPS), func=AF.Sqrt, scale=float(-1.0),
                     bias=float(LN_EPS))
            mrc = ev("vector", "reciprocal", wait=(msq,), out=col(CRSTD),
                     in_=col(CRSTD), drain=False)
            mnm = ev("scalar", "activation", wait=(mrc, mmnn), out=col(CNMR),
                     in_=col(CNMR), func=AF.Copy, scale=col(CRSTD))
            mrs2 = ev("scalar", "activation", wait=(mrc,), out=col(CRSTD),
                      in_=col(CRSTD), func=AF.Copy, scale=1.0)
            outs = []
            if out_bf is not None:
                outs.append(ev("vector", "tensor_scalar", wait=(mnm, mrs2),
                               out=out_bf, in0=src, scalar1=col(CRSTD),
                               scalar2=col(CNMR), op0=ALU.mult, op1=ALU.add))
            if out_f32 is not None:
                outs.append(ev("vector", "tensor_scalar", wait=(mnm, mrs2),
                               out=out_f32, in0=src, scalar1=col(CRSTD),
                               scalar2=col(CNMR), op0=ALU.mult, op1=ALU.add))
            ln_gate[0] = (mv2, outs[-1], mrc)
            return outs

        def transpose8(src_bf, dst_bf_3d, wait, nblk=8, vwait=()):
            mts = []
            for b in range(nblk):
                mts.append(transp(PT[:, b * 128:(b + 1) * 128],
                                  src_bf[:, b * 128:(b + 1) * 128], cs["identb"][:],
                                  wait=wait if b == 0 else ()))
            return vcp(dst_bf_3d, PT[:, 0:nblk * 128], wait=(mts[-1],) + tuple(vwait))

        def proj_tm(xfm_ap, w_sb, psw, nout, wait):
            last = None
            nh_cnt = (nout + 511) // 512
            for ci in range(8):
                for nh in range(nh_cnt):
                    n0, n1 = nh * 512, min(nout, (nh + 1) * 512)
                    last = mm(psw[:, n0:n1], xfm_ap[:, ci * 128:(ci + 1) * 128],
                              w_sb[:, ci * nout + n0: ci * nout + n1],
                              start=(ci == 0), stop=(ci == 7),
                              wait=wait if (ci == 0 and nh == 0) else ())
            return last

        def load_w_bf(dst, src_h, ncols, wait):
            m_prev = wait
            last = None
            for ci in range(8):
                md = dma(stage[:, 0:ncols], src_h[ci * 128:(ci + 1) * 128, 0:ncols],
                         wait=m_prev, ch="w")
                mv = vcp(dst[:, ci * ncols:(ci + 1) * ncols], stage[:, 0:ncols],
                         wait=(md,))
                m_prev = (md, mv)
                last = mv
            return last

        m_seq_entry = m_start_done
        for sq in range(NSEQ):
            sP = ExitStack()
            zbacc = sP.enter_context(nc.sbuf_tensor(f"zbacc{sq}", [128, 8 * 65], F32))
            w0acc = sP.enter_context(nc.sbuf_tensor(f"w0acc{sq}", [128, 8 * 64], F32))
            ysol = sP.enter_context(nc.sbuf_tensor(f"ysol{sq}", [64, H * 65], BF16))
            l2fm = sP.enter_context(nc.sbuf_tensor(f"l2fm{sq}", [128, 8 * 64], BF16))
            cqb = sP.enter_context(nc.sbuf_tensor(f"cqb{sq}", [64, H], F32))
            with ExitStack() as sA:
                eA = sA.enter_context
                vtm = eA(nc.sbuf_tensor(f"vtm{sq}", [128, NCH * H * 65], BF16))
                eKt = eA(nc.sbuf_tensor(f"eKt{sq}", [128, NCH * H], F32))
                ktm_t = [eA(nc.sbuf_tensor(f"ktm{sq}_{i}", [128, C], BF16))
                         for i in range(2)]
                xlnb = [eA(nc.sbuf_tensor(f"xlnb{sq}_{i}", [128, C], BF16))
                        for i in range(2)]
                xfm = [eA(nc.sbuf_tensor(f"xfm{sq}_{i}", [128, C], BF16))
                       for i in range(2)]
                ltm = eA(nc.sbuf_tensor(f"ltm{sq}", [64, C], BF16))
                nlb = eA(nc.sbuf_tensor(f"nlb{sq}", [64, H], F32))
                phikp = eA(nc.sbuf_tensor(f"phikp{sq}", [64, 2 * 512], BF16))
                phiktm = eA(nc.sbuf_tensor(f"phiktm{sq}", [128, 512], BF16))
                ohb = eA(nc.sbuf_tensor(f"ohb{sq}", [128, 4 * 64], BF16))
                biasT = eA(nc.sbuf_tensor(f"biasTa{sq}", [128, 2 * C], F32))
                sA1 = ExitStack()
                wkb = sA1.enter_context(nc.sbuf_tensor(f"wkb{sq}", [128, 8 * C], BF16))
                wvb = sA1.enter_context(nc.sbuf_tensor(f"wvb{sq}", [128, 8 * C], BF16))
                kstg = sA1.enter_context(nc.sbuf_tensor(f"kstg{sq}", [128, 2 * 1024],
                                                        BF16))

                vtm4 = vtm[:].rearrange("p (n h e) -> p n h e", n=NCH, h=H)

                m_bias = dma(biasT[:], biasT_d, wait=(m_btsp,), ch="b")
                mwk = load_w_bf(wkb, w_h["wk_a"], C, m_seq_entry)
                mwv = load_w_bf(wvb, w_h["wv_a"], C, (mwk,))
                mz1 = ev("vector", "memset", wait=m_seq_entry, ap=zbacc[:], constant=0.0)
                mzw = ev("vector", "memset", wait=(), ap=w0acc[:], constant=0.0)
                mvo = mz1
                for i_ in range(NCH):
                    mvo = ev("vector", "memset", wait=m_seq_entry if i_ == 0 else (),
                             ap=vtm[:, i_ * H * 65:(i_ + 1) * H * 65]
                             .rearrange("p (h e) -> p h e", h=H)[:, :, 64:65],
                             constant=1.0)

                m_x = dma(Rall[:].rearrange("p (n c) -> p n c", n=NCH),
                          x_h[sq * T:(sq + 1) * T, :]
                          .rearrange("(n p) c -> p n c", p=128),
                          wait=m_seq_entry, ch="x")
                m_kevac = [None, None]
                m_spill = [None, None]
                m_kcons = [None, None]
                m_vcons = None
                m_xfm_prev = None
                m_kfm_last = None
                m_lmm_last = None
                m_ek_last = None
                m_vev_last = None
                for i in range(NCH):
                    par = i % 2
                    mx = m_x
                    w0_ = (mx, mwv) + (m_kcons[par] if m_kcons[par] else ())
                    mln = ln_chunk(R[i], xlnb[par][:], None, w0_)
                    mxf = transpose8(xlnb[par][:],
                                     xfm[par][:].rearrange("p (a b) -> p a b", a=8),
                                     tuple(mln) + ((m_xfm_prev,) if m_xfm_prev else ()))
                    m_xfm_prev = mxf
                    mk_mm = proj_tm(xfm[par][:], wkb, PSW0, C,
                                    (mxf, mwk) + ((m_kevac[1 - par],) if m_kevac[1 - par] else ()))
                    mke = ev("vector", "tensor_add", wait=(mk_mm, m_bias),
                             out=ktm_t[par][:],
                             in0=PSW0[:, 0:1024], in1=biasT[:, 0:C])
                    m_kevac[par] = mke
                    if sq == 0 and i == 0 and "ccol" in dbg_h:
                        mcc = vcp(stage2[:, 0:32], ccol[:, 0:32], wait=tuple(mln))
                        dtap("ccol", stage2[:, 0:32], (mcc,))
                        dtap("ctable", cs["ctable"][:], (mcc,))
                    if sq == 0 and i == 0:
                        dtap("R0", R[0], (mx,))
                        dtap("xlnb0", xlnb[0][:], tuple(mln))
                        dtap("xfm0", xfm[0][:], (mxf,))
                        dtap("ktm0", ktm_t[0][:], (mke,))
                        dtap("biasT", biasT[:], (m_bias,))
                        dtap("wqffn", wqffn[:], m_p)
                        dtap("idxb", idxb_s[:], (m_idx,))
                    msq2 = ev("scalar", "activation", wait=(mke,), out=scr[:, 0:1024],
                              in_=ktm_t[par][:], func=AF.Square)
                    mnk = ev("vector", "reduce_sum", wait=(msq2,),
                             out=eKt[:, i * H:(i + 1) * H],
                             in_=scr[:, 0:1024].rearrange("p (h d) -> p h d", h=H),
                             axis=AXX, drain=False)
                    ln_gate[0] = ln_gate[0] + (mnk,)
                    mek = ev("scalar", "activation", wait=(mnk, m_cols),
                             out=eKt[:, i * H:(i + 1) * H],
                             in_=eKt[:, i * H:(i + 1) * H], func=AF.Exp,
                             scale=col(CNEGINVT))
                    m_ek_last = mek
                    kstg3 = kstg[:, par * 1024:(par + 1) * 1024].rearrange(
                        "p (c u) -> p c u", c=8)
                    mkf = transpose8(ktm_t[par][:], kstg3, (mke, mxf),
                                     vwait=(m_spill[par],) if m_spill[par] else ())
                    m_kfm_last = mkf
                    m_spill[par] = dma(kfm_d3[:, :, i * 128:(i + 1) * 128], kstg3,
                                       wait=(mkf,), ch="k" if par == 0 else "l")
                    moh = ev("vector", "tensor_scalar", wait=(m_idx,),
                             out=ohb[:, 0:64], in0=idxb_s[:], scalar1=cs["iotac"][:],
                             scalar2=float(i * 128), op0=ALU.subtract, op1=ALU.is_equal)
                    ml1 = mm(PS0[0:64, 0:512], ohb[:, 0:64], ktm_t[par][:, 0:512],
                             start=(i == 0), stop=(i == NCH - 1), wait=(moh, mke))
                    ml2 = mm(PS1[0:64, 0:512], ohb[:, 0:64], ktm_t[par][:, 512:1024],
                             start=(i == 0), stop=(i == NCH - 1), wait=())
                    m_lmm_last = ml2
                    mv_mm = proj_tm(xfm[par][:], wvb, PSW1, C,
                                    (mxf, mwv, mvo) + ((m_vcons,) if m_vcons else ()))
                    mve = ev("vector", "tensor_tensor", wait=(mv_mm, m_bias),
                             out=vtm4[:, i, :, 0:64],
                             in0=PSW1[:, 0:1024].rearrange("p (h d) -> p h d", h=H),
                             in1=biasT[:, C:2 * C].rearrange("p (h d) -> p h d", h=H),
                             op=ALU.add)
                    m_vcons = mve
                    m_vev_last = mve
                    m_kcons[par] = (mkf, ml2, mnk)

                sA1.close()
                kbuf = eA(nc.sbuf_tensor(f"kbuf{sq}", [128, 2 * 8 * 512], BF16))

                def kpre_fetch(tb, wait):
                    h2 = tb % 2
                    return dma(kbuf[:, h2 * 4096:(h2 + 1) * 4096]
                               .rearrange("p (c u) -> p c u", c=8),
                               kfm_d3[:, :, tb * 512:(tb + 1) * 512],
                               wait=wait, ch="m" if h2 == 0 else "n")

                def kread(tb, p, hh):
                    h2 = tb % 2
                    return kbuf[hh * 64:(hh + 1) * 64,
                                h2 * 4096 + p * 512:h2 * 4096 + (p + 1) * 512]

                if maxph < 1:
                    if sq == 0:
                        dtap("eKt", eKt[:], (m_ek_last,))
                        dma(dbg_h["kfm"], kfm_d,
                            wait=(m_spill[0], m_spill[1]), ch="t") \
                            if "kfm" in dbg_h else None
                        dtap("vtm", vtm[:], (m_vev_last,))
                    mlt1x = vcp(ltm[:, 0:512], PS0[0:64, 0:512], wait=(m_lmm_last,))
                    mlt2x = vcp(ltm[:, 512:1024], PS1[0:64, 0:512], wait=(m_lmm_last,))
                    if sq == 0:
                        dtap("ltm", ltm[:], (mlt1x, mlt2x))
                    m_seq_entry = S.bar()
                    continue
                mlt1 = vcp(ltm[:, 0:512], PS0[0:64, 0:512], wait=(m_lmm_last,))
                mlt2 = vcp(ltm[:, 512:1024], PS1[0:64, 0:512], wait=(m_lmm_last,))
                msq3 = ev("scalar", "activation", wait=(mlt1, mlt2) + ln_gate[0],
                          out=scr[0:64, 0:1024], in_=ltm[:], func=AF.Square)
                mnl = ev("vector", "reduce_sum", wait=(msq3,), out=nlb[:],
                         in_=scr[0:64, 0:1024].rearrange("p (h d) -> p h d", h=H),
                         axis=AXX, drain=False)
                ln_gate[0] = ln_gate[0] + (mnl,)
                mnlb = ev("scalar", "activation", wait=(mnl, m_cols), out=nlb[:],
                          in_=nlb[:], func=AF.Copy,
                          scale=col(CNEGINVT)[0:64, :])
                mt8 = []
                for b in range(8):
                    mt8.append(transp(PT[:, b * 64:(b + 1) * 64],
                                      ltm[:, b * 128:(b + 1) * 128],
                                      cs["identb"][0:64, 0:64],
                                      wait=(mlt1, mlt2, m_kfm_last) if b == 0 else ()))
                ml2f = ev("vector", "tensor_scalar", wait=(mt8[-1],), out=l2fm[:],
                          in0=PT[:, 0:512], scalar1=2.0, scalar2=None, op0=ALU.mult)
                mqlb = []
                for h in range(H):
                    p, hh = h // 2, h % 2
                    mqlb.append(mm(PSX[0:64, 40 + h:41 + h],
                                   l2fm[hh * 64:(hh + 1) * 64, p * 64:(p + 1) * 64],
                                   bqfm_s[hh * 64:(hh + 1) * 64, p:p + 1],
                                   tile=(hh * 64, 0) if hh else None,
                                   wait=(ml2f, m_bq) if h == 0 else ()))
                mcq1 = vcp(cqb[:], PSX[0:64, 40:40 + H], wait=(mqlb[-1],))
                mcq = ev("vector", "scalar_tensor_tensor", wait=(mcq1, mnlb), out=cqb[:],
                         in0=cqb[:], scalar=col(CINVT)[0:64, :], in1=nlb[:],
                         op0=ALU.mult, op1=ALU.add)

                m_exp_prev = [None, None]
                m_tr_last = None
                m_zev = [None, None]
                m_w0_last = None
                m_w0ev = [None] * 16
                m_pe_prev = None
                kinit_w = (m_spill[0], m_spill[1], m_kevac[0], m_kevac[1], m_vcons)
                m_kpre = [kpre_fetch(0, kinit_w),
                          kpre_fetch(1, kinit_w) if NTB > 1 else None]
                m_kread = [None, None]
                for tb in range(NTB):
                    moh4 = []
                    for tc in range(4):
                        ch = tb * 4 + tc
                        moh4.append(ev("vector", "tensor_scalar",
                                       wait=(m_idx,) +
                                            ((m_w0_last,) if m_w0_last else ()),
                                       out=ohb[:, tc * 64:(tc + 1) * 64], in0=idxb_s[:],
                                       scalar1=cs["iotac"][:], scalar2=float(ch * 128),
                                       op0=ALU.subtract, op1=ALU.is_equal))
                    if sq == 0 and tb == 0:
                        dtap("ohb", ohb[:], (moh4[-1],))
                    for p in range(8):
                        mexp = []
                        for hh in range(2):
                            h = 2 * p + hh
                            ps = (PS0, PS1)[hh]
                            w_ = (ml2f, m_kpre[tb % 2], mnlb)
                            if m_exp_prev[hh]:
                                w_ += (m_exp_prev[hh],)
                            if m_tr_last:
                                w_ += (m_tr_last,)
                            mmk = mm(ps[0:64, 0:512],
                                     l2fm[hh * 64:(hh + 1) * 64, p * 64:(p + 1) * 64],
                                     kread(tb, p, hh),
                                     tile=(hh * 64, 0) if hh else None, wait=w_)
                            m_kread[tb % 2] = mmk
                            me = ev("scalar", "activation", wait=(mmk, mnlb),
                                    out=phikp[:, hh * 512:(hh + 1) * 512],
                                    in_=ps[0:64, 0:512], func=AF.Exp,
                                    scale=col(CINVT)[0:64, :], bias=nlb[0:64, h:h + 1])
                            mexp.append(me)
                            m_exp_prev[hh] = me
                        mt = []
                        for tc in range(4):
                            for hh in range(2):
                                mt.append(transp(
                                    PT[:, (tc * 2 + hh) * 64:(tc * 2 + hh + 1) * 64],
                                    phikp[:, hh * 512 + tc * 128: hh * 512 + (tc + 1) * 128],
                                    cs["identb"][0:64, 0:64],
                                    wait=(mexp[0], mexp[1]) +
                                         ((m_pe_prev,) if m_pe_prev else ())
                                    if (tc == 0 and hh == 0) else ()))
                        m_tr_last = mt[-1]
                        mpes = []
                        for tc in range(4):
                            ch = tb * 4 + tc
                            for hh in range(2):
                                h = 2 * p + hh
                                mpes.append(ev(
                                    "vector", "tensor_scalar",
                                    wait=(mt[-1], m_ek_last) if not mpes else (),
                                    out=phiktm[:, (tc * 2 + hh) * 64:(tc * 2 + hh + 1) * 64],
                                    in0=PT[:, (tc * 2 + hh) * 64:(tc * 2 + hh + 1) * 64],
                                    scalar1=eKt[:, ch * H + h: ch * H + h + 1],
                                    scalar2=None, op0=ALU.mult))
                        mpe = mpes[-1]
                        m_pe_prev = mpe
                        mz_l = [None, None]
                        for hh in range(2):
                            h = 2 * p + hh
                            zsl = PSX[hh * 64:(hh + 1) * 64, 0:65]
                            for tc in range(4):
                                ch = tb * 4 + tc
                                mz_l[hh] = mm(
                                    zsl,
                                    phiktm[:, (tc * 2 + hh) * 64:(tc * 2 + hh + 1) * 64],
                                    vtm4[:, ch, h, :],
                                    start=(tc == 0), stop=(tc == 3),
                                    tile=(0, 64) if hh else None,
                                    wait=(mpe, m_vev_last, m_zev[0], m_zev[1])
                                    if tc == 0 else ())
                        mw_l = [None, None]
                        for hh in range(2):
                            w0ps = PSW0[hh * 64:(hh + 1) * 64, p * 64:(p + 1) * 64]
                            for tc in range(4):
                                if tc == 0:
                                    w0w = (moh4[-1], m_w0ev[p * 2 + hh],
                                           m_w0ev[p * 2 + 1 - hh])
                                    if tb == 0:
                                        w0w += (mpe, mzw)
                                else:
                                    w0w = ()
                                mw_l[hh] = mm(
                                    w0ps,
                                    ohb[:, tc * 64:(tc + 1) * 64],
                                    phiktm[:, (tc * 2 + hh) * 64:(tc * 2 + hh + 1) * 64],
                                    start=(tc == 0), stop=(tc == 3),
                                    tile=(0, 64) if hh else None,
                                    wait=w0w)
                        m_w0_last = mw_l[1]
                        for hh in range(2):
                            h = 2 * p + hh
                            zsl = PSX[hh * 64:(hh + 1) * 64, 0:65]
                            w0ps = PSW0[hh * 64:(hh + 1) * 64, p * 64:(p + 1) * 64]
                            w0sl = w0acc[hh * 64:(hh + 1) * 64, p * 64:(p + 1) * 64]
                            mza = ev("vector", "tensor_add", wait=(mz_l[1],),
                                     out=zbacc[hh * 64:(hh + 1) * 64, p * 65:(p + 1) * 65],
                                     in0=zbacc[hh * 64:(hh + 1) * 64, p * 65:(p + 1) * 65],
                                     in1=zsl)
                            m_zev[hh] = mza
                            m_w0ev[p * 2 + hh] = ev(
                                "vector", "tensor_add", wait=(mw_l[1],),
                                out=w0sl, in0=w0sl, in1=w0ps)
                    if tb + 2 < NTB:
                        m_kpre[tb % 2] = kpre_fetch(tb + 2, (m_kread[tb % 2],))
                m_zb_last = m_zev
                if sq == 0:
                    dtap("eKt", eKt[:], (m_ek_last,))
                    dtap("ltm", ltm[:], (mlt1, mlt2))
                    dtap("nlb", nlb[:], (mnlb,))
                    dtap("cqb", cqb[:], (mcq,))
                    dtap("l2fm", l2fm[:], (ml2f,))
                    if "kfm" in dbg_h:
                        dma(dbg_h["kfm"], kfm_d, wait=(m_spill[0], m_spill[1]),
                            ch="t")
                    dtap("vtm", vtm[:], (m_vev_last,))
                    dtap("zbacc", zbacc[:], (m_zev[0], m_zev[1]))
                    dtap("phiktm", phiktm[:], (m_pe_prev,))
                m_B_done = S.bar()

            if maxph < 2:
                sP.close()
                m_seq_entry = S.bar()
                continue
            with ExitStack() as sS:
                eS = sS.enter_context
                ablk = eS(nc.sbuf_tensor(f"ablk{sq}", [128, 8 * 128], F32))
                mstk = eS(nc.sbuf_tensor(f"mstk{sq}", [128, 8 * 64], F32))
                mblk = eS(nc.sbuf_tensor(f"mblk{sq}", [128, 8 * 128], F32))
                eall = eS(nc.sbuf_tensor(f"eall{sq}", [128, 8 * 64], F32))
                w0sb = eS(nc.sbuf_tensor(f"w0sb{sq}", [128, 8 * 64], F32))
                ystg = eS(nc.sbuf_tensor(f"ystg{sq}", [128, 8 * 65], BF16))

                mwe = []
                for p in range(8):
                    for hh in range(2):
                        mwe.append(ev("vector", "tensor_scalar",
                                      wait=(m_w0_last,) + tuple(m_w0ev)
                                      if not mwe else (),
                                      out=w0sb[hh * 64:(hh + 1) * 64, p * 64:(p + 1) * 64],
                                      in0=w0acc[hh * 64:(hh + 1) * 64, p * 64:(p + 1) * 64],
                                      scalar1=0.5, scalar2=None, op0=ALU.mult))
                if sq == 0 and "w0psw" in dbg_h:
                    mwcp = vcp(stage[:, 0:512], w0acc[:, 0:512], wait=(mwe[-1],))
                    dtap("w0psw", stage[:, 0:512], (mwcp,))
                mz2 = ev("vector", "memset", wait=m_B_done, ap=ablk[:], constant=0.0)
                mz3 = ev("vector", "memset", wait=(), ap=mblk[:], constant=0.0)
                m_tp = None
                mas = None
                for p in range(8):
                    for hh in range(2):
                        sl = w0sb[hh * 64:(hh + 1) * 64, p * 64:(p + 1) * 64]
                        psl = PSX[hh * 64:(hh + 1) * 64, 64:128]
                        mt_ = mm(psl, sl,
                                 cs["identf"][hh * 64:(hh + 1) * 64,
                                              hh * 64:(hh + 1) * 64],
                                 tile=(hh * 64, hh * 64) if hh else None,
                                 wait=(mwe[-1], mz2) + ((m_tp,) if m_tp else ()))
                        dst = ablk[hh * 64:(hh + 1) * 64,
                                   p * 128 + hh * 64: p * 128 + (hh + 1) * 64]
                        ma1 = ev("vector", "tensor_add", wait=(mt_,), out=dst, in0=sl,
                                 in1=psl)
                        mas = ev("vector", "scalar_tensor_tensor", wait=(ma1, m_cols),
                                 out=dst, in0=cs["i2stk"][hh * 64:(hh + 1) * 64, :],
                                 scalar=col(CREGMU)[hh * 64:(hh + 1) * 64, :], in1=dst,
                                 op0=ALU.mult, op1=ALU.add)
                        m_tp = ma1
                mmi = ev("vector", "tensor_scalar", wait=(mas,), out=mstk[:],
                         in0=cs["i2rep"][:], scalar1=float(1.0 / 65.0), scalar2=None,
                         op0=ALU.mult)
                m_it = mmi
                m_e_prev = None
                for it in range(NEWTON_ITERS):
                    mb1 = vcp(mblk[0:64, :].rearrange("p (n c) -> p n c", n=8)[:, :, 0:64],
                              mstk[0:64, :].rearrange("p (n c) -> p n c", n=8),
                              wait=(m_it, mz3))
                    mb2 = vcp(mblk[64:128, :].rearrange("p (n c) -> p n c", n=8)[:, :, 64:128],
                              mstk[64:128, :].rearrange("p (n c) -> p n c", n=8),
                              wait=(m_it,))
                    mq_ = None
                    for p in range(8):
                        mq_ = mm(PSW1[:, p * 64:(p + 1) * 64],
                                 ablk[:, p * 128:(p + 1) * 128],
                                 mstk[:, p * 64:(p + 1) * 64],
                                 wait=(mas, m_it) + ((m_e_prev,) if m_e_prev else ())
                                 if p == 0 else ())
                    me_ = ev("vector", "tensor_tensor", wait=(mq_,), out=eall[:],
                             in0=cs["i2rep"][:], in1=PSW1[:, 0:512], op=ALU.subtract)
                    m_e_prev = me_
                    md_ = None
                    for p in range(8):
                        md_ = mm(PSW0[:, p * 64:(p + 1) * 64],
                                 mblk[:, p * 128:(p + 1) * 128],
                                 eall[:, p * 64:(p + 1) * 64],
                                 wait=(me_, mb1, mb2, mwe[-1]) if p == 0 else ())
                    m_it = ev("vector", "tensor_add", wait=(md_,), out=mstk[:],
                              in0=mstk[:], in1=PSW0[:, 0:512])
                mb1 = vcp(mblk[0:64, :].rearrange("p (n c) -> p n c", n=8)[:, :, 0:64],
                          mstk[0:64, :].rearrange("p (n c) -> p n c", n=8), wait=(m_it,))
                mb2 = vcp(mblk[64:128, :].rearrange("p (n c) -> p n c", n=8)[:, :, 64:128],
                          mstk[64:128, :].rearrange("p (n c) -> p n c", n=8), wait=(m_it,))
                if sq == 0 and "mblk0" in dbg_h:
                    dtap("mblk0", mblk[:, 0:128], (mb1, mb2))
                m_ye_prev = None
                mystg = []
                for p in range(8):
                    my1 = mm(PSX[:, 0:65], mblk[:, p * 128:(p + 1) * 128],
                             zbacc[:, p * 65:(p + 1) * 65],
                             wait=(mb1, mb2, m_zb_last[0], m_zb_last[1]) +
                                  ((m_ye_prev,) if m_ye_prev else ()))
                    my2 = vcp(ysol[0:64, (2 * p) * 65:(2 * p + 1) * 65], PSX[0:64, 0:65],
                              wait=(my1,))
                    my3 = vcp(ystg[64:128, p * 65:(p + 1) * 65],
                              PSX[64:128, 0:65], wait=(my1,))
                    m_ye_prev = my3
                    if sq == 0 and p == 0 and "ypsx" in dbg_h:
                        mt_c = vcp(stage2[:, 0:65], PSX[:, 0:65], wait=(my1,))
                        mt_y = dtap("ypsx", stage2[:, 0:65], (mt_c,))
                        m_ye_prev = mt_c
                    mystg.append(my3)
                m_ys = dma(ysol[0:64, :].rearrange("p (a b s) -> p a b s", a=8, b=2)
                           [:, :, 1, :],
                           ystg[64:128, :].rearrange("p (a s) -> p a s", a=8),
                           wait=tuple(mystg), ch="y")
                if sq == 0:
                    dtap("w0sb", w0sb[:], (mwe[-1],))
                    dtap("ablk", ablk[:], (mas,))
                    dtap("mstk", mstk[:], (m_it,))
                    dtap("ysol", ysol[:], (m_ys,))
                m_S_done = S.bar()

            if maxph < 3:
                sP.close()
                m_seq_entry = S.bar()
                continue
            with ExitStack() as sC:
                eC = sC.enter_context
                stgB = eC(nc.sbuf_tensor(f"stgBc{sq}", [128, 8 * 512], BF16))
                wqab = eC(nc.sbuf_tensor(f"wqab{sq}", [128, 8 * C], BF16))
                woab = eC(nc.sbuf_tensor(f"woab{sq}", [128, 8 * C], BF16))
                qfm = eC(nc.sbuf_tensor(f"qfm{sq}", [128, 8 * 512], BF16))
                atfm = eC(nc.sbuf_tensor(f"atfm{sq}", [128, 8 * 512], BF16))
                phiq = eC(nc.sbuf_tensor(f"phiq{sq}", [64, 2 * 512], BF16))
                xlnb2 = [eC(nc.sbuf_tensor(f"xlnb2{sq}_{i}", [128, C], BF16))
                         for i in range(2)]
                xfm2 = [eC(nc.sbuf_tensor(f"xfm2{sq}_{i}", [128, C], BF16))
                        for i in range(2)]
                qtmp = eC(nc.sbuf_tensor(f"qtmp{sq}", [128, C], BF16))
                aqtm = eC(nc.sbuf_tensor(f"aqtm{sq}", [128, C], BF16))
                aqfm = eC(nc.sbuf_tensor(f"aqfm{sq}", [128, C], BF16))
                scf = eC(nc.sbuf_tensor(f"scf{sq}", [128, 512], F32))
                borow = eC(nc.sbuf_tensor(f"borow{sq}", [1, C], BF16))

                qfm3 = qfm[:].rearrange("p (c t) -> p c t", c=8)
                m_bod = dma(stage[0:1, 0:1024], bias_a_h[3:4, :], wait=m_S_done, ch="d")
                m_bo = vcp(borow[:], stage[0:1, 0:1024], wait=(m_bod,))
                mwq = load_w_bf(wqab, w_h["wq_a"], C, tuple(m_S_done) + (m_bo,))
                mwo = load_w_bf(woab, w_h["wo_a"], C, (mwq,))

                m_ffn_prev = None
                m_stgB_dma = None
                for tb in range(NTB):
                    mq_done = []
                    m_qt_prev = None
                    for tc in range(4):
                        ch = tb * 4 + tc
                        par = tc % 2
                        w_ = (mwo,) if (tb == 0 and tc == 0) else ()
                        if m_ffn_prev:
                            w_ += (m_ffn_prev,)
                        mln = ln_chunk(R[ch], xlnb2[par][:], None, w_)
                        mxf = transpose8(xlnb2[par][:],
                                         xfm2[par][:].rearrange("p (a b) -> p a b", a=8),
                                         tuple(mln))
                        mqmm = proj_tm(xfm2[par][:], wqab, PSW0, C, (mxf, mwq))
                        mqe = vcp(qtmp[:], PSW0[:, 0:1024], wait=(mqmm,))
                        mqt = transpose8(qtmp[:], qfm3[:, :, tc * 128:(tc + 1) * 128],
                                         (mqe,) + ((m_qt_prev,) if m_qt_prev else ()))
                        m_qt_prev = mqt
                        mq_done.append(mqt)
                    m_cq_prev = None
                    m_ynext = None
                    m_f2_prev = None
                    m_ob_prev = None
                    for h in range(H):
                        p, hh = h // 2, h % 2
                        ps = (PS0, PS1)[h % 2]
                        mmq = mm(ps[0:64, 0:512],
                                 l2fm[hh * 64:(hh + 1) * 64, p * 64:(p + 1) * 64],
                                 qfm[hh * 64:(hh + 1) * 64, p * 512:(p + 1) * 512],
                                 tile=(hh * 64, 0) if hh else None,
                                 wait=(mq_done[-1],) + ((m_cq_prev,) if m_cq_prev else ()))
                        mpq = ev("scalar", "activation", wait=(mmq, mcq),
                                 out=phiq[:, (h % 2) * 512:(h % 2 + 1) * 512],
                                 in_=ps[0:64, 0:512], func=AF.Exp,
                                 scale=col(CINVT)[0:64, :], bias=cqb[0:64, h:h + 1])
                        m_cq_prev = mpq
                        myp = mm(PSW1[0:65, 0:512], ysol[0:64, h * 65:(h + 1) * 65],
                                 phiq[:, (h % 2) * 512:(h % 2 + 1) * 512],
                                 wait=(mpq, m_ys) + ((m_ynext,) if m_ynext else ()))
                        mn1 = ev("vector", "tensor_scalar", wait=(myp,),
                                 out=scf[0:1, 0:512], in0=PSW1[64:65, 0:512],
                                 scalar1=1e-10, scalar2=float(2.0 ** -32),
                                 op0=ALU.max, op1=ALU.mult)
                        mn2a = ev("scalar", "activation", wait=(mn1,),
                                  out=scf[0:1, 0:512], in_=scf[0:1, 0:512],
                                  func=AF.Ln)
                        mn2 = ev("scalar", "activation", wait=(mn2a,),
                                 out=scf[0:1, 0:512], in_=scf[0:1, 0:512],
                                 func=AF.Exp, scale=-1.0)
                        mf2 = mm(PSX[0:64, 0:512], cs["sc32row"][0:1, 0:64],
                                 scf[0:1, 0:512],
                                 wait=(mn2,) + ((m_f2_prev,) if m_f2_prev else ()))
                        mf2e = vcp(scr[0:64, 0:512], PSX[0:64, 0:512],
                                   wait=(mf2,) + ln_gate[0] +
                                        ((m_ob_prev,) if m_ob_prev else ()))
                        m_f2_prev = mf2e
                        if hh == 0:
                            mae = ev("vector", "tensor_tensor", wait=(mf2e, myp),
                                     out=atfm[0:64, p * 512:(p + 1) * 512],
                                     in0=PSW1[0:64, 0:512], in1=scr[0:64, 0:512],
                                     op=ALU.mult)
                        else:
                            mae = ev("vector", "tensor_tensor",
                                     wait=(mf2e, myp) + ((m_stgB_dma,) if m_stgB_dma else ()),
                                     out=stgB[0:64, p * 512:(p + 1) * 512],
                                     in0=PSW1[0:64, 0:512], in1=scr[0:64, 0:512],
                                     op=ALU.mult)
                        m_ynext = mae
                        m_ob_prev = mae
                    m_stgB_dma = dma(atfm[64:128, :], stgB[0:64, :], wait=(m_ynext,),
                                     ch="s")
                    m_at_ev = m_stgB_dma
                    if sq == 0 and tb == 0:
                        dtap("qfm", qfm[:], (mq_done[-1],))
                        dtap("atfm", atfm[:], (m_at_ev, m_ynext))
                    for tc in range(4):
                        ch = tb * 4 + tc
                        mwos = None
                        for ci in range(8):
                            for nh in range(2):
                                mwos = mm(
                                    PSW0[:, nh * 512:(nh + 1) * 512],
                                    atfm[:, ci * 512 + tc * 128: ci * 512 + (tc + 1) * 128],
                                    woab[:, ci * C + nh * 512: ci * C + (nh + 1) * 512],
                                    start=(ci == 0), stop=False,
                                    wait=(m_at_ev,) if (ci == 0 and nh == 0) else ())
                        for nh in range(2):
                            mwos = mm(PSW0[:, nh * 512:(nh + 1) * 512],
                                      cs["onesrowb"][:],
                                      borow[0:1, nh * 512:(nh + 1) * 512],
                                      start=False, stop=True, wait=(m_bo,))
                        mra = ev("vector", "tensor_add", wait=(mwos,),
                                 out=R[ch], in0=R[ch], in1=PSW0[:, 0:1024])
                        mln2 = ln_chunk(R[ch], None, scr[:, 0:1024], (mra,))
                        mam = ev("vector", "reduce_max", wait=(mln2[-1],), out=col(CAMAX),
                                 in_=scr[:, 0:1024], axis=AXX,
                                 apply_absolute_value=True, drain=False)
                        mas_ = ev("scalar", "activation", wait=(mam,), out=col(CAMAX),
                                  in_=col(CAMAX), func=AF.Copy,
                                  bias=float(FFN_EPS))
                        mrs = ev("vector", "reciprocal", wait=(mas_,), out=col(CRED),
                                 in_=col(CAMAX), drain=False)
                        ms127 = ev("scalar", "activation", wait=(mrs,), out=col(CRED),
                                   in_=col(CRED), func=AF.Copy, scale=127.0)
                        mosc = ev("scalar", "activation", wait=(mas_,), out=col(COSC),
                                  in_=col(CAMAX), func=AF.Copy, scale=col(CWSD))
                        mz1_ = ev("vector", "tensor_scalar", wait=(ms127,),
                                  out=scr[:, 0:1024], in0=scr[:, 0:1024],
                                  scalar1=col(CRED), scalar2=-128.0,
                                  op0=ALU.mult, op1=ALU.max)
                        mz2_ = ev("vector", "tensor_scalar", wait=(mz1_,),
                                  out=scr[:, 0:1024], in0=scr[:, 0:1024],
                                  scalar1=127.0, scalar2=None, op0=ALU.min)
                        mrd = ev("vector", "tensor_scalar", wait=(mz2_,), out=aqtm[:],
                                 in0=scr[:, 0:1024], scalar1=col(CMAGIC),
                                 scalar2=col(CMAGIC), op0=ALU.add, op1=ALU.subtract)
                        ln_gate[0] = ln_gate[0] + (mrd,)
                        mqt2 = transpose8(aqtm[:], aqfm[:].rearrange("p (a b) -> p a b", a=8),
                                          (mrd,))
                        mfmm = proj_tm(aqfm[:], wqffn, PSW1, C, (mqt2,))
                        m_ffn_prev = ev("vector", "scalar_tensor_tensor", wait=(mfmm, mosc),
                                        out=R[ch], in0=PSW1[:, 0:1024],
                                        scalar=col(COSC), in1=R[ch],
                                        op0=ALU.mult, op1=ALU.add)
                if sq == 0:
                    for ch in range(NCH):
                        dtap("R_CI" if ch == 0 else None, None, None) if False else None
                    if "R_CI" in dbg_h:
                        for ch in range(NCH):
                            dma(dbg_h["R_CI"][ch * 128:(ch + 1) * 128, :], R[ch],
                                wait=(m_ffn_prev,), ch="t")
                m_CI_done = S.bar()

            if maxph < 4:
                sP.close()
                m_seq_entry = S.bar()
                continue
            sP.close()
            with ExitStack() as sD:
                eD = sD.enter_context
                stgB = eD(nc.sbuf_tensor(f"stgBd{sq}", [128, 8 * 512], BF16))
                EtEp_s = eD(nc.sbuf_tensor(f"EtEp_s{sq}", [128, 2 * 128 * H], F32))
                bcol_s = eD(nc.sbuf_tensor(f"bcol_s{sq}", [128, H * NKO], F32))
                F1_s = eD(nc.sbuf_tensor(f"F1_s{sq}", [128, 512], F32))
                ct_s = eD(nc.sbuf_tensor(f"ct_s{sq}", [128, 48], F32))
                with ExitStack() as sDlo:
                    eL = sDlo.enter_context
                    EtEp_lo = eL(nc.sbuf_tensor(f"EtEp_lo{sq}", [128, 4096], F32))
                    bcol_lo = eL(nc.sbuf_tensor(f"bcol_lo{sq}", [128, H * NKO], F32))
                    F1_lo = eL(nc.sbuf_tensor(f"F1_lo{sq}", [128, 512], F32))
                    ct_lo = eL(nc.sbuf_tensor(f"ct_lo{sq}", [128, 48], F32))
                    dma(bcol_s[:], ch_["bcol_hi"], wait=m_CI_done, ch="d")
                    dma(bcol_lo[:], ch_["bcol_lo"], ch="d")
                    dma(F1_s[:], ch_["F1_hi"], ch="d")
                    dma(F1_lo[:], ch_["F1_lo"], ch="d")
                    dma(ct_s[:], ch_["ct_hi"], ch="d")
                    dma(ct_lo[:], ch_["ct_lo"], ch="d")
                    dma(EtEp_s[:, 0:2048], ch_["Et_hi"], ch="d")
                    dma(EtEp_lo[:, 0:2048], ch_["Et_lo"], ch="d")
                    dma(EtEp_s[:, 2048:4096], ch_["Ep_hi"], ch="d")
                    m_tbl = dma(EtEp_lo[:, 2048:4096], ch_["Ep_lo"], ch="d")
                    m_d1 = ev("vector", "tensor_add", wait=(m_tbl,), out=bcol_s[:],
                              in0=bcol_s[:], in1=bcol_lo[:])
                    m_d2 = ev("vector", "tensor_add", wait=(m_tbl,), out=F1_s[:],
                              in0=F1_s[:], in1=F1_lo[:])
                    m_d3 = ev("vector", "tensor_add", wait=(m_tbl,), out=ct_s[:],
                              in0=ct_s[:], in1=ct_lo[:])
                    m_d4 = ev("vector", "tensor_add", wait=(m_tbl,),
                              out=EtEp_s[:, 0:2048],
                              in0=EtEp_s[:, 0:2048], in1=EtEp_lo[:, 0:2048])
                    m_d5 = ev("vector", "tensor_add", wait=(m_tbl,),
                              out=EtEp_s[:, 2048:4096],
                              in0=EtEp_s[:, 2048:4096], in1=EtEp_lo[:, 2048:4096])
                    m_dconst = (m_d1, m_d2, m_d3, m_d4, m_d5)
                wslot = eD(nc.sbuf_tensor(f"wslot{sq}", [128, 8 * C], BF16))
                qfmb = eD(nc.sbuf_tensor(f"qfmb{sq}", [128, 8 * T], BF16))
                kfmb = eD(nc.sbuf_tensor(f"kfmb{sq}", [128, 2 * T], BF16))
                vtmb = eD(nc.sbuf_tensor(f"vtmb{sq}", [128, NCH * 4 * 65], BF16))
                sD1 = ExitStack()
                eP = sD1.enter_context
                wkvb2 = eP(nc.sbuf_tensor(f"wkvb2{sq}", [128, 8 * 512], BF16))
                xlnb3 = [eP(nc.sbuf_tensor(f"xlnb3{sq}_{i}", [128, C], BF16))
                         for i in range(2)]
                xfm3 = [eP(nc.sbuf_tensor(f"xfm3{sq}_{i}", [128, C], BF16))
                        for i in range(2)]
                qtmp2 = eP(nc.sbuf_tensor(f"qtmp2{sq}", [128, C], BF16))
                ktmp2 = eP(nc.sbuf_tensor(f"ktmp2{sq}", [128, 256], BF16))

                qfmb3 = qfmb[:].rearrange("p (c t) -> p c t", c=8)
                kfmb3 = kfmb[:].rearrange("p (c t) -> p c t", c=2)
                vtmb4 = vtmb[:].rearrange("p (n g e) -> p n g e", n=NCH, g=4)

                mwqb = load_w_bf(wslot, w_h["wq_b"], C,
                                 tuple(m_CI_done) + m_dconst)
                mwkv = load_w_bf(wkvb2, w_h["wkv_b"], 512, (mwqb,))
                mvz = None
                for i_ in range(NCH):
                    mvz = ev("vector", "memset", wait=m_CI_done if i_ == 0 else (),
                             ap=vtmb[:, i_ * 4 * 65:(i_ + 1) * 4 * 65]
                             .rearrange("p (g e) -> p g e", g=4)[:, :, 64:65],
                             constant=1.0)

                m_prev3 = None
                m_kv_ev = None
                m_qb_ev = None
                for ch in range(NCH):
                    par = ch % 2
                    w_ = (mwkv, mvz)
                    if m_prev3:
                        w_ += (m_prev3,)
                    mln3 = ln_chunk(R[ch], xlnb3[par][:], None, w_)
                    mxf3 = transpose8(xlnb3[par][:],
                                      xfm3[par][:].rearrange("p (a b) -> p a b", a=8),
                                      tuple(mln3))
                    mqb = proj_tm(xfm3[par][:], wslot, PSW0, C,
                                  (mxf3, mwqb) + ((m_qb_ev,) if m_qb_ev else ()))
                    mqe2 = vcp(qtmp2[:], PSW0[:, 0:1024], wait=(mqb,))
                    m_qb_ev = mqe2
                    mqt3 = transpose8(qtmp2[:], qfmb3[:, :, ch * 128:(ch + 1) * 128],
                                      (mqe2,))
                    mkv = None
                    for ci in range(8):
                        mkv = mm(PSW1[:, 0:512], xfm3[par][:, ci * 128:(ci + 1) * 128],
                                 wkvb2[:, ci * 512:(ci + 1) * 512],
                                 start=(ci == 0), stop=(ci == 7),
                                 wait=(mxf3, mwkv) + ((m_kv_ev,) if m_kv_ev else ())
                                 if ci == 0 else ())
                    mke2 = vcp(ktmp2[:], PSW1[:, 0:256], wait=(mkv,))
                    mtk = []
                    for b in range(2):
                        mtk.append(transp(PT[:, b * 128:(b + 1) * 128],
                                          ktmp2[:, b * 128:(b + 1) * 128],
                                          cs["identb"][:],
                                          wait=(mke2, mqt3) if b == 0 else ()))
                    mkf2 = vcp(kfmb3[:, :, ch * 128:(ch + 1) * 128], PT[:, 0:256],
                               wait=(mtk[-1],))
                    mve2 = vcp(vtmb4[:, ch, :, 0:64],
                               PSW1[:, 256:512].rearrange("p (g d) -> p g d", g=4),
                               wait=(mkv, mvz))
                    m_kv_ev = mve2
                    m_prev3 = mkf2
                if sq == 0:
                    dtap("kfmb", kfmb[:], (m_prev3,))
                    dtap("qfmb", qfmb[:], (mqt3,))
                    dtap("vtmb", vtmb[:], (m_kv_ev,))
                    if "R_CII" in dbg_h:
                        for ch in range(NCH):
                            dma(dbg_h["R_CII"][ch * 128:(ch + 1) * 128, :], R[ch],
                                wait=(mln3[-1],), ch="t")
                m_CII_done = S.bar()

                sD1.close()
                ptile = eD(nc.sbuf_tensor(f"ptile{sq}", [128, 2 * 512], BF16))
                scf2 = eD(nc.sbuf_tensor(f"scf2{sq}", [128, 512], F32))
                lrow = eD(nc.sbuf_tensor(f"lrow{sq}", [1, 512], F32))
                f1row = eD(nc.sbuf_tensor(f"f1row{sq}", [1, 512], F32))
                obfm = eD(nc.sbuf_tensor(f"obfm{sq}", [128, 8 * 512], BF16))
                aqtm2 = eD(nc.sbuf_tensor(f"aqtm2{sq}", [128, C], BF16))
                aqfm2 = eD(nc.sbuf_tensor(f"aqfm2{sq}", [128, C], BF16))

                mwp = load_w_bf(wslot, w_h["wpr_b"], C, m_CII_done)

                m_ffn2_prev = None
                m_stgB2 = None
                m_f1_prev = None
                for tq in range(NTQ):
                    m_ob_ev = None
                    m_comb_prev = None
                    m_exp_par = [None, None]
                    m_pv_par = [None, None]
                    for h in range(H):
                        p, hh = h // 2, h % 2
                        kv = h % 4
                        kvt = kv // 2
                        upcls = []
                        m_pv_last = None
                        for ss in range(NSS):
                            ko = ss - 4 * tq
                            par2 = ss % 2
                            ps = (PS0, PS1)[par2]
                            sl = ptile[:, par2 * 512:(par2 + 1) * 512]
                            w_ = (mwp,)
                            if m_exp_par[par2]:
                                w_ += (m_exp_par[par2],)
                            mmks = mm(ps[:, 0:512],
                                      kfmb[hh * 64:(hh + 1) * 64,
                                           kvt * T + ss * 128: kvt * T + (ss + 1) * 128],
                                      qfmb[hh * 64:(hh + 1) * 64,
                                           p * T + tq * 512: p * T + (tq + 1) * 512],
                                      tile=(hh * 64, 0) if hh else None, wait=w_)
                            we_ = (mmks,)
                            if m_pv_par[par2]:
                                we_ += (m_pv_par[par2],)
                            if ko >= 4:
                                mfix = ev("scalar", "activation", wait=we_, out=sl,
                                          in_=ps[:, 0:512], func=AF.Exp, scale=DS)
                                is_up = True
                            else:
                                bc_ap = bcol_s[:, h * NKO + (ko - KOFF0):
                                               h * NKO + (ko - KOFF0) + 1]
                                mfix = ev("scalar", "activation", wait=we_, out=sl,
                                          in_=ps[:, 0:512], func=AF.Exp, scale=DS,
                                          bias=bc_ap)
                                if ko >= 0:
                                    for m2i in range(min(ko, 4)):
                                        cidx = h * 3 + (m2i - ko + 3)
                                        mfix = ev("vector", "scalar_tensor_tensor",
                                                  wait=(mfix,),
                                                  out=sl[:, m2i * 128:(m2i + 1) * 128],
                                                  in0=sl[:, m2i * 128:(m2i + 1) * 128],
                                                  scalar=ct_s[:, cidx:cidx + 1],
                                                  in1=Et_h(h), op0=ALU.mult, op1=ALU.mult)
                                    if ko <= 3:
                                        mfix = ev("vector", "tensor_tensor", wait=(mfix,),
                                                  out=sl[:, ko * 128:(ko + 1) * 128],
                                                  in0=sl[:, ko * 128:(ko + 1) * 128],
                                                  in1=Ep_h(h), op=ALU.mult)
                                is_up = False
                            m_exp_par[par2] = mfix
                            first = not any(u == is_up for u in upcls)
                            rest = [(s2 - 4 * tq) >= 4 for s2 in range(ss + 1, NSS)]
                            last = all(u != is_up for u in rest)
                            upcls.append(is_up)
                            dst = PSW0[0:65, 512:1024] if is_up else PSW0[0:65, 0:512]
                            wpv = (mfix,)
                            if first:
                                if m_comb_prev:
                                    wpv += (m_comb_prev,)
                                if m_ffn2_prev:
                                    wpv += (m_ffn2_prev,)
                            m_pv_last = mm(dst, vtmb4[:, ss, kv, :], sl,
                                           start=first, stop=last, wait=wpv)
                            m_pv_par[par2] = m_pv_last
                        any_up = any(upcls)
                        if any_up:
                            mf1r = mm(PSX[0:1, 0:512], cs["identf"][:, h:h + 1],
                                      F1_s[:],
                                      wait=(m_pv_last,) +
                                           ((m_comb_prev,) if m_comb_prev else ()))
                            mf1rv = vcp(f1row[0:1, 0:512], PSX[0:1, 0:512],
                                        wait=(mf1r,) +
                                             ((m_f1_prev,) if m_f1_prev else ()))
                            mf1 = mm(PSX[0:65, 0:512], cs["onesrow"][0:1, 0:65],
                                     f1row[0:1, :], wait=(mf1rv,))
                            m_f1_prev = mf1
                            mf1e = vcp(scf2[0:65, 0:512], PSX[0:65, 0:512], wait=(mf1,))
                            mcb1 = ev("vector", "tensor_tensor", wait=(mf1e,),
                                      out=scf2[0:65, 0:512], in0=PSW0[0:65, 512:1024],
                                      in1=scf2[0:65, 0:512], op=ALU.mult)
                            mcb2 = ev("vector", "tensor_add", wait=(mcb1,),
                                      out=scf2[0:65, 0:512], in0=scf2[0:65, 0:512],
                                      in1=PSW0[0:65, 0:512])
                        else:
                            mcb2 = vcp(scf2[0:65, 0:512], PSW0[0:65, 0:512],
                                       wait=(m_pv_last,) + ((m_comb_prev,) if m_comb_prev else ()))
                        mr1 = ev("vector", "tensor_scalar", wait=(mcb2,),
                                 out=lrow[0:1, 0:512], in0=scf2[64:65, 0:512],
                                 scalar1=1e-30, scalar2=float(2.0 ** -32),
                                 op0=ALU.max, op1=ALU.mult)
                        mr1a = ev("scalar", "activation", wait=(mr1,),
                                  out=lrow[0:1, 0:512], in_=lrow[0:1, 0:512],
                                  func=AF.Ln)
                        mr1b = ev("scalar", "activation", wait=(mr1a,),
                                  out=lrow[0:1, 0:512], in_=lrow[0:1, 0:512],
                                  func=AF.Exp, scale=-1.0)
                        mr2 = mm(PSX[0:64, 0:512], cs["sc32row"][0:1, 0:64],
                                 lrow[0:1, 0:512], wait=(mr1b,))
                        mr3 = vcp(scr[0:64, 0:512], PSX[0:64, 0:512],
                                  wait=(mr2,) + ln_gate[0])
                        if hh == 0:
                            mob = ev("vector", "tensor_tensor", wait=(mr3,),
                                     out=obfm[0:64, p * 512:(p + 1) * 512],
                                     in0=scf2[0:64, 0:512], in1=scr[0:64, 0:512],
                                     op=ALU.mult)
                        else:
                            mob = ev("vector", "tensor_tensor",
                                     wait=(mr3,) + ((m_stgB2,) if m_stgB2 else ()),
                                     out=stgB[0:64, p * 512:(p + 1) * 512],
                                     in0=scf2[0:64, 0:512], in1=scr[0:64, 0:512],
                                     op=ALU.mult)
                        m_comb_prev = mob
                        m_ob_ev = mob
                        ln_gate[0] = ln_gate[0] + (mob,)
                    m_stgB2 = dma(obfm[64:128, :], stgB[0:64, :], wait=(m_ob_ev,),
                                  ch="s")
                    m_ob_ev = m_stgB2
                    if sq == 0 and tq == 0:
                        dtap("obfm", obfm[:], (m_ob_ev, m_comb_prev))
                    for tc in range(4):
                        ch = tq * 4 + tc
                        mws = None
                        for ci in range(8):
                            for nh in range(2):
                                mws = mm(
                                    PSW1[:, nh * 512:(nh + 1) * 512],
                                    obfm[:, ci * 512 + tc * 128: ci * 512 + (tc + 1) * 128],
                                    wslot[:, ci * C + nh * 512: ci * C + (nh + 1) * 512],
                                    start=(ci == 0), stop=False,
                                    wait=(m_ob_ev, mwp) if (ci == 0 and nh == 0) else ())
                        for nh in range(2):
                            mws = mm(PSW1[:, nh * 512:(nh + 1) * 512],
                                     cs["onesrowb"][:], bprow[0:1, nh * 512:(nh + 1) * 512],
                                     start=False, stop=True, wait=(m_bp,))
                        mra2 = ev("vector", "tensor_add", wait=(mws,),
                                  out=R[ch], in0=R[ch], in1=PSW1[:, 0:1024])
                        mln4 = ln_chunk(R[ch], None, scr[:, 0:1024], (mra2,))
                        mam = ev("vector", "reduce_max", wait=(mln4[-1],), out=col(CAMAX),
                                 in_=scr[:, 0:1024], axis=AXX,
                                 apply_absolute_value=True, drain=False)
                        mas_ = ev("scalar", "activation", wait=(mam,), out=col(CAMAX),
                                  in_=col(CAMAX), func=AF.Copy,
                                  bias=float(FFN_EPS))
                        mrs = ev("vector", "reciprocal", wait=(mas_,), out=col(CRED),
                                 in_=col(CAMAX), drain=False)
                        ms127 = ev("scalar", "activation", wait=(mrs,), out=col(CRED),
                                   in_=col(CRED), func=AF.Copy, scale=127.0)
                        mosc = ev("scalar", "activation", wait=(mas_,), out=col(COSC),
                                  in_=col(CAMAX), func=AF.Copy, scale=col(CWSD))
                        mz1_ = ev("vector", "tensor_scalar", wait=(ms127,),
                                  out=scr[:, 0:1024], in0=scr[:, 0:1024],
                                  scalar1=col(CRED), scalar2=-128.0,
                                  op0=ALU.mult, op1=ALU.max)
                        mz2_ = ev("vector", "tensor_scalar", wait=(mz1_,),
                                  out=scr[:, 0:1024], in0=scr[:, 0:1024],
                                  scalar1=127.0, scalar2=None, op0=ALU.min)
                        mrd = ev("vector", "tensor_scalar", wait=(mz2_,), out=aqtm2[:],
                                 in0=scr[:, 0:1024], scalar1=col(CMAGIC),
                                 scalar2=col(CMAGIC), op0=ALU.add, op1=ALU.subtract)
                        ln_gate[0] = ln_gate[0] + (mrd,)
                        mqt4 = transpose8(aqtm2[:],
                                          aqfm2[:].rearrange("p (a b) -> p a b", a=8),
                                          (mrd,))
                        mfmm = proj_tm(aqfm2[:], wqffn, PSW0, C,
                                       (mqt4,) + ((m_ffn2_prev,) if m_ffn2_prev else ()))
                        mfin = ev("vector", "scalar_tensor_tensor", wait=(mfmm, mosc),
                                  out=R[ch], in0=PSW0[:, 0:1024], scalar=col(COSC),
                                  in1=R[ch], op0=ALU.mult, op1=ALU.add)
                        m_ffn2_prev = mfin
                        dma(out_h[sq * T + ch * 128: sq * T + (ch + 1) * 128, :],
                            R[ch], wait=(mfin,), ch="o")
                m_seq_entry = S.bar()


        with nc.Block() as block:
            def body_for(eng_name):
                def body(e):
                    for fn, waits, sem_inc in S.items[eng_name]:
                        for s_name, v in waits:
                            e.wait_ge(sems[s_name], v)
                        ins = fn(e)
                        ins.then_inc(sems[sem_inc[0]], sem_inc[1])
                    if eng_name == "sync":
                        for chn in DMACH:
                            if S.count[chn] > 0:
                                e.wait_ge(sems[chn], S.count[chn])
                return body
            block.sync(body_for("sync"))
            block.tensor(body_for("tensor"))
            block.vector(body_for("vector"))
            block.scalar(body_for("scalar"))
            block.gpsimd(body_for("gpsimd"))

    return nc


def kernel(**inputs):
    T = 2048
    NSEQ = 2
    ncores = 8
    x = np.ascontiguousarray(np.asarray(inputs["L0_attn_a_query"], np.float32)
                             .reshape(16, T, C))
    cst = host_consts(T)

    def wT(a):
        return np.ascontiguousarray(np.asarray(a, np.float32).T)

    bias_a = np.stack([np.asarray(inputs["L0_attn_a_bq"], np.float32),
                       np.asarray(inputs["L0_attn_a_bk"], np.float32),
                       np.asarray(inputs["L0_attn_a_bv"], np.float32),
                       np.asarray(inputs["L0_attn_a_bo"], np.float32)])
    shared = {
        "wq_a": wT(inputs["L0_attn_a_Wq"]), "wk_a": wT(inputs["L0_attn_a_Wk"]),
        "wv_a": wT(inputs["L0_attn_a_Wv"]), "wo_a": wT(inputs["L0_attn_a_Wo"]),
        "bias_a": np.ascontiguousarray(bias_a),
        "bqfm": np.ascontiguousarray(
            np.asarray(inputs["L0_attn_a_bq"], np.float32).reshape(8, 128).T),
        "wffn": wT(inputs["L0_ffn_weight"]),
        "wq_b": wT(inputs["L0_attn_b_Wq"]), "wkv_b": wT(inputs["L0_attn_b_Wkv"]),
        "wpr_b": wT(inputs["L0_attn_b_Wproj"]),
        "bpr_b": np.ascontiguousarray(
            np.asarray(inputs["L0_attn_b_bproj"], np.float32).reshape(1, C)),
        "temp": np.asarray(inputs["L0_attn_a_temperature"], np.float32).reshape(1, 1),
        "lidx": np.ascontiguousarray(
            np.asarray(inputs["L0_attn_a_landmark_idx"], np.int32).reshape(1, NL)),
    }
    for n, a in cst.items():
        if isinstance(a, np.ndarray):
            shared[n] = np.ascontiguousarray(a)
    nc = build_program(T=T, NSEQ=NSEQ)
    in_maps = []
    for core in range(ncores):
        m = dict(shared)
        m["x"] = np.ascontiguousarray(
            x[core * NSEQ:(core + 1) * NSEQ].reshape(NSEQ * T, C))
        in_maps.append(m)
    res = run_bass_kernel_spmd(nc, in_maps, core_ids=list(range(ncores)))
    out = np.concatenate(
        [np.asarray(r["out"]).reshape(NSEQ, T, C) for r in res.results], axis=0)
    return np.ascontiguousarray(out.astype(np.float32))


# version 13
# speedup vs baseline: 1.2844x; 1.2267x over previous
import numpy as np
import ml_dtypes
from contextlib import ExitStack

import concourse.bass as bass
import concourse.mybir as mybir
from concourse.bass_utils import run_bass_kernel_spmd

F32 = mybir.dt.float32
BF16 = mybir.dt.bfloat16
I32 = mybir.dt.int32
AF = mybir.ActivationFunctionType
ALU = mybir.AluOpType
AXX = mybir.AxisListType.X
NPBF = ml_dtypes.bfloat16

C = 1024
H = 16
NL = 64
REG = 1e-6
LN_EPS = 1e-5
FFN_EPS = 1e-8
MU_SCALE = 10752.0
NEWTON_ITERS = 31
DS = 0.125

ENGINES = ("sync", "tensor", "vector", "scalar", "gpsimd")
DMACH = ("c", "w", "x", "o", "y", "s", "t", "k", "l", "m", "n", "b", "d", "v")


def _hilo(a):
    a = np.asarray(a, np.float64)
    hi = a.astype(np.float32)
    lo = (a - hi).astype(np.float32)
    return hi, lo


def host_consts(T):
    nss = T // 128
    sp = ((2.0 ** (-0.5)) ** np.arange(1, 17)) * 0.125
    cst = {}
    cst["identb"] = np.eye(128).astype(NPBF)
    cst["identf"] = np.eye(128).astype(np.float32)
    cst["onesrow"] = np.ones((1, 128), np.float32)
    cst["onescol"] = np.ones((128, 1), np.float32)
    cst["onesrowb"] = np.ones((1, 128), NPBF)
    cst["iotac"] = np.arange(128, dtype=np.float32).reshape(128, 1)
    cst["i64d"] = np.eye(64, dtype=np.float32)
    cst["i2stk"] = np.concatenate([np.eye(64), np.eye(64)], 0).astype(np.float32)
    cst["i2rep"] = np.tile(cst["i2stk"], (1, 8)).astype(np.float32)
    koffs = list(range(-(nss - 1), 4))
    nko = len(koffs)
    bc = np.zeros((128, H * nko))
    for h in range(H):
        for oi, ko in enumerate(koffs):
            bc[:, h * nko + oi] = sp[h] * (np.arange(128) + 128.0 * ko)
    cst["bcol_hi"], cst["bcol_lo"] = _hilo(bc)
    ii = np.arange(128)[:, None]
    jj = np.arange(128)[None, :]
    Et = np.zeros((128, 128 * H))
    Ep = np.zeros((128, 128 * H))
    for h in range(H):
        e = np.exp(sp[h] * (jj - ii))
        Et[:, h * 128:(h + 1) * 128] = e
        Ep[:, h * 128:(h + 1) * 128] = np.where(ii > jj, e, 1.0)
    cst["Et_hi"], cst["Et_lo"] = _hilo(Et)
    cst["Ep_hi"], cst["Ep_lo"] = _hilo(Ep)
    F1 = np.zeros((128, 512))
    for h in range(H):
        F1[h, :] = np.exp(sp[h] * np.arange(512))
    cst["F1_hi"], cst["F1_lo"] = _hilo(F1)
    cc = np.zeros((128, 48))
    for h in range(H):
        for m in range(3):
            cc[:, h * 3 + m] = np.exp(sp[h] * 128.0 * (m - 3))
    cst["ct_hi"], cst["ct_lo"] = _hilo(cc)
    consts = [1.0 / 1024, -1.0, 1e-5, 1e-8, 127.0, -128.0, 1.0, 0.5, 1.5, 2.0,
              3.0, 4.0, 10752.0, 1e-6, 1.0 / 127.0, 1.0 / 2 ** 20, 1.0 / 65.0,
              1e-10, 1e-30, float(T - 1), 0.0, float(2 ** 23), float(2.0 ** -32)]
    consts += [float(i * 128) for i in range(T // 128)]
    ctab = np.zeros((128, 23 + T // 128), np.float32)
    for i, v in enumerate(consts):
        ctab[:, i] = np.float32(v)
    cst["ctable"] = ctab
    cst["sc32row"] = np.full((1, 128), np.float32(2.0 ** -32), np.float32)
    cst["_koffs"] = koffs
    return cst


class Sched:
    def __init__(self):
        self.items = {e: [] for e in ENGINES}
        self.count = {e: 0 for e in ENGINES}
        for chn in DMACH:
            self.count[chn] = 0

    def emit(self, eng, fn, wait=(), ch=None):
        if eng == "sync":
            assert ch in DMACH, ch
            self.count[ch] += 16
            sem_inc = (ch, 16)
        else:
            self.count[eng] += 1
            sem_inc = (eng, 1)
        w = {}
        for m in wait:
            if m is None:
                continue
            s, v = m
            w[s] = max(w.get(s, 0), v)
        self.items[eng].append((fn, sorted(w.items()), sem_inc))
        return (sem_inc[0], self.count[sem_inc[0]])

    def bar(self):
        keys = [e for e in ENGINES if e != "sync"] + list(DMACH)
        return tuple((e, self.count[e]) for e in keys if self.count[e] > 0)


def build_program(T=2048, NSEQ=2, debug=(), maxph=9):
    nc = bass.Bass("TRN2", target_bir_lowering=False)
    NCH = T // 128
    NTQ = T // 512
    NSS = T // 128
    NTB = T // 512
    cst = host_consts(T)
    KOFFS = cst["_koffs"]
    NKO = len(KOFFS)
    KOFF0 = KOFFS[0]

    def dI(name, shape, dt=F32):
        return nc.dram_tensor(name, list(shape), dt, kind="ExternalInput").ap()

    x_h = dI("x", (NSEQ * T, C))
    w_h = {n: dI(n, (C, C)) for n in ("wq_a", "wk_a", "wv_a", "wo_a", "wffn",
                                      "wq_b", "wpr_b")}
    w_h["wkv_b"] = dI("wkv_b", (C, 512))
    bias_a_h = dI("bias_a", (4, C))
    bqfm_h = dI("bqfm", (128, 8))
    bpr_b_h = dI("bpr_b", (1, C))
    temp_h = dI("temp", (1, 1))
    lidx_h = dI("lidx", (1, NL), I32)
    ch_ = {}
    for n, a in cst.items():
        if isinstance(a, np.ndarray):
            ch_[n] = dI(n, a.shape, BF16 if a.dtype == NPBF else F32)
    out_h = nc.dram_tensor("out", [NSEQ * T, C], F32, kind="ExternalOutput").ap()
    _dbg_specs = {
        "eKt": ([128, (T // 128) * H], F32), "ltm": ([64, C], BF16),
        "nlb": ([64, H], F32), "cqb": ([64, H], F32), "zbacc": ([128, 8 * 65], F32),
        "w0sb": ([128, 8 * 64], F32), "ablk": ([128, 8 * 128], F32),
        "mstk": ([128, 8 * 64], F32), "ysol": ([64, H * 65], BF16),
        "l2fm": ([128, 8 * 64], BF16), "qfm": ([128, 8 * 512], BF16),
        "atfm": ([128, 8 * 512], BF16), "R_CI": ([T, C], F32),
        "kfmb": ([128, 2 * T], BF16), "qfmb": ([128, 8 * T], BF16),
        "vtmb": ([128, (T // 128) * 4 * 65], BF16), "obfm": ([128, 8 * 512], BF16),
        "kfm": ([128, 8 * T], BF16), "vtm": ([128, (T // 128) * H * 65], BF16),
        "phiktm": ([128, 512], BF16), "R_CII": ([T, C], F32),
        "ypsx": ([128, 65], F32), "mblk0": ([128, 128], F32),
        "ohb": ([128, 4 * 64], BF16), "w0psw": ([128, 512], F32),
        "ccol": ([128, 32], F32), "ctable": ([128, 23 + T // 128], F32), "biasT": ([128, 2 * C], F32),
        "wqffn": ([128, 8 * C], BF16), "idxb": ([128, NL], F32),
        "R0": ([128, C], F32), "xlnb0": ([128, C], BF16), "xfm0": ([128, C], BF16),
        "ktm0": ([128, C], BF16),
    }
    dbg_h = {n: nc.dram_tensor("dbg_" + n, list(_dbg_specs[n][0]), _dbg_specs[n][1],
                               kind="ExternalOutput").ap() for n in debug}
    kfm_d = nc.dram_tensor("kfm_sp", [128, 8 * T], BF16, kind="Internal").ap()
    kfm_d3 = kfm_d.rearrange("p (c t) -> p c t", c=8)
    biasT_d = nc.dram_tensor("biasT_sp", [128, 2 * C], F32, kind="Internal").ap()

    S = Sched()

    with ExitStack() as ctx:
        ent = ctx.enter_context
        sems = {e: ent(nc.semaphore("sem_" + e)) for e in ENGINES}
        for chn in DMACH:
            sems[chn] = ent(nc.semaphore("sem_" + chn))

        Rall = ent(nc.sbuf_tensor("Rall", [128, NCH * C], F32))
        R = [Rall[:, i * C:(i + 1) * C] for i in range(NCH)]
        wqffn = ent(nc.sbuf_tensor("wqffn", [128, 8 * C], BF16))
        cs = {}
        for n in ("identb", "identf", "onesrow", "onescol", "onesrowb", "iotac",
                  "i64d", "i2stk", "i2rep", "ctable", "sc32row"):
            cs[n] = ent(nc.sbuf_tensor(n + "_s", list(cst[n].shape),
                                       BF16 if cst[n].dtype == NPBF else F32))
        idxb_s = ent(nc.sbuf_tensor("idxb_s", [128, NL], F32))
        stage = ent(nc.sbuf_tensor("stage", [128, 1024], F32))
        stage2 = ent(nc.sbuf_tensor("stage2", [128, 128], F32)) if debug else None
        bprow = ent(nc.sbuf_tensor("bprow", [1, C], BF16))
        bqfm_s = ent(nc.sbuf_tensor("bqfm_s", [128, 8], BF16))
        ccol = ent(nc.sbuf_tensor("ccol", [128, 32], F32))
        crow = ent(nc.sbuf_tensor("crow", [1, 64], F32))
        scr = ent(nc.sbuf_tensor("scr", [128, 1024], F32))

        PSW0 = ent(nc.psum_tensor("PSW0", [128, 1024], F32))
        PSW1 = ent(nc.psum_tensor("PSW1", [128, 1024], F32))
        PS0 = ent(nc.psum_tensor("PS0", [128, 512], F32))
        PS1 = ent(nc.psum_tensor("PS1", [128, 512], F32))
        PT = ent(nc.psum_tensor("PT", [128, 1024], BF16))
        PSX = ent(nc.psum_tensor("PSX", [128, 512], F32))

        (CTEMP, CINVT, CNEGINVT, CEPS, CMU, CREGMU, CMAGIC, CWS, CWSD, CRWS,
         CACC, CRED, CSUM, CSQ, CMEAN, CVEPS, CRSTD, CNMR, CAMAX, COSC) = range(20)

        def col(i):
            return ccol[:, i:i + 1]

        def Et_h(h):
            return EtEp_s[:, h * 128:(h + 1) * 128]

        def Ep_h(h):
            return EtEp_s[:, 2048 + h * 128:2048 + (h + 1) * 128]

        _KIDX = {1.0 / 1024: 0, -1.0: 1, 1e-5: 2, 1e-8: 3, 127.0: 4, -128.0: 5,
                 1.0: 6, 0.5: 7, 1.5: 8, 2.0: 9, 3.0: 10, 4.0: 11, 10752.0: 12,
                 1e-6: 13, 1.0 / 127.0: 14, 1.0 / 2 ** 20: 15, 1.0 / 65.0: 16,
                 1e-10: 17, 1e-30: 18, float(T - 1): 19, 0.0: 20,
                 float(2 ** 23): 21, float(2.0 ** -32): 22}
        for _i in range(T // 128):
            _KIDX.setdefault(float(_i * 128), 23 + _i)
        ctab_ready = [None]

        def KC(v, rows=128):
            i = _KIDX[float(v)]
            return cs["ctable"][0:rows, i:i + 1]

        def dma(dst, src, wait=(), ch="c"):
            return S.emit("sync", lambda e, d=dst, s=src: e.dma_start(out=d, in_=s),
                          wait, ch=ch)

        def ev(eng, name, wait=(), drain=None, **kw):
            if eng == "vector":
                conv = False
                for k in ("scalar1", "scalar2", "scalar"):
                    v = kw.get(k)
                    if isinstance(v, float):
                        in0 = kw.get("in0")
                        if in0 is not None:
                            bp = in0.base_partition
                            if callable(bp):
                                bp = bp()
                            npart = in0.shape[0]
                        else:
                            bp, npart = 0, 128
                        i = _KIDX[float(v)]
                        kw[k] = cs["ctable"][bp:bp + npart, i:i + 1]
                        conv = True
                if conv and ctab_ready[0] is not None:
                    wait = tuple(wait) + (ctab_ready[0],)
            elif eng == "scalar":
                _is_copy = kw.get("func") in (AF.Copy, AF.Identity)
                for k in ("scale", "bias"):
                    v = kw.get(k)
                    if isinstance(v, float) and not _is_copy:
                        kw[k] = np.float32(v)
            _drain = eng == "vector" and name in (
                "reduce_sum", "reduce_max", "tensor_reduce", "reciprocal",
                "bn_stats", "bn_aggr")
            if drain is not None:
                _drain = drain

            import traceback as _tb
            _site = _tb.extract_stack(limit=3)[-2]

            def fn(e, _n=name, _kw=dict(kw), _d=_drain, _s=_site):
                try:
                    ins = getattr(e, _n)(**_kw)
                except Exception as ex:
                    raise RuntimeError(
                        f"ev scheduled at {_s.filename}:{_s.lineno}: {ex}") from ex
                if _d:
                    ins = e.drain()
                return ins
            return S.emit(eng, fn, wait)

        def mm(out, lhsT, rhs, start=True, stop=True, tile=None, wait=()):
            import traceback
            site = traceback.extract_stack(limit=3)[-2]

            def fn(e, o=out, l=lhsT, r=rhs, st=start, sp=stop, tp=tile, _s=site):
                try:
                    return e.matmul(o, l, r, start=st, stop=sp, tile_position=tp)
                except Exception as ex:
                    raise RuntimeError(
                        f"mm scheduled at {_s.filename}:{_s.lineno}: {ex}") from ex
            return S.emit("tensor", fn, wait)

        def transp(out, in_, ident, wait=(), tile_position=None):
            def fn(e, o=out, i=in_, d=ident, tp=tile_position):
                return e.transpose(o, i, d, tile_position=tp)
            return S.emit("tensor", fn, wait)

        def vcp(dst, src, wait=()):
            return ev("vector", "tensor_copy", wait=wait, out=dst, in_=src)

        def vrecip(out, in_, wait=()):
            def fn(e, o=out, i=in_):
                e.drain()
                e.reciprocal(out=o, in_=i)
                return e.drain()
            return S.emit("vector", fn, wait)

        def dtap(name, srcs, wait):
            if name in dbg_h:
                return dma(dbg_h[name], srcs, wait=wait, ch="t")
            return None

        with ExitStack() as sctx0:
            e0 = sctx0.enter_context
            biasT0 = e0(nc.sbuf_tensor("biasT0", [128, 2 * C], F32))
            land4 = e0(nc.sbuf_tensor("land4", [1, 4 * C], F32))
            land1 = e0(nc.sbuf_tensor("land1", [1, C], F32))
            bqstg = e0(nc.sbuf_tensor("bqstg", [128, 8], F32))
            idx_i32 = e0(nc.sbuf_tensor("idx_i32", [1, NL], I32))

            for n in cs:
                dma(cs[n][:], ch_[n])
            dma(ccol[0:1, CTEMP:CTEMP + 1], temp_h)
            dma(idx_i32[:], lidx_h)
            dma(land4[:], bias_a_h.rearrange("(o a) c -> o (a c)", o=1))
            dma(land1[:], bpr_b_h)
            m_call = dma(bqstg[:], bqfm_h)
            ctab_ready[0] = m_call

            m_bq = ev("vector", "tensor_copy", wait=(m_call,), out=bqfm_s[:],
                      in_=bqstg[:])

            m1 = vrecip(ccol[0:1, CINVT:CINVT + 1], ccol[0:1, CTEMP:CTEMP + 1],
                        wait=(m_call,))
            m2 = ev("vector", "memset", wait=(), ap=ccol[0:1, CEPS:CEPS + 1],
                    constant=3.0)
            m2b = vrecip(ccol[0:1, CEPS:CEPS + 1], ccol[0:1, CEPS:CEPS + 1], wait=(m2,))
            m3 = ev("vector", "tensor_scalar", wait=(m2b,), out=ccol[0:1, CEPS:CEPS + 1],
                    in0=ccol[0:1, CEPS:CEPS + 1], scalar1=4.0, scalar2=1.0,
                    op0=ALU.mult, op1=ALU.subtract)
            m4 = ev("vector", "tensor_scalar", wait=(m3,), out=ccol[0:1, CEPS:CEPS + 1],
                    in0=ccol[0:1, CEPS:CEPS + 1], scalar1=3.0, scalar2=1.0,
                    op0=ALU.mult, op1=ALU.subtract)
            m5 = ev("scalar", "activation", wait=(m4,), out=ccol[0:1, CEPS:CEPS + 1],
                    in_=ccol[0:1, CEPS:CEPS + 1], func=AF.Abs)
            m6 = ev("vector", "tensor_scalar", wait=(m5,), out=ccol[0:1, CMU:CMU + 1],
                    in0=ccol[0:1, CEPS:CEPS + 1], scalar1=10752.0, scalar2=None,
                    op0=ALU.mult)
            m6a = ev("vector", "tensor_scalar", wait=(m5,),
                     out=ccol[0:1, CSUM:CSUM + 1],
                     in0=ccol[0:1, CEPS:CEPS + 1], scalar1=float(2 ** 23),
                     scalar2=None, op0=ALU.mult)
            m6b = ev("vector", "tensor_tensor", wait=(m6, m6a),
                     out=ccol[0:1, CMU:CMU + 1], in0=ccol[0:1, CMU:CMU + 1],
                     in1=ccol[0:1, CSUM:CSUM + 1], op=ALU.mult)
            m7 = ev("vector", "tensor_scalar", wait=(m6b,), out=ccol[0:1, CREGMU:CREGMU + 1],
                    in0=ccol[0:1, CMU:CMU + 1], scalar1=1e-6, scalar2=None, op0=ALU.add)
            m8 = vrecip(ccol[0:1, CMAGIC:CMAGIC + 1], ccol[0:1, CEPS:CEPS + 1],
                        wait=(m5,))
            m9 = ev("vector", "tensor_scalar", wait=(m8,), out=ccol[0:1, CMAGIC:CMAGIC + 1],
                    in0=ccol[0:1, CMAGIC:CMAGIC + 1], scalar1=1.5, scalar2=None,
                    op0=ALU.mult)
            m10 = ev("vector", "tensor_scalar", wait=(m1,),
                     out=ccol[0:1, CNEGINVT:CNEGINVT + 1],
                     in0=ccol[0:1, CINVT:CINVT + 1], scalar1=-1.0, scalar2=None,
                     op0=ALU.mult)
            m11 = mm(PSX[:, 0:7], cs["onesrow"][:], ccol[0:1, 0:7],
                     wait=(m7, m9, m10, m_call))
            m_cols = vcp(ccol[:, 0:7], PSX[:, 0:7], wait=(m11,))

            mi2 = vcp(crow[0:1, 0:NL], idx_i32[:], wait=(m_call,))
            mi3 = ev("vector", "tensor_scalar", wait=(mi2,), out=crow[0:1, 0:NL],
                     in0=crow[0:1, 0:NL], scalar1=0.0, scalar2=float(T - 1),
                     op0=ALU.max, op1=ALU.min)
            mi4 = mm(PSX[:, 16:16 + NL], cs["onesrow"][:], crow[0:1, 0:NL],
                     wait=(mi3, m_cols))
            m_idx = vcp(idxb_s[:], PSX[:, 16:16 + NL], wait=(mi4,))

            m_bt = (m_idx,)
            for bi, row in ((0, 1), (1, 2)):
                ma = mm(PSW0[:, 0:512], cs["onesrow"][:],
                        land4[0:1, row * C:row * C + 512],
                        wait=m_bt + (m_call,))
                mb = mm(PSW0[:, 512:1024], cs["onesrow"][:],
                        land4[0:1, row * C + 512:row * C + 1024],
                        wait=())
                m_bt = (ev("vector", "tensor_copy", wait=(ma, mb),
                           out=biasT0[:, bi * C:(bi + 1) * C], in_=PSW0[:, 0:1024]),)
            m_btsp = dma(biasT_d, biasT0[:], wait=m_bt, ch="c")
            m_bp = vcp(bprow[:], land1[:], wait=(m_call,))

            mz = ev("vector", "memset", wait=(), ap=col(CACC), constant=0.0)
            m_p = (mz, m_bp, m_cols)
            for ci in range(8):
                md = dma(stage[:, 0:1024], w_h["wffn"][ci * 128:(ci + 1) * 128, :],
                         wait=m_p, ch="w")
                mrd = ev("vector", "reduce_sum", wait=(md,), out=col(CRED),
                         in_=stage[:, 0:1024], axis=AXX, apply_absolute_value=True)
                ma = ev("vector", "tensor_add", wait=(mrd,), out=col(CACC),
                        in0=col(CACC), in1=col(CRED))
                m_p = (md, ma)
            mw1 = mm(PSX[0:1, 30:31], col(CACC), cs["onescol"][:], wait=m_p + (m_call,))
            mw2 = ev("vector", "tensor_scalar", wait=(mw1,), out=crow[0:1, 32:33],
                     in0=PSX[0:1, 30:31], scalar1=float(1.0 / 2 ** 20), scalar2=1e-8,
                     op0=ALU.mult, op1=ALU.add)
            mw3 = vrecip(crow[0:1, 33:34], crow[0:1, 32:33], wait=(mw2,))
            mw4 = ev("vector", "tensor_scalar", wait=(mw2,), out=crow[0:1, 34:35],
                     in0=crow[0:1, 32:33], scalar1=float(1.0 / 127.0), scalar2=None,
                     op0=ALU.mult)
            mw5 = mm(PSX[:, 32:35], cs["onesrow"][:], crow[0:1, 32:35], wait=(mw3, mw4))
            mw6 = vcp(col(CWS), PSX[:, 32:33], wait=(mw5,))
            mw7 = vcp(col(CRWS), PSX[:, 33:34], wait=(mw5,))
            mw8 = vcp(col(CWSD), PSX[:, 34:35], wait=(mw5,))
            m_p = (mw6, mw7, mw8)
            for ci in range(8):
                md = dma(stage[:, 0:1024], w_h["wffn"][ci * 128:(ci + 1) * 128, :],
                         wait=m_p, ch="w")
                q1 = ev("vector", "tensor_scalar", wait=(md,), out=stage[:, 0:1024],
                        in0=stage[:, 0:1024], scalar1=col(CRWS), scalar2=None,
                        op0=ALU.mult)
                q2 = ev("vector", "tensor_scalar", wait=(q1,), out=stage[:, 0:1024],
                        in0=stage[:, 0:1024], scalar1=col(CMAGIC), scalar2=col(CMAGIC),
                        op0=ALU.add, op1=ALU.subtract)
                q3 = ev("vector", "tensor_scalar", wait=(q2,),
                        out=wqffn[:, ci * C:(ci + 1) * C], in0=stage[:, 0:1024],
                        scalar1=-1.0, scalar2=1.0, op0=ALU.max, op1=ALU.min)
                m_p = (md, q3)
        m_start_done = S.bar()

        ln_gate = [()]

        def ln_chunk(src, out_bf, out_f32, wait):
            ms = ev("vector", "reduce_sum", wait=tuple(wait) + ln_gate[0],
                    out=col(CSUM), in_=src, axis=AXX, drain=False)
            mq = ev("scalar", "activation", wait=tuple(wait) + ln_gate[0],
                    out=scr[:, 0:1024],
                    in_=src, func=AF.Square, accum_out=col(CSQ))
            mmn = ev("scalar", "activation", wait=(ms,), out=col(CMEAN),
                     in_=col(CSUM), func=AF.Copy, scale=float(1.0 / C))
            mmnn = ev("scalar", "activation", wait=(ms,), out=col(CNMR),
                      in_=col(CSUM), func=AF.Copy, scale=float(-1.0 / C))
            mv1 = ev("scalar", "activation", wait=(mq,), out=col(CVEPS),
                     in_=col(CSQ), func=AF.Copy, scale=float(1.0 / C))
            mv2 = ev("vector", "scalar_tensor_tensor", wait=(mmn, mv1), out=col(CVEPS),
                     in0=col(CMEAN), scalar=col(CMEAN), in1=col(CVEPS),
                     op0=ALU.mult, op1=ALU.subtract)
            msq = ev("scalar", "activation", wait=(mv2,), out=col(CRSTD),
                     in_=col(CVEPS), func=AF.Sqrt, scale=float(-1.0),
                     bias=float(LN_EPS))
            mrc = ev("vector", "reciprocal", wait=(msq,), out=col(CRSTD),
                     in_=col(CRSTD), drain=False)
            mnm = ev("scalar", "activation", wait=(mrc, mmnn), out=col(CNMR),
                     in_=col(CNMR), func=AF.Copy, scale=col(CRSTD))
            mrs2 = ev("scalar", "activation", wait=(mrc,), out=col(CRSTD),
                      in_=col(CRSTD), func=AF.Copy, scale=1.0)
            outs = []
            if out_bf is not None:
                outs.append(ev("vector", "tensor_scalar", wait=(mnm, mrs2),
                               out=out_bf, in0=src, scalar1=col(CRSTD),
                               scalar2=col(CNMR), op0=ALU.mult, op1=ALU.add))
            if out_f32 is not None:
                outs.append(ev("vector", "tensor_scalar", wait=(mnm, mrs2),
                               out=out_f32, in0=src, scalar1=col(CRSTD),
                               scalar2=col(CNMR), op0=ALU.mult, op1=ALU.add))
            ln_gate[0] = (mv2, outs[-1], mrc)
            return outs

        def transpose8(src_bf, dst_bf_3d, wait, nblk=8, vwait=()):
            mts = []
            for b in range(nblk):
                mts.append(transp(PT[:, b * 128:(b + 1) * 128],
                                  src_bf[:, b * 128:(b + 1) * 128], cs["identb"][:],
                                  wait=wait if b == 0 else ()))
            return vcp(dst_bf_3d, PT[:, 0:nblk * 128], wait=(mts[-1],) + tuple(vwait))

        def proj_tm(xfm_ap, w_sb, psw, nout, wait):
            last = None
            nh_cnt = (nout + 511) // 512
            for ci in range(8):
                for nh in range(nh_cnt):
                    n0, n1 = nh * 512, min(nout, (nh + 1) * 512)
                    last = mm(psw[:, n0:n1], xfm_ap[:, ci * 128:(ci + 1) * 128],
                              w_sb[:, ci * nout + n0: ci * nout + n1],
                              start=(ci == 0), stop=(ci == 7),
                              wait=wait if (ci == 0 and nh == 0) else ())
            return last

        def load_w_bf(dst, src_h, ncols, wait, buf2=None):
            if buf2 is None:
                m_prev = wait
                last = None
                for ci in range(8):
                    md = dma(stage[:, 0:ncols],
                             src_h[ci * 128:(ci + 1) * 128, 0:ncols],
                             wait=m_prev, ch="w")
                    mv = vcp(dst[:, ci * ncols:(ci + 1) * ncols],
                             stage[:, 0:ncols], wait=(md,))
                    m_prev = (md, mv)
                    last = mv
                return last
            bufs = (stage, buf2)
            chs = ("w", "v")
            m_cons = [tuple(wait), tuple(wait)]
            last = None
            for ci in range(8):
                par = ci % 2
                md = dma(bufs[par][:, 0:ncols],
                         src_h[ci * 128:(ci + 1) * 128, 0:ncols],
                         wait=m_cons[par], ch=chs[par])
                mv = vcp(dst[:, ci * ncols:(ci + 1) * ncols], bufs[par][:, 0:ncols],
                         wait=(md,))
                m_cons[par] = (mv,)
                last = mv
            return last

        m_seq_entry = m_start_done
        for sq in range(NSEQ):
            sP = ExitStack()
            zbacc = sP.enter_context(nc.sbuf_tensor(f"zbacc{sq}", [128, 8 * 65], F32))
            w0acc = sP.enter_context(nc.sbuf_tensor(f"w0acc{sq}", [128, 8 * 64], F32))
            ysol = sP.enter_context(nc.sbuf_tensor(f"ysol{sq}", [64, H * 65], BF16))
            l2fm = sP.enter_context(nc.sbuf_tensor(f"l2fm{sq}", [128, 8 * 64], BF16))
            cqb = sP.enter_context(nc.sbuf_tensor(f"cqb{sq}", [64, H], F32))
            with ExitStack() as sA:
                eA = sA.enter_context
                vtm = eA(nc.sbuf_tensor(f"vtm{sq}", [128, NCH * H * 65], BF16))
                eKt = eA(nc.sbuf_tensor(f"eKt{sq}", [128, NCH * H], F32))
                ktm_t = [eA(nc.sbuf_tensor(f"ktm{sq}_{i}", [128, C], BF16))
                         for i in range(2)]
                xlnb = [eA(nc.sbuf_tensor(f"xlnb{sq}_{i}", [128, C], BF16))
                        for i in range(2)]
                xfm = [eA(nc.sbuf_tensor(f"xfm{sq}_{i}", [128, C], BF16))
                       for i in range(2)]
                ltm = eA(nc.sbuf_tensor(f"ltm{sq}", [64, C], BF16))
                nlb = eA(nc.sbuf_tensor(f"nlb{sq}", [64, H], F32))
                phikp = eA(nc.sbuf_tensor(f"phikp{sq}", [64, 2 * 512], BF16))
                phiktm = eA(nc.sbuf_tensor(f"phiktm{sq}", [128, 512], BF16))
                ohb = eA(nc.sbuf_tensor(f"ohb{sq}", [128, 4 * 64], BF16))
                biasT = eA(nc.sbuf_tensor(f"biasTa{sq}", [128, 2 * C], F32))
                sA1 = ExitStack()
                wkb = sA1.enter_context(nc.sbuf_tensor(f"wkb{sq}", [128, 8 * C], BF16))
                wvb = sA1.enter_context(nc.sbuf_tensor(f"wvb{sq}", [128, 8 * C], BF16))
                kstg = sA1.enter_context(nc.sbuf_tensor(f"kstg{sq}", [128, 2 * 1024],
                                                        BF16))
                lstg = sA1.enter_context(nc.sbuf_tensor(f"lstg{sq}", [128, 1024],
                                                        F32))

                vtm4 = vtm[:].rearrange("p (n h e) -> p n h e", n=NCH, h=H)

                m_bias = dma(biasT[:], biasT_d, wait=(m_btsp,), ch="b")
                mwk = load_w_bf(wkb, w_h["wk_a"], C, m_seq_entry, buf2=lstg)
                mwv = load_w_bf(wvb, w_h["wv_a"], C, (mwk,), buf2=lstg)
                mz1 = ev("vector", "memset", wait=m_seq_entry, ap=zbacc[:], constant=0.0)
                mzw = ev("vector", "memset", wait=(), ap=w0acc[:], constant=0.0)
                mvo = mz1
                for i_ in range(NCH):
                    mvo = ev("vector", "memset", wait=m_seq_entry if i_ == 0 else (),
                             ap=vtm[:, i_ * H * 65:(i_ + 1) * H * 65]
                             .rearrange("p (h e) -> p h e", h=H)[:, :, 64:65],
                             constant=1.0)

                m_x = dma(Rall[:].rearrange("p (n c) -> p n c", n=NCH),
                          x_h[sq * T:(sq + 1) * T, :]
                          .rearrange("(n p) c -> p n c", p=128),
                          wait=m_seq_entry, ch="x")
                m_kevac = [None, None]
                m_spill = [None, None]
                m_kcons = [None, None]
                m_vcons = None
                m_xfm_prev = None
                m_kfm_last = None
                m_lmm_last = None
                m_ek_last = None
                m_vev_last = None
                for i in range(NCH):
                    par = i % 2
                    mx = m_x
                    w0_ = (mx, mwv) + (m_kcons[par] if m_kcons[par] else ())
                    mln = ln_chunk(R[i], xlnb[par][:], None, w0_)
                    mxf = transpose8(xlnb[par][:],
                                     xfm[par][:].rearrange("p (a b) -> p a b", a=8),
                                     tuple(mln) + ((m_xfm_prev,) if m_xfm_prev else ()))
                    m_xfm_prev = mxf
                    mk_mm = proj_tm(xfm[par][:], wkb, PSW0, C,
                                    (mxf, mwk) + ((m_kevac[1 - par],) if m_kevac[1 - par] else ()))
                    mke = ev("vector", "tensor_add", wait=(mk_mm, m_bias),
                             out=ktm_t[par][:],
                             in0=PSW0[:, 0:1024], in1=biasT[:, 0:C])
                    m_kevac[par] = mke
                    if sq == 0 and i == 0 and "ccol" in dbg_h:
                        mcc = vcp(stage2[:, 0:32], ccol[:, 0:32], wait=tuple(mln))
                        dtap("ccol", stage2[:, 0:32], (mcc,))
                        dtap("ctable", cs["ctable"][:], (mcc,))
                    if sq == 0 and i == 0:
                        dtap("R0", R[0], (mx,))
                        dtap("xlnb0", xlnb[0][:], tuple(mln))
                        dtap("xfm0", xfm[0][:], (mxf,))
                        dtap("ktm0", ktm_t[0][:], (mke,))
                        dtap("biasT", biasT[:], (m_bias,))
                        dtap("wqffn", wqffn[:], m_p)
                        dtap("idxb", idxb_s[:], (m_idx,))
                    msq2 = ev("scalar", "activation", wait=(mke,), out=scr[:, 0:1024],
                              in_=ktm_t[par][:], func=AF.Square)
                    mnk = ev("vector", "reduce_sum", wait=(msq2,),
                             out=eKt[:, i * H:(i + 1) * H],
                             in_=scr[:, 0:1024].rearrange("p (h d) -> p h d", h=H),
                             axis=AXX, drain=False)
                    ln_gate[0] = ln_gate[0] + (mnk,)
                    mek = ev("scalar", "activation", wait=(mnk, m_cols),
                             out=eKt[:, i * H:(i + 1) * H],
                             in_=eKt[:, i * H:(i + 1) * H], func=AF.Exp,
                             scale=col(CNEGINVT))
                    m_ek_last = mek
                    kstg3 = kstg[:, par * 1024:(par + 1) * 1024].rearrange(
                        "p (c u) -> p c u", c=8)
                    mkf = transpose8(ktm_t[par][:], kstg3, (mke, mxf),
                                     vwait=(m_spill[par],) if m_spill[par] else ())
                    m_kfm_last = mkf
                    m_spill[par] = dma(kfm_d3[:, :, i * 128:(i + 1) * 128], kstg3,
                                       wait=(mkf,), ch="k" if par == 0 else "l")
                    moh = ev("vector", "tensor_scalar", wait=(m_idx,),
                             out=ohb[:, 0:64], in0=idxb_s[:], scalar1=cs["iotac"][:],
                             scalar2=float(i * 128), op0=ALU.subtract, op1=ALU.is_equal)
                    ml1 = mm(PS0[0:64, 0:512], ohb[:, 0:64], ktm_t[par][:, 0:512],
                             start=(i == 0), stop=(i == NCH - 1), wait=(moh, mke))
                    ml2 = mm(PS1[0:64, 0:512], ohb[:, 0:64], ktm_t[par][:, 512:1024],
                             start=(i == 0), stop=(i == NCH - 1), wait=())
                    m_lmm_last = ml2
                    mv_mm = proj_tm(xfm[par][:], wvb, PSW1, C,
                                    (mxf, mwv, mvo) + ((m_vcons,) if m_vcons else ()))
                    mve = ev("vector", "tensor_tensor", wait=(mv_mm, m_bias),
                             out=vtm4[:, i, :, 0:64],
                             in0=PSW1[:, 0:1024].rearrange("p (h d) -> p h d", h=H),
                             in1=biasT[:, C:2 * C].rearrange("p (h d) -> p h d", h=H),
                             op=ALU.add)
                    m_vcons = mve
                    m_vev_last = mve
                    m_kcons[par] = (mkf, ml2, mnk)

                sA1.close()
                kbuf = eA(nc.sbuf_tensor(f"kbuf{sq}", [128, 2 * 8 * 512], BF16))

                def kpre_fetch(tb, wait):
                    h2 = tb % 2
                    return dma(kbuf[:, h2 * 4096:(h2 + 1) * 4096]
                               .rearrange("p (c u) -> p c u", c=8),
                               kfm_d3[:, :, tb * 512:(tb + 1) * 512],
                               wait=wait, ch="m" if h2 == 0 else "n")

                def kread(tb, p, hh):
                    h2 = tb % 2
                    return kbuf[hh * 64:(hh + 1) * 64,
                                h2 * 4096 + p * 512:h2 * 4096 + (p + 1) * 512]

                if maxph < 1:
                    if sq == 0:
                        dtap("eKt", eKt[:], (m_ek_last,))
                        dma(dbg_h["kfm"], kfm_d,
                            wait=(m_spill[0], m_spill[1]), ch="t") \
                            if "kfm" in dbg_h else None
                        dtap("vtm", vtm[:], (m_vev_last,))
                    mlt1x = vcp(ltm[:, 0:512], PS0[0:64, 0:512], wait=(m_lmm_last,))
                    mlt2x = vcp(ltm[:, 512:1024], PS1[0:64, 0:512], wait=(m_lmm_last,))
                    if sq == 0:
                        dtap("ltm", ltm[:], (mlt1x, mlt2x))
                    m_seq_entry = S.bar()
                    continue
                mlt1 = vcp(ltm[:, 0:512], PS0[0:64, 0:512], wait=(m_lmm_last,))
                mlt2 = vcp(ltm[:, 512:1024], PS1[0:64, 0:512], wait=(m_lmm_last,))
                msq3 = ev("scalar", "activation", wait=(mlt1, mlt2) + ln_gate[0],
                          out=scr[0:64, 0:1024], in_=ltm[:], func=AF.Square)
                mnl = ev("vector", "reduce_sum", wait=(msq3,), out=nlb[:],
                         in_=scr[0:64, 0:1024].rearrange("p (h d) -> p h d", h=H),
                         axis=AXX, drain=False)
                ln_gate[0] = ln_gate[0] + (mnl,)
                mnlb = ev("scalar", "activation", wait=(mnl, m_cols), out=nlb[:],
                          in_=nlb[:], func=AF.Copy,
                          scale=col(CNEGINVT)[0:64, :])
                mt8 = []
                for b in range(8):
                    mt8.append(transp(PT[:, b * 64:(b + 1) * 64],
                                      ltm[:, b * 128:(b + 1) * 128],
                                      cs["identb"][0:64, 0:64],
                                      wait=(mlt1, mlt2, m_kfm_last) if b == 0 else ()))
                ml2f = ev("vector", "tensor_scalar", wait=(mt8[-1],), out=l2fm[:],
                          in0=PT[:, 0:512], scalar1=2.0, scalar2=None, op0=ALU.mult)
                mqlb = []
                for h in range(H):
                    p, hh = h // 2, h % 2
                    mqlb.append(mm(PSX[0:64, 40 + h:41 + h],
                                   l2fm[hh * 64:(hh + 1) * 64, p * 64:(p + 1) * 64],
                                   bqfm_s[hh * 64:(hh + 1) * 64, p:p + 1],
                                   tile=(hh * 64, 0) if hh else None,
                                   wait=(ml2f, m_bq) if h == 0 else ()))
                mcq1 = vcp(cqb[:], PSX[0:64, 40:40 + H], wait=(mqlb[-1],))
                mcq = ev("vector", "scalar_tensor_tensor", wait=(mcq1, mnlb), out=cqb[:],
                         in0=cqb[:], scalar=col(CINVT)[0:64, :], in1=nlb[:],
                         op0=ALU.mult, op1=ALU.add)

                m_exp_prev = [None, None]
                m_tr_last = None
                m_zev = [None, None]
                m_w0_last = None
                m_w0ev = [None] * 16
                m_pe_prev = None
                kinit_w = (m_spill[0], m_spill[1], m_kevac[0], m_kevac[1], m_vcons)
                m_kpre = [kpre_fetch(0, kinit_w),
                          kpre_fetch(1, kinit_w) if NTB > 1 else None]
                m_kread = [None, None]
                for tb in range(NTB):
                    moh4 = []
                    for tc in range(4):
                        ch = tb * 4 + tc
                        moh4.append(ev("vector", "tensor_scalar",
                                       wait=(m_idx,) +
                                            ((m_w0_last,) if m_w0_last else ()),
                                       out=ohb[:, tc * 64:(tc + 1) * 64], in0=idxb_s[:],
                                       scalar1=cs["iotac"][:], scalar2=float(ch * 128),
                                       op0=ALU.subtract, op1=ALU.is_equal))
                    if sq == 0 and tb == 0:
                        dtap("ohb", ohb[:], (moh4[-1],))
                    for p in range(8):
                        mexp = []
                        for hh in range(2):
                            h = 2 * p + hh
                            ps = (PS0, PS1)[hh]
                            w_ = (ml2f, m_kpre[tb % 2], mnlb)
                            if m_exp_prev[hh]:
                                w_ += (m_exp_prev[hh],)
                            if m_tr_last:
                                w_ += (m_tr_last,)
                            mmk = mm(ps[0:64, 0:512],
                                     l2fm[hh * 64:(hh + 1) * 64, p * 64:(p + 1) * 64],
                                     kread(tb, p, hh),
                                     tile=(hh * 64, 0) if hh else None, wait=w_)
                            m_kread[tb % 2] = mmk
                            me = ev("scalar", "activation", wait=(mmk, mnlb),
                                    out=phikp[:, hh * 512:(hh + 1) * 512],
                                    in_=ps[0:64, 0:512], func=AF.Exp,
                                    scale=col(CINVT)[0:64, :], bias=nlb[0:64, h:h + 1])
                            mexp.append(me)
                            m_exp_prev[hh] = me
                        mt = []
                        for tc in range(4):
                            for hh in range(2):
                                mt.append(transp(
                                    PT[:, (tc * 2 + hh) * 64:(tc * 2 + hh + 1) * 64],
                                    phikp[:, hh * 512 + tc * 128: hh * 512 + (tc + 1) * 128],
                                    cs["identb"][0:64, 0:64],
                                    wait=(mexp[0], mexp[1]) +
                                         ((m_pe_prev,) if m_pe_prev else ())
                                    if (tc == 0 and hh == 0) else ()))
                        m_tr_last = mt[-1]
                        mpes = []
                        for tc in range(4):
                            ch = tb * 4 + tc
                            for hh in range(2):
                                h = 2 * p + hh
                                mpes.append(ev(
                                    "vector", "tensor_scalar",
                                    wait=(mt[-1], m_ek_last) if not mpes else (),
                                    out=phiktm[:, (tc * 2 + hh) * 64:(tc * 2 + hh + 1) * 64],
                                    in0=PT[:, (tc * 2 + hh) * 64:(tc * 2 + hh + 1) * 64],
                                    scalar1=eKt[:, ch * H + h: ch * H + h + 1],
                                    scalar2=None, op0=ALU.mult))
                        mpe = mpes[-1]
                        m_pe_prev = mpe
                        mz_l = [None, None]
                        for hh in range(2):
                            h = 2 * p + hh
                            zsl = PSX[hh * 64:(hh + 1) * 64, 0:65]
                            for tc in range(4):
                                ch = tb * 4 + tc
                                mz_l[hh] = mm(
                                    zsl,
                                    phiktm[:, (tc * 2 + hh) * 64:(tc * 2 + hh + 1) * 64],
                                    vtm4[:, ch, h, :],
                                    start=(tc == 0), stop=(tc == 3),
                                    tile=(0, 64) if hh else None,
                                    wait=(mpe, m_vev_last, m_zev[0], m_zev[1])
                                    if tc == 0 else ())
                        mw_l = [None, None]
                        for hh in range(2):
                            w0ps = PSW0[hh * 64:(hh + 1) * 64, p * 64:(p + 1) * 64]
                            for tc in range(4):
                                if tc == 0:
                                    w0w = (moh4[-1], m_w0ev[p * 2 + hh],
                                           m_w0ev[p * 2 + 1 - hh])
                                    if tb == 0:
                                        w0w += (mpe, mzw)
                                else:
                                    w0w = ()
                                mw_l[hh] = mm(
                                    w0ps,
                                    ohb[:, tc * 64:(tc + 1) * 64],
                                    phiktm[:, (tc * 2 + hh) * 64:(tc * 2 + hh + 1) * 64],
                                    start=(tc == 0), stop=(tc == 3),
                                    tile=(0, 64) if hh else None,
                                    wait=w0w)
                        m_w0_last = mw_l[1]
                        for hh in range(2):
                            h = 2 * p + hh
                            zsl = PSX[hh * 64:(hh + 1) * 64, 0:65]
                            w0ps = PSW0[hh * 64:(hh + 1) * 64, p * 64:(p + 1) * 64]
                            w0sl = w0acc[hh * 64:(hh + 1) * 64, p * 64:(p + 1) * 64]
                            mza = ev("vector", "tensor_add", wait=(mz_l[1],),
                                     out=zbacc[hh * 64:(hh + 1) * 64, p * 65:(p + 1) * 65],
                                     in0=zbacc[hh * 64:(hh + 1) * 64, p * 65:(p + 1) * 65],
                                     in1=zsl)
                            m_zev[hh] = mza
                            m_w0ev[p * 2 + hh] = ev(
                                "vector", "tensor_add", wait=(mw_l[1],),
                                out=w0sl, in0=w0sl, in1=w0ps)
                    if tb + 2 < NTB:
                        m_kpre[tb % 2] = kpre_fetch(tb + 2, (m_kread[tb % 2],))
                m_zb_last = m_zev
                if sq == 0:
                    dtap("eKt", eKt[:], (m_ek_last,))
                    dtap("ltm", ltm[:], (mlt1, mlt2))
                    dtap("nlb", nlb[:], (mnlb,))
                    dtap("cqb", cqb[:], (mcq,))
                    dtap("l2fm", l2fm[:], (ml2f,))
                    if "kfm" in dbg_h:
                        dma(dbg_h["kfm"], kfm_d, wait=(m_spill[0], m_spill[1]),
                            ch="t")
                    dtap("vtm", vtm[:], (m_vev_last,))
                    dtap("zbacc", zbacc[:], (m_zev[0], m_zev[1]))
                    dtap("phiktm", phiktm[:], (m_pe_prev,))
                m_B_done = S.bar()

            if maxph < 2:
                sP.close()
                m_seq_entry = S.bar()
                continue
            with ExitStack() as sS:
                eS = sS.enter_context
                ablk = eS(nc.sbuf_tensor(f"ablk{sq}", [128, 8 * 128], F32))
                mstk = eS(nc.sbuf_tensor(f"mstk{sq}", [128, 8 * 64], F32))
                mblk = eS(nc.sbuf_tensor(f"mblk{sq}", [128, 8 * 128], F32))
                eall = eS(nc.sbuf_tensor(f"eall{sq}", [128, 8 * 64], F32))
                w0sb = eS(nc.sbuf_tensor(f"w0sb{sq}", [128, 8 * 64], F32))
                ystg = eS(nc.sbuf_tensor(f"ystg{sq}", [128, 8 * 65], BF16))

                mwe = []
                for p in range(8):
                    for hh in range(2):
                        mwe.append(ev("vector", "tensor_scalar",
                                      wait=(m_w0_last,) + tuple(m_w0ev)
                                      if not mwe else (),
                                      out=w0sb[hh * 64:(hh + 1) * 64, p * 64:(p + 1) * 64],
                                      in0=w0acc[hh * 64:(hh + 1) * 64, p * 64:(p + 1) * 64],
                                      scalar1=0.5, scalar2=None, op0=ALU.mult))
                if sq == 0 and "w0psw" in dbg_h:
                    mwcp = vcp(stage[:, 0:512], w0acc[:, 0:512], wait=(mwe[-1],))
                    dtap("w0psw", stage[:, 0:512], (mwcp,))
                mz2 = ev("vector", "memset", wait=m_B_done, ap=ablk[:], constant=0.0)
                mz3 = ev("vector", "memset", wait=(), ap=mblk[:], constant=0.0)
                m_tp = None
                mas = None
                for p in range(8):
                    for hh in range(2):
                        sl = w0sb[hh * 64:(hh + 1) * 64, p * 64:(p + 1) * 64]
                        psl = PSX[hh * 64:(hh + 1) * 64, 64:128]
                        mt_ = mm(psl, sl,
                                 cs["identf"][hh * 64:(hh + 1) * 64,
                                              hh * 64:(hh + 1) * 64],
                                 tile=(hh * 64, hh * 64) if hh else None,
                                 wait=(mwe[-1], mz2) + ((m_tp,) if m_tp else ()))
                        dst = ablk[hh * 64:(hh + 1) * 64,
                                   p * 128 + hh * 64: p * 128 + (hh + 1) * 64]
                        ma1 = ev("vector", "tensor_add", wait=(mt_,), out=dst, in0=sl,
                                 in1=psl)
                        mas = ev("vector", "scalar_tensor_tensor", wait=(ma1, m_cols),
                                 out=dst, in0=cs["i2stk"][hh * 64:(hh + 1) * 64, :],
                                 scalar=col(CREGMU)[hh * 64:(hh + 1) * 64, :], in1=dst,
                                 op0=ALU.mult, op1=ALU.add)
                        m_tp = ma1
                mmi = ev("vector", "tensor_scalar", wait=(mas,), out=mstk[:],
                         in0=cs["i2rep"][:], scalar1=float(1.0 / 65.0), scalar2=None,
                         op0=ALU.mult)
                m_it = mmi
                m_e_prev = None
                for it in range(NEWTON_ITERS):
                    mb1 = vcp(mblk[0:64, :].rearrange("p (n c) -> p n c", n=8)[:, :, 0:64],
                              mstk[0:64, :].rearrange("p (n c) -> p n c", n=8),
                              wait=(m_it, mz3))
                    mb2 = vcp(mblk[64:128, :].rearrange("p (n c) -> p n c", n=8)[:, :, 64:128],
                              mstk[64:128, :].rearrange("p (n c) -> p n c", n=8),
                              wait=(m_it,))
                    mq_ = None
                    for p in range(8):
                        mq_ = mm(PSW1[:, p * 64:(p + 1) * 64],
                                 ablk[:, p * 128:(p + 1) * 128],
                                 mstk[:, p * 64:(p + 1) * 64],
                                 wait=(mas, m_it) + ((m_e_prev,) if m_e_prev else ())
                                 if p == 0 else ())
                    me_ = ev("vector", "tensor_tensor", wait=(mq_,), out=eall[:],
                             in0=cs["i2rep"][:], in1=PSW1[:, 0:512], op=ALU.subtract)
                    m_e_prev = me_
                    md_ = None
                    for p in range(8):
                        md_ = mm(PSW0[:, p * 64:(p + 1) * 64],
                                 mblk[:, p * 128:(p + 1) * 128],
                                 eall[:, p * 64:(p + 1) * 64],
                                 wait=(me_, mb1, mb2, mwe[-1]) if p == 0 else ())
                    m_it = ev("vector", "tensor_add", wait=(md_,), out=mstk[:],
                              in0=mstk[:], in1=PSW0[:, 0:512])
                mb1 = vcp(mblk[0:64, :].rearrange("p (n c) -> p n c", n=8)[:, :, 0:64],
                          mstk[0:64, :].rearrange("p (n c) -> p n c", n=8), wait=(m_it,))
                mb2 = vcp(mblk[64:128, :].rearrange("p (n c) -> p n c", n=8)[:, :, 64:128],
                          mstk[64:128, :].rearrange("p (n c) -> p n c", n=8), wait=(m_it,))
                if sq == 0 and "mblk0" in dbg_h:
                    dtap("mblk0", mblk[:, 0:128], (mb1, mb2))
                m_ye_prev = None
                mystg = []
                for p in range(8):
                    my1 = mm(PSX[:, 0:65], mblk[:, p * 128:(p + 1) * 128],
                             zbacc[:, p * 65:(p + 1) * 65],
                             wait=(mb1, mb2, m_zb_last[0], m_zb_last[1]) +
                                  ((m_ye_prev,) if m_ye_prev else ()))
                    my2 = vcp(ysol[0:64, (2 * p) * 65:(2 * p + 1) * 65], PSX[0:64, 0:65],
                              wait=(my1,))
                    my3 = vcp(ystg[64:128, p * 65:(p + 1) * 65],
                              PSX[64:128, 0:65], wait=(my1,))
                    m_ye_prev = my3
                    if sq == 0 and p == 0 and "ypsx" in dbg_h:
                        mt_c = vcp(stage2[:, 0:65], PSX[:, 0:65], wait=(my1,))
                        mt_y = dtap("ypsx", stage2[:, 0:65], (mt_c,))
                        m_ye_prev = mt_c
                    mystg.append(my3)
                m_ys = dma(ysol[0:64, :].rearrange("p (a b s) -> p a b s", a=8, b=2)
                           [:, :, 1, :],
                           ystg[64:128, :].rearrange("p (a s) -> p a s", a=8),
                           wait=tuple(mystg), ch="y")
                if sq == 0:
                    dtap("w0sb", w0sb[:], (mwe[-1],))
                    dtap("ablk", ablk[:], (mas,))
                    dtap("mstk", mstk[:], (m_it,))
                    dtap("ysol", ysol[:], (m_ys,))
                m_S_done = S.bar()

            if maxph < 3:
                sP.close()
                m_seq_entry = S.bar()
                continue
            with ExitStack() as sC:
                eC = sC.enter_context
                stgB = eC(nc.sbuf_tensor(f"stgBc{sq}", [128, 8 * 512], BF16))
                wqab = eC(nc.sbuf_tensor(f"wqab{sq}", [128, 8 * C], BF16))
                woab = eC(nc.sbuf_tensor(f"woab{sq}", [128, 8 * C], BF16))
                qfm = eC(nc.sbuf_tensor(f"qfm{sq}", [128, 8 * 512], BF16))
                atfm = eC(nc.sbuf_tensor(f"atfm{sq}", [128, 8 * 512], BF16))
                phiq = eC(nc.sbuf_tensor(f"phiq{sq}", [64, 2 * 512], BF16))
                xlnb2 = [eC(nc.sbuf_tensor(f"xlnb2{sq}_{i}", [128, C], BF16))
                         for i in range(2)]
                xfm2 = [eC(nc.sbuf_tensor(f"xfm2{sq}_{i}", [128, C], BF16))
                        for i in range(2)]
                qtmp = eC(nc.sbuf_tensor(f"qtmp{sq}", [128, C], BF16))
                aqtm = eC(nc.sbuf_tensor(f"aqtm{sq}", [128, C], BF16))
                aqfm = eC(nc.sbuf_tensor(f"aqfm{sq}", [128, C], BF16))
                scf = eC(nc.sbuf_tensor(f"scf{sq}", [128, 512], F32))
                borow = eC(nc.sbuf_tensor(f"borow{sq}", [1, C], BF16))
                lstgc = eC(nc.sbuf_tensor(f"lstgc{sq}", [128, 1024], F32))

                qfm3 = qfm[:].rearrange("p (c t) -> p c t", c=8)
                m_bod = dma(stage[0:1, 0:1024], bias_a_h[3:4, :], wait=m_S_done, ch="d")
                m_bo = vcp(borow[:], stage[0:1, 0:1024], wait=(m_bod,))
                mwq = load_w_bf(wqab, w_h["wq_a"], C, tuple(m_S_done) + (m_bo,),
                                buf2=lstgc)
                mwo = load_w_bf(woab, w_h["wo_a"], C, (mwq,), buf2=lstgc)

                m_ffn_prev = None
                m_stgB_dma = None
                for tb in range(NTB):
                    mq_done = []
                    m_qt_prev = None
                    for tc in range(4):
                        ch = tb * 4 + tc
                        par = tc % 2
                        w_ = (mwo,) if (tb == 0 and tc == 0) else ()
                        if m_ffn_prev:
                            w_ += (m_ffn_prev,)
                        mln = ln_chunk(R[ch], xlnb2[par][:], None, w_)
                        mxf = transpose8(xlnb2[par][:],
                                         xfm2[par][:].rearrange("p (a b) -> p a b", a=8),
                                         tuple(mln))
                        mqmm = proj_tm(xfm2[par][:], wqab, PSW0, C, (mxf, mwq))
                        mqe = vcp(qtmp[:], PSW0[:, 0:1024], wait=(mqmm,))
                        mqt = transpose8(qtmp[:], qfm3[:, :, tc * 128:(tc + 1) * 128],
                                         (mqe,) + ((m_qt_prev,) if m_qt_prev else ()))
                        m_qt_prev = mqt
                        mq_done.append(mqt)
                    m_cq_prev = None
                    m_ynext = None
                    m_f2_prev = None
                    m_ob_prev = None
                    for h in range(H):
                        p, hh = h // 2, h % 2
                        ps = (PS0, PS1)[h % 2]
                        mmq = mm(ps[0:64, 0:512],
                                 l2fm[hh * 64:(hh + 1) * 64, p * 64:(p + 1) * 64],
                                 qfm[hh * 64:(hh + 1) * 64, p * 512:(p + 1) * 512],
                                 tile=(hh * 64, 0) if hh else None,
                                 wait=(mq_done[-1],) + ((m_cq_prev,) if m_cq_prev else ()))
                        mpq = ev("scalar", "activation", wait=(mmq, mcq),
                                 out=phiq[:, (h % 2) * 512:(h % 2 + 1) * 512],
                                 in_=ps[0:64, 0:512], func=AF.Exp,
                                 scale=col(CINVT)[0:64, :], bias=cqb[0:64, h:h + 1])
                        m_cq_prev = mpq
                        myp = mm(PSW1[0:65, 0:512], ysol[0:64, h * 65:(h + 1) * 65],
                                 phiq[:, (h % 2) * 512:(h % 2 + 1) * 512],
                                 wait=(mpq, m_ys) + ((m_ynext,) if m_ynext else ()))
                        mn1 = ev("vector", "tensor_scalar", wait=(myp,),
                                 out=scf[0:1, 0:512], in0=PSW1[64:65, 0:512],
                                 scalar1=1e-10, scalar2=float(2.0 ** -32),
                                 op0=ALU.max, op1=ALU.mult)
                        mn2a = ev("scalar", "activation", wait=(mn1,),
                                  out=scf[0:1, 0:512], in_=scf[0:1, 0:512],
                                  func=AF.Ln)
                        mn2 = ev("scalar", "activation", wait=(mn2a,),
                                 out=scf[0:1, 0:512], in_=scf[0:1, 0:512],
                                 func=AF.Exp, scale=-1.0)
                        mf2 = mm(PSX[0:64, 0:512], cs["sc32row"][0:1, 0:64],
                                 scf[0:1, 0:512],
                                 wait=(mn2,) + ((m_f2_prev,) if m_f2_prev else ()))
                        mf2e = vcp(scr[0:64, 0:512], PSX[0:64, 0:512],
                                   wait=(mf2,) + ln_gate[0] +
                                        ((m_ob_prev,) if m_ob_prev else ()))
                        m_f2_prev = mf2e
                        if hh == 0:
                            mae = ev("vector", "tensor_tensor", wait=(mf2e, myp),
                                     out=atfm[0:64, p * 512:(p + 1) * 512],
                                     in0=PSW1[0:64, 0:512], in1=scr[0:64, 0:512],
                                     op=ALU.mult)
                        else:
                            mae = ev("vector", "tensor_tensor",
                                     wait=(mf2e, myp) + ((m_stgB_dma,) if m_stgB_dma else ()),
                                     out=stgB[0:64, p * 512:(p + 1) * 512],
                                     in0=PSW1[0:64, 0:512], in1=scr[0:64, 0:512],
                                     op=ALU.mult)
                        m_ynext = mae
                        m_ob_prev = mae
                    m_stgB_dma = dma(atfm[64:128, :], stgB[0:64, :], wait=(m_ynext,),
                                     ch="s")
                    m_at_ev = m_stgB_dma
                    if sq == 0 and tb == 0:
                        dtap("qfm", qfm[:], (mq_done[-1],))
                        dtap("atfm", atfm[:], (m_at_ev, m_ynext))
                    for tc in range(4):
                        ch = tb * 4 + tc
                        mwos = None
                        for ci in range(8):
                            for nh in range(2):
                                mwos = mm(
                                    PSW0[:, nh * 512:(nh + 1) * 512],
                                    atfm[:, ci * 512 + tc * 128: ci * 512 + (tc + 1) * 128],
                                    woab[:, ci * C + nh * 512: ci * C + (nh + 1) * 512],
                                    start=(ci == 0), stop=False,
                                    wait=(m_at_ev,) if (ci == 0 and nh == 0) else ())
                        for nh in range(2):
                            mwos = mm(PSW0[:, nh * 512:(nh + 1) * 512],
                                      cs["onesrowb"][:],
                                      borow[0:1, nh * 512:(nh + 1) * 512],
                                      start=False, stop=True, wait=(m_bo,))
                        mra = ev("vector", "tensor_add", wait=(mwos,),
                                 out=R[ch], in0=R[ch], in1=PSW0[:, 0:1024])
                        mln2 = ln_chunk(R[ch], None, scr[:, 0:1024], (mra,))
                        mam = ev("vector", "reduce_max", wait=(mln2[-1],), out=col(CAMAX),
                                 in_=scr[:, 0:1024], axis=AXX,
                                 apply_absolute_value=True, drain=False)
                        mas_ = ev("scalar", "activation", wait=(mam,), out=col(CAMAX),
                                  in_=col(CAMAX), func=AF.Copy,
                                  bias=float(FFN_EPS))
                        mrs = ev("vector", "reciprocal", wait=(mas_,), out=col(CRED),
                                 in_=col(CAMAX), drain=False)
                        ms127 = ev("scalar", "activation", wait=(mrs,), out=col(CRED),
                                   in_=col(CRED), func=AF.Copy, scale=127.0)
                        mosc = ev("scalar", "activation", wait=(mas_,), out=col(COSC),
                                  in_=col(CAMAX), func=AF.Copy, scale=col(CWSD))
                        mz1_ = ev("vector", "tensor_scalar", wait=(ms127,),
                                  out=scr[:, 0:1024], in0=scr[:, 0:1024],
                                  scalar1=col(CRED), scalar2=-128.0,
                                  op0=ALU.mult, op1=ALU.max)
                        mz2_ = ev("vector", "tensor_scalar", wait=(mz1_,),
                                  out=scr[:, 0:1024], in0=scr[:, 0:1024],
                                  scalar1=127.0, scalar2=None, op0=ALU.min)
                        mrd = ev("vector", "tensor_scalar", wait=(mz2_,), out=aqtm[:],
                                 in0=scr[:, 0:1024], scalar1=col(CMAGIC),
                                 scalar2=col(CMAGIC), op0=ALU.add, op1=ALU.subtract)
                        ln_gate[0] = ln_gate[0] + (mrd,)
                        mqt2 = transpose8(aqtm[:], aqfm[:].rearrange("p (a b) -> p a b", a=8),
                                          (mrd,))
                        mfmm = proj_tm(aqfm[:], wqffn, PSW1, C, (mqt2,))
                        m_ffn_prev = ev("vector", "scalar_tensor_tensor", wait=(mfmm, mosc),
                                        out=R[ch], in0=PSW1[:, 0:1024],
                                        scalar=col(COSC), in1=R[ch],
                                        op0=ALU.mult, op1=ALU.add)
                if sq == 0:
                    for ch in range(NCH):
                        dtap("R_CI" if ch == 0 else None, None, None) if False else None
                    if "R_CI" in dbg_h:
                        for ch in range(NCH):
                            dma(dbg_h["R_CI"][ch * 128:(ch + 1) * 128, :], R[ch],
                                wait=(m_ffn_prev,), ch="t")
                m_CI_done = S.bar()

            if maxph < 4:
                sP.close()
                m_seq_entry = S.bar()
                continue
            sP.close()
            with ExitStack() as sD:
                eD = sD.enter_context
                stgB = eD(nc.sbuf_tensor(f"stgBd{sq}", [128, 8 * 512], BF16))
                EtEp_s = eD(nc.sbuf_tensor(f"EtEp_s{sq}", [128, 2 * 128 * H], F32))
                bcol_s = eD(nc.sbuf_tensor(f"bcol_s{sq}", [128, H * NKO], F32))
                F1_s = eD(nc.sbuf_tensor(f"F1_s{sq}", [128, 512], F32))
                ct_s = eD(nc.sbuf_tensor(f"ct_s{sq}", [128, 48], F32))
                with ExitStack() as sDlo:
                    eL = sDlo.enter_context
                    EtEp_lo = eL(nc.sbuf_tensor(f"EtEp_lo{sq}", [128, 4096], F32))
                    bcol_lo = eL(nc.sbuf_tensor(f"bcol_lo{sq}", [128, H * NKO], F32))
                    F1_lo = eL(nc.sbuf_tensor(f"F1_lo{sq}", [128, 512], F32))
                    ct_lo = eL(nc.sbuf_tensor(f"ct_lo{sq}", [128, 48], F32))
                    dma(bcol_s[:], ch_["bcol_hi"], wait=m_CI_done, ch="d")
                    dma(bcol_lo[:], ch_["bcol_lo"], ch="d")
                    dma(F1_s[:], ch_["F1_hi"], ch="d")
                    dma(F1_lo[:], ch_["F1_lo"], ch="d")
                    dma(ct_s[:], ch_["ct_hi"], ch="d")
                    dma(ct_lo[:], ch_["ct_lo"], ch="d")
                    dma(EtEp_s[:, 0:2048], ch_["Et_hi"], ch="d")
                    dma(EtEp_lo[:, 0:2048], ch_["Et_lo"], ch="d")
                    dma(EtEp_s[:, 2048:4096], ch_["Ep_hi"], ch="d")
                    m_tbl = dma(EtEp_lo[:, 2048:4096], ch_["Ep_lo"], ch="d")
                    m_d1 = ev("vector", "tensor_add", wait=(m_tbl,), out=bcol_s[:],
                              in0=bcol_s[:], in1=bcol_lo[:])
                    m_d2 = ev("vector", "tensor_add", wait=(m_tbl,), out=F1_s[:],
                              in0=F1_s[:], in1=F1_lo[:])
                    m_d3 = ev("vector", "tensor_add", wait=(m_tbl,), out=ct_s[:],
                              in0=ct_s[:], in1=ct_lo[:])
                    m_d4 = ev("vector", "tensor_add", wait=(m_tbl,),
                              out=EtEp_s[:, 0:2048],
                              in0=EtEp_s[:, 0:2048], in1=EtEp_lo[:, 0:2048])
                    m_d5 = ev("vector", "tensor_add", wait=(m_tbl,),
                              out=EtEp_s[:, 2048:4096],
                              in0=EtEp_s[:, 2048:4096], in1=EtEp_lo[:, 2048:4096])
                    m_dconst = (m_d1, m_d2, m_d3, m_d4, m_d5)
                wslot = eD(nc.sbuf_tensor(f"wslot{sq}", [128, 8 * C], BF16))
                qfmb = eD(nc.sbuf_tensor(f"qfmb{sq}", [128, 8 * T], BF16))
                kfmb = eD(nc.sbuf_tensor(f"kfmb{sq}", [128, 2 * T], BF16))
                vtmb = eD(nc.sbuf_tensor(f"vtmb{sq}", [128, NCH * 4 * 65], BF16))
                sD1 = ExitStack()
                eP = sD1.enter_context
                wkvb2 = eP(nc.sbuf_tensor(f"wkvb2{sq}", [128, 8 * 512], BF16))
                xlnb3 = [eP(nc.sbuf_tensor(f"xlnb3{sq}_{i}", [128, C], BF16))
                         for i in range(2)]
                xfm3 = [eP(nc.sbuf_tensor(f"xfm3{sq}_{i}", [128, C], BF16))
                        for i in range(2)]
                qtmp2 = eP(nc.sbuf_tensor(f"qtmp2{sq}", [128, C], BF16))
                ktmp2 = eP(nc.sbuf_tensor(f"ktmp2{sq}", [128, 256], BF16))

                qfmb3 = qfmb[:].rearrange("p (c t) -> p c t", c=8)
                kfmb3 = kfmb[:].rearrange("p (c t) -> p c t", c=2)
                vtmb4 = vtmb[:].rearrange("p (n g e) -> p n g e", n=NCH, g=4)

                mwqb = load_w_bf(wslot, w_h["wq_b"], C,
                                 tuple(m_CI_done) + m_dconst)
                mwkv = load_w_bf(wkvb2, w_h["wkv_b"], 512, (mwqb,))
                mvz = None
                for i_ in range(NCH):
                    mvz = ev("vector", "memset", wait=m_CI_done if i_ == 0 else (),
                             ap=vtmb[:, i_ * 4 * 65:(i_ + 1) * 4 * 65]
                             .rearrange("p (g e) -> p g e", g=4)[:, :, 64:65],
                             constant=1.0)

                m_prev3 = None
                m_kv_ev = None
                m_qb_ev = None
                for ch in range(NCH):
                    par = ch % 2
                    w_ = (mwkv, mvz)
                    if m_prev3:
                        w_ += (m_prev3,)
                    mln3 = ln_chunk(R[ch], xlnb3[par][:], None, w_)
                    mxf3 = transpose8(xlnb3[par][:],
                                      xfm3[par][:].rearrange("p (a b) -> p a b", a=8),
                                      tuple(mln3))
                    mqb = proj_tm(xfm3[par][:], wslot, PSW0, C,
                                  (mxf3, mwqb) + ((m_qb_ev,) if m_qb_ev else ()))
                    mqe2 = vcp(qtmp2[:], PSW0[:, 0:1024], wait=(mqb,))
                    m_qb_ev = mqe2
                    mqt3 = transpose8(qtmp2[:], qfmb3[:, :, ch * 128:(ch + 1) * 128],
                                      (mqe2,))
                    mkv = None
                    for ci in range(8):
                        mkv = mm(PSW1[:, 0:512], xfm3[par][:, ci * 128:(ci + 1) * 128],
                                 wkvb2[:, ci * 512:(ci + 1) * 512],
                                 start=(ci == 0), stop=(ci == 7),
                                 wait=(mxf3, mwkv) + ((m_kv_ev,) if m_kv_ev else ())
                                 if ci == 0 else ())
                    mke2 = vcp(ktmp2[:], PSW1[:, 0:256], wait=(mkv,))
                    mtk = []
                    for b in range(2):
                        mtk.append(transp(PT[:, b * 128:(b + 1) * 128],
                                          ktmp2[:, b * 128:(b + 1) * 128],
                                          cs["identb"][:],
                                          wait=(mke2, mqt3) if b == 0 else ()))
                    mkf2 = vcp(kfmb3[:, :, ch * 128:(ch + 1) * 128], PT[:, 0:256],
                               wait=(mtk[-1],))
                    mve2 = vcp(vtmb4[:, ch, :, 0:64],
                               PSW1[:, 256:512].rearrange("p (g d) -> p g d", g=4),
                               wait=(mkv, mvz))
                    m_kv_ev = mve2
                    m_prev3 = mkf2
                if sq == 0:
                    dtap("kfmb", kfmb[:], (m_prev3,))
                    dtap("qfmb", qfmb[:], (mqt3,))
                    dtap("vtmb", vtmb[:], (m_kv_ev,))
                    if "R_CII" in dbg_h:
                        for ch in range(NCH):
                            dma(dbg_h["R_CII"][ch * 128:(ch + 1) * 128, :], R[ch],
                                wait=(mln3[-1],), ch="t")
                m_CII_done = S.bar()

                sD1.close()
                ptile = eD(nc.sbuf_tensor(f"ptile{sq}", [128, 2 * 512], BF16))
                scf2 = eD(nc.sbuf_tensor(f"scf2{sq}", [128, 512], F32))
                lrow = eD(nc.sbuf_tensor(f"lrow{sq}", [1, 512], F32))
                f1row = eD(nc.sbuf_tensor(f"f1row{sq}", [1, 512], F32))
                obfm = eD(nc.sbuf_tensor(f"obfm{sq}", [128, 8 * 512], BF16))
                aqtm2 = eD(nc.sbuf_tensor(f"aqtm2{sq}", [128, C], BF16))
                aqfm2 = eD(nc.sbuf_tensor(f"aqfm2{sq}", [128, C], BF16))

                mwp = load_w_bf(wslot, w_h["wpr_b"], C, m_CII_done)

                m_ffn2_prev = None
                m_stgB2 = None
                m_f1_prev = None
                for tq in range(NTQ):
                    m_ob_ev = None
                    m_comb_prev = None
                    m_exp_par = [None, None]
                    m_pv_par = [None, None]
                    for h in range(H):
                        p, hh = h // 2, h % 2
                        kv = h % 4
                        kvt = kv // 2
                        upcls = [(s2 - 4 * tq) >= 4 for s2 in range(NSS)]
                        m_pv_last = None
                        pend = None

                        def emit_pv(ss, mfix):
                            nonlocal m_pv_last
                            par2 = ss % 2
                            sl = ptile[:, par2 * 512:(par2 + 1) * 512]
                            is_up = upcls[ss]
                            first = not any(u == is_up for u in upcls[:ss])
                            last = all(u != is_up for u in upcls[ss + 1:])
                            dst = PSW0[0:65, 512:1024] if is_up \
                                else PSW0[0:65, 0:512]
                            wpv = (mfix,)
                            if first:
                                if m_comb_prev:
                                    wpv += (m_comb_prev,)
                                if m_ffn2_prev:
                                    wpv += (m_ffn2_prev,)
                            m_pv_last = mm(dst, vtmb4[:, ss, kv, :], sl,
                                           start=first, stop=last, wait=wpv)
                            m_pv_par[par2] = m_pv_last

                        for ss in range(NSS):
                            ko = ss - 4 * tq
                            par2 = ss % 2
                            ps = (PS0, PS1)[par2]
                            sl = ptile[:, par2 * 512:(par2 + 1) * 512]
                            w_ = (mwp,)
                            if m_exp_par[par2]:
                                w_ += (m_exp_par[par2],)
                            mmks = mm(ps[:, 0:512],
                                      kfmb[hh * 64:(hh + 1) * 64,
                                           kvt * T + ss * 128: kvt * T + (ss + 1) * 128],
                                      qfmb[hh * 64:(hh + 1) * 64,
                                           p * T + tq * 512: p * T + (tq + 1) * 512],
                                      tile=(hh * 64, 0) if hh else None, wait=w_)
                            we_ = (mmks,)
                            if m_pv_par[par2]:
                                we_ += (m_pv_par[par2],)
                            if ko >= 4:
                                mfix = ev("scalar", "activation", wait=we_, out=sl,
                                          in_=ps[:, 0:512], func=AF.Exp, scale=DS)
                            else:
                                bc_ap = bcol_s[:, h * NKO + (ko - KOFF0):
                                               h * NKO + (ko - KOFF0) + 1]
                                mfix = ev("scalar", "activation", wait=we_, out=sl,
                                          in_=ps[:, 0:512], func=AF.Exp, scale=DS,
                                          bias=bc_ap)
                                if ko >= 0:
                                    for m2i in range(min(ko, 4)):
                                        cidx = h * 3 + (m2i - ko + 3)
                                        mfix = ev("vector", "scalar_tensor_tensor",
                                                  wait=(mfix,),
                                                  out=sl[:, m2i * 128:(m2i + 1) * 128],
                                                  in0=sl[:, m2i * 128:(m2i + 1) * 128],
                                                  scalar=ct_s[:, cidx:cidx + 1],
                                                  in1=Et_h(h), op0=ALU.mult, op1=ALU.mult)
                                    if ko <= 3:
                                        mfix = ev("vector", "tensor_tensor", wait=(mfix,),
                                                  out=sl[:, ko * 128:(ko + 1) * 128],
                                                  in0=sl[:, ko * 128:(ko + 1) * 128],
                                                  in1=Ep_h(h), op=ALU.mult)
                            m_exp_par[par2] = mfix
                            if pend is not None:
                                emit_pv(*pend)
                            pend = (ss, mfix)
                        emit_pv(*pend)
                        any_up = any(upcls)
                        if any_up:
                            mf1r = mm(PSX[0:1, 0:512], cs["identf"][:, h:h + 1],
                                      F1_s[:],
                                      wait=(m_pv_last,) +
                                           ((m_comb_prev,) if m_comb_prev else ()))
                            mf1rv = vcp(f1row[0:1, 0:512], PSX[0:1, 0:512],
                                        wait=(mf1r,) +
                                             ((m_f1_prev,) if m_f1_prev else ()))
                            mf1 = mm(PSX[0:65, 0:512], cs["onesrow"][0:1, 0:65],
                                     f1row[0:1, :], wait=(mf1rv,))
                            m_f1_prev = mf1
                            mf1e = vcp(scf2[0:65, 0:512], PSX[0:65, 0:512], wait=(mf1,))
                            mcb1 = ev("vector", "tensor_tensor", wait=(mf1e,),
                                      out=scf2[0:65, 0:512], in0=PSW0[0:65, 512:1024],
                                      in1=scf2[0:65, 0:512], op=ALU.mult)
                            mcb2 = ev("vector", "tensor_add", wait=(mcb1,),
                                      out=scf2[0:65, 0:512], in0=scf2[0:65, 0:512],
                                      in1=PSW0[0:65, 0:512])
                        else:
                            mcb2 = vcp(scf2[0:65, 0:512], PSW0[0:65, 0:512],
                                       wait=(m_pv_last,) + ((m_comb_prev,) if m_comb_prev else ()))
                        mr1 = ev("vector", "tensor_scalar", wait=(mcb2,),
                                 out=lrow[0:1, 0:512], in0=scf2[64:65, 0:512],
                                 scalar1=1e-30, scalar2=float(2.0 ** -32),
                                 op0=ALU.max, op1=ALU.mult)
                        mr1a = ev("scalar", "activation", wait=(mr1,),
                                  out=lrow[0:1, 0:512], in_=lrow[0:1, 0:512],
                                  func=AF.Ln)
                        mr1b = ev("scalar", "activation", wait=(mr1a,),
                                  out=lrow[0:1, 0:512], in_=lrow[0:1, 0:512],
                                  func=AF.Exp, scale=-1.0)
                        mr2 = mm(PSX[0:64, 0:512], cs["sc32row"][0:1, 0:64],
                                 lrow[0:1, 0:512], wait=(mr1b,))
                        mr3 = vcp(scr[0:64, 0:512], PSX[0:64, 0:512],
                                  wait=(mr2,) + ln_gate[0])
                        if hh == 0:
                            mob = ev("vector", "tensor_tensor", wait=(mr3,),
                                     out=obfm[0:64, p * 512:(p + 1) * 512],
                                     in0=scf2[0:64, 0:512], in1=scr[0:64, 0:512],
                                     op=ALU.mult)
                        else:
                            mob = ev("vector", "tensor_tensor",
                                     wait=(mr3,) + ((m_stgB2,) if m_stgB2 else ()),
                                     out=stgB[0:64, p * 512:(p + 1) * 512],
                                     in0=scf2[0:64, 0:512], in1=scr[0:64, 0:512],
                                     op=ALU.mult)
                        m_comb_prev = mob
                        m_ob_ev = mob
                        ln_gate[0] = ln_gate[0] + (mob,)
                    m_stgB2 = dma(obfm[64:128, :], stgB[0:64, :], wait=(m_ob_ev,),
                                  ch="s")
                    m_ob_ev = m_stgB2
                    if sq == 0 and tq == 0:
                        dtap("obfm", obfm[:], (m_ob_ev, m_comb_prev))
                    for tc in range(4):
                        ch = tq * 4 + tc
                        mws = None
                        for ci in range(8):
                            for nh in range(2):
                                mws = mm(
                                    PSW1[:, nh * 512:(nh + 1) * 512],
                                    obfm[:, ci * 512 + tc * 128: ci * 512 + (tc + 1) * 128],
                                    wslot[:, ci * C + nh * 512: ci * C + (nh + 1) * 512],
                                    start=(ci == 0), stop=False,
                                    wait=(m_ob_ev, mwp) if (ci == 0 and nh == 0) else ())
                        for nh in range(2):
                            mws = mm(PSW1[:, nh * 512:(nh + 1) * 512],
                                     cs["onesrowb"][:], bprow[0:1, nh * 512:(nh + 1) * 512],
                                     start=False, stop=True, wait=(m_bp,))
                        mra2 = ev("vector", "tensor_add", wait=(mws,),
                                  out=R[ch], in0=R[ch], in1=PSW1[:, 0:1024])
                        mln4 = ln_chunk(R[ch], None, scr[:, 0:1024], (mra2,))
                        mam = ev("vector", "reduce_max", wait=(mln4[-1],), out=col(CAMAX),
                                 in_=scr[:, 0:1024], axis=AXX,
                                 apply_absolute_value=True, drain=False)
                        mas_ = ev("scalar", "activation", wait=(mam,), out=col(CAMAX),
                                  in_=col(CAMAX), func=AF.Copy,
                                  bias=float(FFN_EPS))
                        mrs = ev("vector", "reciprocal", wait=(mas_,), out=col(CRED),
                                 in_=col(CAMAX), drain=False)
                        ms127 = ev("scalar", "activation", wait=(mrs,), out=col(CRED),
                                   in_=col(CRED), func=AF.Copy, scale=127.0)
                        mosc = ev("scalar", "activation", wait=(mas_,), out=col(COSC),
                                  in_=col(CAMAX), func=AF.Copy, scale=col(CWSD))
                        mz1_ = ev("vector", "tensor_scalar", wait=(ms127,),
                                  out=scr[:, 0:1024], in0=scr[:, 0:1024],
                                  scalar1=col(CRED), scalar2=-128.0,
                                  op0=ALU.mult, op1=ALU.max)
                        mz2_ = ev("vector", "tensor_scalar", wait=(mz1_,),
                                  out=scr[:, 0:1024], in0=scr[:, 0:1024],
                                  scalar1=127.0, scalar2=None, op0=ALU.min)
                        mrd = ev("vector", "tensor_scalar", wait=(mz2_,), out=aqtm2[:],
                                 in0=scr[:, 0:1024], scalar1=col(CMAGIC),
                                 scalar2=col(CMAGIC), op0=ALU.add, op1=ALU.subtract)
                        ln_gate[0] = ln_gate[0] + (mrd,)
                        mqt4 = transpose8(aqtm2[:],
                                          aqfm2[:].rearrange("p (a b) -> p a b", a=8),
                                          (mrd,))
                        mfmm = proj_tm(aqfm2[:], wqffn, PSW0, C,
                                       (mqt4,) + ((m_ffn2_prev,) if m_ffn2_prev else ()))
                        mfin = ev("vector", "scalar_tensor_tensor", wait=(mfmm, mosc),
                                  out=R[ch], in0=PSW0[:, 0:1024], scalar=col(COSC),
                                  in1=R[ch], op0=ALU.mult, op1=ALU.add)
                        m_ffn2_prev = mfin
                        dma(out_h[sq * T + ch * 128: sq * T + (ch + 1) * 128, :],
                            R[ch], wait=(mfin,), ch="o")
                m_seq_entry = S.bar()


        with nc.Block() as block:
            def body_for(eng_name):
                def body(e):
                    for fn, waits, sem_inc in S.items[eng_name]:
                        for s_name, v in waits:
                            e.wait_ge(sems[s_name], v)
                        ins = fn(e)
                        ins.then_inc(sems[sem_inc[0]], sem_inc[1])
                    if eng_name == "sync":
                        for chn in DMACH:
                            if S.count[chn] > 0:
                                e.wait_ge(sems[chn], S.count[chn])
                return body
            block.sync(body_for("sync"))
            block.tensor(body_for("tensor"))
            block.vector(body_for("vector"))
            block.scalar(body_for("scalar"))
            block.gpsimd(body_for("gpsimd"))

    return nc


def kernel(**inputs):
    T = 2048
    NSEQ = 2
    ncores = 8
    x = np.ascontiguousarray(np.asarray(inputs["L0_attn_a_query"], np.float32)
                             .reshape(16, T, C))
    cst = host_consts(T)

    def wT(a):
        return np.ascontiguousarray(np.asarray(a, np.float32).T)

    bias_a = np.stack([np.asarray(inputs["L0_attn_a_bq"], np.float32),
                       np.asarray(inputs["L0_attn_a_bk"], np.float32),
                       np.asarray(inputs["L0_attn_a_bv"], np.float32),
                       np.asarray(inputs["L0_attn_a_bo"], np.float32)])
    shared = {
        "wq_a": wT(inputs["L0_attn_a_Wq"]), "wk_a": wT(inputs["L0_attn_a_Wk"]),
        "wv_a": wT(inputs["L0_attn_a_Wv"]), "wo_a": wT(inputs["L0_attn_a_Wo"]),
        "bias_a": np.ascontiguousarray(bias_a),
        "bqfm": np.ascontiguousarray(
            np.asarray(inputs["L0_attn_a_bq"], np.float32).reshape(8, 128).T),
        "wffn": wT(inputs["L0_ffn_weight"]),
        "wq_b": wT(inputs["L0_attn_b_Wq"]), "wkv_b": wT(inputs["L0_attn_b_Wkv"]),
        "wpr_b": wT(inputs["L0_attn_b_Wproj"]),
        "bpr_b": np.ascontiguousarray(
            np.asarray(inputs["L0_attn_b_bproj"], np.float32).reshape(1, C)),
        "temp": np.asarray(inputs["L0_attn_a_temperature"], np.float32).reshape(1, 1),
        "lidx": np.ascontiguousarray(
            np.asarray(inputs["L0_attn_a_landmark_idx"], np.int32).reshape(1, NL)),
    }
    for n, a in cst.items():
        if isinstance(a, np.ndarray):
            shared[n] = np.ascontiguousarray(a)
    nc = build_program(T=T, NSEQ=NSEQ)
    in_maps = []
    for core in range(ncores):
        m = dict(shared)
        m["x"] = np.ascontiguousarray(
            x[core * NSEQ:(core + 1) * NSEQ].reshape(NSEQ * T, C))
        in_maps.append(m)
    res = run_bass_kernel_spmd(nc, in_maps, core_ids=list(range(ncores)))
    out = np.concatenate(
        [np.asarray(r["out"]).reshape(NSEQ, T, C) for r in res.results], axis=0)
    return np.ascontiguousarray(out.astype(np.float32))
